# version 11
# baseline (speedup 1.0000x reference)
import contextlib
import numpy as np
import ml_dtypes
import concourse.bass as bass
import concourse.mybir as mybir
from concourse.bass_utils import run_bass_kernel_spmd

F32 = mybir.dt.float32
BF16 = mybir.dt.bfloat16
AF = mybir.ActivationFunctionType
ALU = mybir.AluOpType
AX = mybir.AxisListType

D = 2048
NC_D = 16
H = 16
NB = 4
E = 16
EI = 1024
NC_EI = 8
SI = 4096
NC_SI = 32

EPS_LN = 1e-5
EPS_MEM = 1e-6
BIG = 1.0e30
DELTA = 1e-7

CFG = {"T_O": 1024, "T_CORE": 2048, "CHAIN": False, "DBG": False}



def _tile_lhsT(W):
    K, M = W.shape
    nk, nm = K // 128, M // 128
    return np.ascontiguousarray(
        W.reshape(nk, 128, nm, 128).transpose(2, 1, 0, 3).reshape(nm, 128, nk * 128))


def _prep(inputs):
    T_O = CFG["T_O"]
    Tc = CFG["T_CORE"]
    n_ot = Tc // T_O
    x = np.asarray(inputs["L0_attn_hidden_states"], dtype=np.float32)
    xt = np.ascontiguousarray(x.reshape(-1, D))
    per_core = []
    n_cores = xt.shape[0] // Tc
    for c in range(n_cores):
        xs = xt[c * Tc:(c + 1) * Tc].T
        xs = xs.reshape(NC_D, 128, n_ot, T_O).transpose(2, 1, 0, 3)
        per_core.append(np.ascontiguousarray(xs.reshape(n_ot, 128, NC_D * T_O)))

    Wq = np.asarray(inputs["L0_attn_Wq"], dtype=np.float32)
    Wo = np.asarray(inputs["L0_attn_Wo"], dtype=np.float32)
    shared = {
        "wq": _tile_lhsT(np.ascontiguousarray(Wq.T)),
        "wo": _tile_lhsT(np.ascontiguousarray(Wo.T)),
        "mem": np.ascontiguousarray(
            np.asarray(inputs["L0_attn_memories"], dtype=np.float32)
            .transpose(1, 2, 0, 3).reshape(H, 128, NB * 128)),
        "mnt": np.ascontiguousarray(
            np.asarray(inputs["L0_attn_memory_norms"], dtype=np.float32)
            .transpose(2, 1, 0).reshape(128, H * NB)),
        "mn4": np.ascontiguousarray(
            np.asarray(inputs["L0_attn_memory_norms"], dtype=np.float32)
            .reshape(NB, H * 128)),
        "bankw": np.ascontiguousarray(
            np.asarray(inputs["L0_attn_bank_weights"], dtype=np.float32)),
        "rwt": np.ascontiguousarray(
            np.asarray(inputs["L0_ffn_router_weight"], dtype=np.float32)
            .T.reshape(NC_D, 128, E).transpose(1, 0, 2).reshape(128, NC_D * E)),
        "ebias": np.ascontiguousarray(np.broadcast_to(
            np.asarray(inputs["L0_ffn_e_bias"], dtype=np.float32)[None, :], (128, E))),
        "wup": np.ascontiguousarray(
            np.asarray(inputs["L0_ffn_w_up"], dtype=np.float32)
            .reshape(E, NC_D, 128, NC_EI, 128).transpose(0, 3, 2, 1, 4)
            .reshape(E, NC_EI, 128, NC_D * 128)),
        "wdn": np.ascontiguousarray(
            np.asarray(inputs["L0_ffn_w_down"], dtype=np.float32)
            .reshape(E, NC_EI, 128, NC_D, 128).transpose(0, 3, 2, 1, 4)
            .reshape(E, NC_D, 128, NC_EI * 128)),
        "shu": _tile_lhsT(np.asarray(inputs["L0_ffn_shared_up"], dtype=np.float32)),
        "shd": _tile_lhsT(np.asarray(inputs["L0_ffn_shared_down"], dtype=np.float32)),
    }
    return per_core, shared, n_ot



class Prog:
    ENGINES = ("sync", "gpsimd", "tensor", "vector", "scalar")

    def __init__(self, nc, chain=False):
        self.nc = nc
        self.chain = chain
        self.ops = {e: [] for e in self.ENGINES}
        self.sems = {}
        self.cnt = {}

    def sem(self, name):
        h = self.nc.alloc_semaphore(name)
        self.sems[name] = h
        self.cnt[name] = 0
        return h

    def op(self, eng, build, inc=None, inc_amt=1):
        if self.chain and eng in ("vector", "scalar"):
            own = "v" if eng == "vector" else "s"
            self.wait(eng, own, self.cnt[own])
            if inc is None:
                inc = own
        if inc is None:
            self.ops[eng].append(build)
            return None
        h = self.sems[inc]
        self.cnt[inc] += inc_amt
        val = self.cnt[inc]
        self.ops[eng].append(lambda e, b=build, hh=h, a=inc_amt: b(e).then_inc(hh, a))
        return val

    def wait(self, eng, sem, val):
        if val is None or val <= 0:
            return
        h = self.sems[sem]
        self.ops[eng].append(lambda e, hh=h, v=val: e.wait_ge(hh, v))

    def val(self, sem):
        return self.cnt[sem]

    def barrier(self):
        tv, vv, sv = self.val("t"), self.val("v"), self.val("s")
        self.wait("tensor", "v", vv)
        self.wait("tensor", "s", sv)
        self.wait("vector", "t", tv)
        self.wait("scalar", "t", tv)
        self.wait("vector", "s", sv)
        self.wait("scalar", "v", vv)


class Slot:
    def __init__(self):
        self.rd = {"t": 0, "v": 0, "s": 0}

    def gate_writer(self, p, eng):
        for sem, v in self.rd.items():
            if v > 0:
                p.wait(eng, sem, v)
        self.rd = {"t": 0, "v": 0, "s": 0}

    def read(self, sem, val):
        if val is not None and val > self.rd[sem]:
            self.rd[sem] = val


class Ring:
    def __init__(self, p, buf_ap, n_elems, n_sems=8, tag="wr"):
        self.p = p
        self.buf = buf_ap
        self.cap = n_elems
        self.pos = 0
        self.live = []
        self.sem_names = []
        for i in range(n_sems):
            name = f"{tag}{i}"
            p.sem(name)
            self.sem_names.append(name)
        self.sem_uses = {n: 0 for n in self.sem_names}
        self.next_sem = 0
        self.sid = 0

    def load(self, src_ap, n_free):
        assert n_free <= self.cap
        if self.pos + n_free > self.cap:
            self.pos = 0
        start, end = self.pos, self.pos + n_free
        need_t = 0
        for (s0, e0, done_t, sid0) in self.live:
            if s0 < end and e0 > start:
                assert done_t is not None, f"ring overlap with unconsumed slab {sid0}"
                need_t = max(need_t, done_t)
        self.live = [r for r in self.live if not (r[0] < end and r[1] > start)]
        sem = self.sem_names[self.next_sem]
        self.next_sem = (self.next_sem + 1) % len(self.sem_names)
        if self.sem_uses[sem] > 0:
            self.p.wait("gpsimd", sem, 16 * self.sem_uses[sem])
        if need_t > 0:
            self.p.wait("gpsimd", "t", need_t)
        dst = self.buf[:, start:end]
        self.p.op("gpsimd", lambda e, d=dst, s=src_ap: e.dma_start(out=d, in_=s),
                  inc=sem, inc_amt=16)
        self.sem_uses[sem] += 1
        rec = [start, end, None, self.sid]
        self.live.append(rec)
        self.sid += 1
        self.pos = end
        return dst, (sem, 16 * self.sem_uses[sem]), rec

    def consumed(self, rec):
        rec[2] = self.p.val("t")



def build(n_ot, T_O):
    NF = min(512, T_O)
    NS = T_O // NF
    NCH = T_O // 128
    nc = bass.Bass("TRN2", target_bir_lowering=False)

    x_in = nc.dram_tensor("x", [n_ot, 128, NC_D * T_O], F32, kind="ExternalInput").ap()
    out_d = nc.dram_tensor("out", [n_ot, 128, NC_D * T_O], F32, kind="ExternalOutput").ap()
    wq_d = nc.dram_tensor("wq", [NC_D, 128, NC_D * 128], F32, kind="ExternalInput").ap()
    wo_d = nc.dram_tensor("wo", [NC_D, 128, NC_D * 128], F32, kind="ExternalInput").ap()
    mem_d = nc.dram_tensor("mem", [H, 128, NB * 128], F32, kind="ExternalInput").ap()
    mnt_d = nc.dram_tensor("mnt", [128, H * NB], F32, kind="ExternalInput").ap()
    mn4_d = nc.dram_tensor("mn4", [NB, H * 128], F32, kind="ExternalInput").ap()
    bankw_d = nc.dram_tensor("bankw", [H, NB], F32, kind="ExternalInput").ap()
    rwt_d = nc.dram_tensor("rwt", [128, NC_D * E], F32, kind="ExternalInput").ap()
    ebias_d = nc.dram_tensor("ebias", [128, E], F32, kind="ExternalInput").ap()
    wup_d = nc.dram_tensor("wup", [E, NC_EI, 128, NC_D * 128], F32, kind="ExternalInput").ap()
    wdn_d = nc.dram_tensor("wdn", [E, NC_D, 128, NC_EI * 128], F32, kind="ExternalInput").ap()
    shu_d = nc.dram_tensor("shu", [NC_SI, 128, NC_D * 128], F32, kind="ExternalInput").ap()
    shd_d = nc.dram_tensor("shd", [NC_D, 128, NC_SI * 128], F32, kind="ExternalInput").ap()
    DBG = CFG.get("DBG", False)
    if DBG:
        dbg_d = {k: nc.dram_tensor(f"dbg_{k}", [128, NC_D * T_O], F32, kind="ExternalOutput").ap()
                 for k in ("ln1", "comb", "attn", "x1", "ln2", "ffn", "x4", "x5", "x6")}
        dbg_lg = nc.dram_tensor("dbg_lg", [E, T_O], F32, kind="ExternalOutput").ap()
        dbg_rows = nc.dram_tensor("dbg_rows", [1, 4 * T_O], F32, kind="ExternalOutput").ap()
        dbg_cw = nc.dram_tensor("dbg_cw", [E, T_O], F32, kind="ExternalOutput").ap()
        p_dbg = [0]

    inl = {
        "ones128_bf": np.ones((128, 1), np.float32).astype(ml_dtypes.bfloat16),
        "onesr_bf": np.ones((1, 128), np.float32).astype(ml_dtypes.bfloat16),
        "onesr16_f": np.ones((1, 16), np.float32),
        "ones128_f": np.ones((128, 1), np.float32),
        "ident_f": np.eye(128, dtype=np.float32),
        "gdelta": np.ascontiguousarray(np.broadcast_to(
            (np.arange(NB) * DELTA).astype(np.float32)[None, :], (128, NB))),
        "edelta": np.ascontiguousarray(np.broadcast_to(
            (np.arange(E) * DELTA).astype(np.float32)[None, :], (128, E))),
        "bsel4": np.kron(np.eye(NB, dtype=np.float32),
                         np.ones((1, 128), np.float32)).astype(ml_dtypes.bfloat16),
        "bsel16": np.kron(np.eye(E, dtype=np.float32),
                          np.ones((1, 128), np.float32)).astype(ml_dtypes.bfloat16),
    }
    inl_d = {k: nc.inline_tensor(v, "c_" + k) for k, v in inl.items()}

    p = Prog(nc, chain=CFG.get("CHAIN", False))
    for s in ("t", "v", "s", "ldc", "ldg", "ldm0", "ldm1", "ldm2"):
        p.sem(s)
    for i in range(4):
        p.sem(f"xs{i}")
        p.sem(f"xl{i}")
    xs_uses = [0, 0, 0, 0]
    xl_uses = [0, 0, 0, 0]
    xl_at = {}

    A = nc.alloc_sbuf_tensor
    x_pl = A("x_pl", [128, NC_D * T_O], BF16).ap()
    ln_pl = A("ln_pl", [128, NC_D * T_O], BF16).ap()
    cs_pl = A("cs_pl", [128, NC_D * T_O], BF16).ap()
    big_pl = A("big_pl", [128, NC_D * T_O], BF16).ap()
    xsq = A("xsq", [128, 2 * T_O], BF16).ap()
    sig_bf = A("sig_bf", [128, 3 * NF], BF16).ap()
    sig_sc = A("sig_sc", [128, 2 * NF], BF16).ap()
    expb = A("expb", [128, 2 * NF], F32).ap()
    tfp = A("tfp", [128, 2 * NF], F32).ap()
    tfp2 = A("tfp2", [128, 2 * NF], F32).ap()
    rows = A("rows", [1, 4 * T_O], F32).ap()
    rows_bf = A("rows_bf", [1, 2 * T_O], BF16).ap()
    nrm = A("nrm", [NB, 2 * NF], F32).ap()
    nrm_bf = A("nrm_bf", [NB, 2 * NF], BF16).ap()
    lg = A("lg", [E, T_O], F32).ap()
    sc_tm = A("sc_tm", [128, NCH * E], F32).ap()
    isl = A("isl", [128, 6 * NCH * E], F32).ap()
    red1 = A("red1", [128, 6 * NCH], F32).ap()
    comb_bfs = A("comb_bfs", [E, T_O], BF16).ap()
    mnt = A("mnt_s", [128, H * NB], BF16).ap()
    rwt = A("rwt_s", [128, NC_D * E], F32).ap()
    ebias_t = A("ebias_s", [128, E], F32).ap()
    bankw = A("bankw_s", [H, NB], F32).ap()
    mn4 = A("mn4_s", [NB, H * 128], BF16).ap()
    wTa = A("wTa", [NB, H], F32).ap()
    lnW = A("lnW", [NB, H], F32).ap()
    wsm = A("wsm", [H, NB], F32).ap()
    rwsum = A("rwsum", [E, 1], F32).ap()
    tiny = A("tiny", [E, 8], F32).ap()
    mem_bf = A("mem_bf", [128, 3 * NB * 128], BF16).ap()
    cst = {}
    for k, v in inl.items():
        dt = BF16 if v.dtype == ml_dtypes.bfloat16 else F32
        cst[k] = A("s_" + k, list(v.shape), dt).ap()

    RING_ELEMS = 4096
    ring_buf = A("ring", [128, RING_ELEMS], BF16).ap()
    ring = Ring(p, ring_buf, RING_ELEMS, n_sems=8)

    PB = [nc.alloc_psum_tensor(f"PB{i}", [128, NF], F32).ap() for i in range(8)]

    def IA(slot, c, w=E):
        base = (slot * NCH + c) * E
        return isl[:, base:base + w]

    sl_xsq = [Slot(), Slot()]
    sl_sig = [Slot(), Slot(), Slot()]
    sl_sigsc = [Slot(), Slot()]
    sl_expb = [Slot(), Slot()]
    sl_tfp = [Slot(), Slot()]
    sl_tfp2 = [Slot(), Slot()]
    sl_nrm = Slot()
    sl_nrmbf = [Slot(), Slot()]
    sl_rows = Slot()
    sl_mem = [Slot(), Slot(), Slot()]
    memslab_uses = [0, 0, 0]

    def sync_dma(dst, src, sem):
        return p.op("sync", lambda e, d=dst, s=src: e.dma_start(out=d, in_=s),
                    inc=sem, inc_amt=16)

    def g_dma(dst, src, sem):
        return p.op("gpsimd", lambda e, d=dst, s=src: e.dma_start(out=d, in_=s),
                    inc=sem, inc_amt=16)

    n_const = 0
    for k in cst:
        sync_dma(cst[k][:, :], inl_d[k].ap()[:, :], "ldc")
        n_const += 1
    for dst, src in ((rwt, rwt_d), (ebias_t, ebias_d), (bankw, bankw_d)):
        sync_dma(dst[:, :], src[:, :], "ldc")
        n_const += 1
    ldc_all = 16 * n_const
    g_dma(mnt[:, :], mnt_d[:, :], "ldg")
    g_dma(mn4[:, :], mn4_d[:, :], "ldg")
    ldg_all = 32
    for kc in range(NC_D):
        sid = kc % 4
        if xl_uses[sid] > 0:
            p.wait("gpsimd", f"xl{sid}", 16 * xl_uses[sid])
        xl_uses[sid] += 1
        xl_at[(0, kc)] = 16 * xl_uses[sid]
        g_dma(x_pl[:, kc * T_O:(kc + 1) * T_O], x_in[0][:, kc * T_O:(kc + 1) * T_O], f"xl{sid}")

    for eng in ("vector", "scalar", "tensor"):
        p.wait(eng, "ldc", ldc_all)
        p.wait(eng, "ldg", ldg_all)

    v_rm = p.op("vector", lambda e: e.reduce_max(tiny[0:H, 0:1], bankw[:, :], axis=AX.X), inc="v")
    v_as = p.op("vector", lambda e: e.reduce_sum(tiny[0:NB, 1:2], mn4[:, :], axis=AX.X), inc="v")
    p.wait("vector", "v", v_rm)
    v_sub = p.op("vector", lambda e: e.tensor_scalar(
        wsm[:, :], bankw[:, :], tiny[0:H, 0:1], None, ALU.subtract), inc="v")
    p.wait("scalar", "v", v_sub)
    s_exp = p.op("scalar", lambda e: e.activation(wsm[:, :], wsm[:, :], AF.Exp), inc="s")
    p.wait("vector", "s", s_exp)
    v_sum = p.op("vector", lambda e: e.reduce_sum(tiny[0:H, 2:3], wsm[:, :], axis=AX.X), inc="v")
    v_act = p.op("vector", lambda e: e.tensor_scalar(
        tiny[0:NB, 3:4], tiny[0:NB, 1:2], EPS_MEM, None, ALU.is_ge), inc="v")
    p.wait("vector", "v", v_sum)
    v_rcp = p.op("vector", lambda e: e.reciprocal(tiny[0:H, 4:5], tiny[0:H, 2:3]), inc="v")
    p.wait("vector", "v", v_rcp)
    v_wsm = p.op("vector", lambda e: e.tensor_scalar(
        wsm[:, :], wsm[:, :], tiny[0:H, 4:5], None, ALU.mult), inc="v")
    if True:
        ps_i0 = PB[0][0:NB, 0:H]
        ps_i1 = PB[1][0:E, 0:1]
        p.wait("tensor", "v", v_wsm)
        p.op("tensor", lambda e: e.transpose(
            ps_i0, wsm[:, :], cst["ident_f"][0:H, 0:H]), inc="t")
        t_rw = None
        for kc in range(NC_D):
            t_rw = p.op("tensor", lambda e, k=kc: e.matmul(
                ps_i1, rwt[:, k * E:(k + 1) * E], cst["ones128_f"][:, :],
                start=(k == 0), stop=(k == NC_D - 1)), inc="t")
        p.wait("vector", "t", t_rw)
        p.wait("vector", "v", v_act)
        v_wta = p.op("vector", lambda e: e.tensor_scalar(
            wTa[:, :], ps_i0, tiny[0:NB, 3:4], None, ALU.mult), inc="v")
        p.op("vector", lambda e: e.tensor_copy(rwsum[:, :], ps_i1), inc="v")
        p.wait("scalar", "v", v_wta)
        p.op("scalar", lambda e: e.activation(lnW[:, :], wTa[:, :], AF.Ln), inc="s")
    p.barrier()


    def ln_rows_from(ps_sum, ps_sq):
        t_now = p.val("t")
        p.wait("vector", "t", t_now)
        sl_rows.gate_writer(p, "vector")
        v_ev = None
        for sub in range(NS):
            f0 = sub * NF
            p.op("vector", lambda e, ss=sub, f0=f0: e.tensor_scalar(
                rows[:, f0:f0 + NF], ps_sum[ss], 1.0 / D, None, ALU.mult), inc="v")
            v_ev = p.op("vector", lambda e, ss=sub, f0=f0: e.tensor_scalar(
                rows[:, T_O + f0:T_O + f0 + NF], ps_sq[ss], 1.0 / D, None, ALU.mult), inc="v")
        p.wait("vector", "v", v_ev)
        p.op("vector", lambda e: e.tensor_tensor(
            rows[:, 2 * T_O:3 * T_O], rows[:, 0:T_O], rows[:, 0:T_O], ALU.mult))
        p.op("vector", lambda e: e.tensor_tensor(
            rows[:, 2 * T_O:3 * T_O], rows[:, T_O:2 * T_O], rows[:, 2 * T_O:3 * T_O], ALU.subtract))
        v_var = p.op("vector", lambda e: e.tensor_scalar(
            rows[:, 2 * T_O:3 * T_O], rows[:, 2 * T_O:3 * T_O], EPS_LN, None, ALU.add), inc="v")
        p.wait("vector", "v", v_var)
        p.wait("scalar", "v", v_var)
        p.op("scalar", lambda e: e.activation(
            rows[:, T_O:2 * T_O], rows[:, 2 * T_O:3 * T_O], AF.Ln))
        s_rstd = p.op("scalar", lambda e: e.activation(
            rows[:, 2 * T_O:3 * T_O], rows[:, T_O:2 * T_O], AF.Exp, scale=-0.5), inc="s")
        p.wait("vector", "s", s_rstd)
        p.op("vector", lambda e: e.tensor_tensor(
            rows[:, 3 * T_O:4 * T_O], rows[:, 0:T_O], rows[:, 2 * T_O:3 * T_O], ALU.mult))
        p.op("vector", lambda e: e.tensor_copy(rows_bf[:, 0:T_O], rows[:, 2 * T_O:3 * T_O]))
        v_last = p.op("vector", lambda e: e.tensor_copy(
            rows_bf[:, T_O:2 * T_O], rows[:, 3 * T_O:4 * T_O]), inc="v")
        return v_last

    def ln_bcast(ps_rb, ps_t1, after_v, to_sbuf=True):
        p.wait("tensor", "v", after_v)
        t_last = None
        for sub in range(NS):
            f0 = sub * NF
            p.op("tensor", lambda e, ss=sub, f0=f0, ps_rb=ps_rb: e.matmul(
                ps_rb[ss][:, :], cst["onesr_bf"][0:1, :],
                rows_bf[0:1, f0:f0 + NF], start=True, stop=True))
            t_last = p.op("tensor", lambda e, ss=sub, f0=f0, ps_t1=ps_t1: e.matmul(
                ps_t1[ss][:, :], cst["onesr_bf"][0:1, :],
                rows_bf[0:1, T_O + f0:T_O + f0 + NF], start=True, stop=True), inc="t")
        sl_rows.read("t", t_last)
        if not to_sbuf:
            return t_last
        p.wait("scalar", "t", t_last)
        for _sl in sl_sig:
            _sl.gate_writer(p, "scalar")
        sl_sigsc[0].gate_writer(p, "scalar")
        sl_sigsc[1].gate_writer(p, "scalar")
        s_last = None
        for sub in range(NS):
            p.op("scalar", lambda e, ss=sub: e.copy(
                sig_bf[:, ss * NF:(ss + 1) * NF], ps_rb[ss][:, :]))
            s_last = p.op("scalar", lambda e, ss=sub: e.copy(
                sig_sc[:, ss * NF:(ss + 1) * NF], ps_t1[ss][:, :]), inc="s")
        return s_last

    def stats_chunk(src_bf, kc, ps_sum, ps_sq, buf, dep_v=None, dep_s=None):
        sl_xsq[buf].gate_writer(p, "scalar")
        if dep_v:
            p.wait("scalar", "v", dep_v)
        if dep_s:
            p.wait("scalar", "s", dep_s)
        s_c = p.op("scalar", lambda e, b=buf, s=src_bf: e.activation(
            xsq[:, b * T_O:(b + 1) * T_O], s, AF.Square), inc="s")
        p.wait("tensor", "s", s_c)
        if dep_v:
            p.wait("tensor", "v", dep_v)
        tl = None
        for sub in range(NS):
            f0 = sub * NF
            p.op("tensor", lambda e, k=kc, ss=sub, src=src_bf, f0=f0: e.matmul(
                ps_sum[ss], cst["ones128_bf"][:, :], src[:, f0:f0 + NF],
                start=(k == 0), stop=(k == NC_D - 1)))
            tl = p.op("tensor", lambda e, k=kc, ss=sub, b=buf, f0=f0: e.matmul(
                ps_sq[ss], cst["ones128_bf"][:, :],
                xsq[:, b * T_O + f0:b * T_O + f0 + NF],
                start=(k == 0), stop=(k == NC_D - 1)), inc="t")
        sl_xsq[buf].read("t", tl)
        return tl

    def load_mem_slab(h):
        buf = h % 3
        sem = f"ldm{buf}"
        if memslab_uses[buf] > 0:
            p.wait("gpsimd", sem, 16 * memslab_uses[buf])
            p.wait("gpsimd", "t", sl_mem[buf].rd["t"])
        p.op("gpsimd", lambda e, h=h, b=buf: e.dma_start(
            out=mem_bf[:, b * NB * 128:(b + 1) * NB * 128], in_=mem_d[h]),
            inc=sem, inc_amt=16)
        memslab_uses[buf] += 1
        return buf, (sem, 16 * memslab_uses[buf])

    def dbg_dump(name, src_pl, ot):
        if not DBG or ot != 0:
            return
        p.wait("gpsimd", "v", p.val("v"))
        p.wait("gpsimd", "s", p.val("s"))
        p.wait("gpsimd", "t", p.val("t"))
        p_dbg[0] += 16
        p.op("gpsimd", lambda e, d=dbg_d[name], s=src_pl: e.dma_start(out=d, in_=s),
             inc="ldc", inc_amt=16)
        val = p.val("ldc")
        for eng in ("tensor", "vector", "scalar", "gpsimd"):
            p.wait(eng, "ldc", val)

    def dbg_dump_small(dst, src_ap):
        if not DBG:
            return
        p.wait("gpsimd", "v", p.val("v"))
        p.op("gpsimd", lambda e, d=dst, s=src_ap: e.dma_start(out=d, in_=s),
             inc="ldc", inc_amt=16)
        val = p.val("ldc")
        for eng in ("tensor", "vector", "scalar", "gpsimd"):
            p.wait(eng, "ldc", val)

    for ot in range(n_ot):

        if True:
            ps_sum = [PB[4 + i][0:1, :] for i in range(NS)]
            ps_sq = [PB[6 + i][0:1, :] for i in range(NS)]
            for kc in range(NC_D):
                sid = kc % 4
                p.wait("scalar", f"xl{sid}", xl_at[(ot, kc)])
                p.wait("vector", f"xl{sid}", xl_at[(ot, kc)])
                p.wait("tensor", f"xl{sid}", xl_at[(ot, kc)])
                stats_chunk(x_pl[:, kc * T_O:(kc + 1) * T_O], kc, ps_sum, ps_sq, kc % 2)
            v_rows = ln_rows_from(ps_sum, ps_sq)
        if True:
            ps_rb = [PB[4 + i] for i in range(NS)]
            ps_t1 = [PB[6 + i] for i in range(NS)]
            s_bc = ln_bcast(ps_rb, ps_t1, v_rows)
            p.wait("vector", "s", s_bc)
            ln1_v = None
            for kc in range(NC_D):
                c0 = kc * T_O
                for sub in range(NS):
                    f0 = sub * NF
                    b = (kc * NS + sub) % 2
                    sl_xsq[b].gate_writer(p, "vector")
                    p.op("vector", lambda e, c=c0, b=b, f0=f0, ss=sub: e.tensor_tensor(
                        xsq[:, b * T_O:b * T_O + NF], x_pl[:, c + f0:c + f0 + NF],
                        sig_bf[:, ss * NF:(ss + 1) * NF], ALU.mult))
                    ln1_v = p.op("vector", lambda e, c=c0, b=b, f0=f0, ss=sub: e.tensor_tensor(
                        ln_pl[:, c + f0:c + f0 + NF], xsq[:, b * T_O:b * T_O + NF],
                        sig_sc[:, ss * NF:(ss + 1) * NF], ALU.subtract), inc="v")
                    sl_xsq[b].read("v", ln1_v)
                    for ss2 in range(NS):
                        sl_sig[ss2].read("v", ln1_v)
                        sl_sigsc[ss2].read("v", ln1_v)
            p.wait("tensor", "v", ln1_v)
        p.barrier()
        dbg_dump("ln1", ln_pl[:, :], ot)

        if True:
            ps_q = [PB[0], PB[1]]
            ps_n = PB[6][0:NB, 0:NF]
            ps_rn = [PB[2], PB[3]]
            ps_c = [PB[4], PB[5]]
            sl_q = [Slot(), Slot()]
            sl_n = Slot()
            sl_rn = [Slot(), Slot()]
            sl_c = [Slot(), Slot()]
            n_it = H * NS
            meta = {}
            st2 = {}

            def p2_load(it):
                h, sub = divmod(it, NS)
                if sub == 0:
                    wq_slab, wq_wait, wq_rec = ring.load(wq_d[h], NC_D * 128)
                    mbuf, mem_wait = load_mem_slab(h)
                    meta[h] = (wq_slab, wq_wait, wq_rec, mbuf, mem_wait)

            def p2_q(it):
                h, sub = divmod(it, NS)
                wq_slab, wq_wait, wq_rec, mbuf, mem_wait = meta[h]
                qi = it % 2
                f0 = sub * NF
                if sub == 0:
                    p.wait("tensor", wq_wait[0], wq_wait[1])
                sl_q[qi].gate_writer(p, "tensor")
                t_q = None
                for kc in range(NC_D):
                    t_q = p.op("tensor", lambda e, k=kc, q=qi, f0=f0, wq_slab=wq_slab: e.matmul(
                        ps_q[q], wq_slab[:, k * 128:(k + 1) * 128],
                        ln_pl[:, k * T_O + f0:k * T_O + f0 + NF],
                        start=(k == 0), stop=(k == NC_D - 1)), inc="t")
                if sub == NS - 1:
                    ring.consumed(wq_rec)
                st2[it] = {"t_q": t_q}

            def p2_sigma(it):
                qi = it % 2
                si = it % 3
                p.wait("scalar", "t", st2[it]["t_q"])
                sl_expb[qi].gate_writer(p, "scalar")
                s_e = p.op("scalar", lambda e, q=qi: e.activation(
                    expb[:, q * NF:(q + 1) * NF], ps_q[q], AF.Exp), inc="s")
                p.wait("vector", "s", s_e)
                v_em = p.op("vector", lambda e, q=qi: e.tensor_scalar(
                    expb[:, q * NF:(q + 1) * NF], expb[:, q * NF:(q + 1) * NF],
                    1.0, None, ALU.min), inc="v")
                p.wait("vector", "v", v_em)
                sl_sig[si].gate_writer(p, "vector")
                v_sig = p.op("vector", lambda e, q=qi, si=si: e.scalar_tensor_tensor(
                    sig_bf[:, si * NF:(si + 1) * NF], ps_q[q], 0.0,
                    expb[:, q * NF:(q + 1) * NF], ALU.max, ALU.add), inc="v")
                sl_q[qi].read("s", s_e)
                sl_q[qi].read("v", v_sig)
                sl_expb[qi].read("v", v_sig)
                st2[it]["v_sig"] = v_sig

            def p2_norms(it):
                h, sub = divmod(it, NS)
                si = it % 3
                sl_n.gate_writer(p, "tensor")
                p.wait("tensor", "v", st2[it]["v_sig"])
                t_n = p.op("tensor", lambda e, h=h, si=si: e.matmul(
                    ps_n, mnt[:, h * NB:(h + 1) * NB],
                    sig_bf[:, si * NF:(si + 1) * NF], start=True, stop=True), inc="t")
                sl_sig[si].read("t", t_n)
                p.wait("vector", "t", t_n)
                sl_nrm.gate_writer(p, "vector")
                v_cl = p.op("vector", lambda e: e.tensor_scalar(
                    nrm[:, 0:NF], ps_n, EPS_MEM, None, ALU.max), inc="v")
                sl_n.read("v", v_cl)
                p.wait("scalar", "v", v_cl)
                p.op("scalar", lambda e: e.activation(
                    nrm[:, NF:2 * NF], nrm[:, 0:NF], AF.Ln))
                ni = it % 2
                sl_nrmbf[ni].gate_writer(p, "scalar")
                s_rn = p.op("scalar", lambda e, h=h, ni=ni: e.activation(
                    nrm_bf[:, ni * NF:(ni + 1) * NF], nrm[:, NF:2 * NF], AF.Exp,
                    scale=-1.0, bias=lnW[:, h:h + 1]), inc="s")
                sl_nrm.read("s", s_rn)
                st2[it]["s_rn"] = s_rn

            def p2_banks(it):
                h, sub = divmod(it, NS)
                qi = it % 2
                f0 = sub * NF
                mbuf, mem_wait = meta[h][3], meta[h][4]
                ci = it % 2
                sl_c[ci].gate_writer(p, "tensor")
                if sub == 0:
                    p.wait("tensor", mem_wait[0], mem_wait[1])
                t_c = None
                t_rn_last = None
                ni = it % 2
                for n in range(NB):
                    ri = n % 2
                    sl_rn[ri].gate_writer(p, "tensor")
                    if n == 0:
                        p.wait("tensor", "s", st2[it]["s_rn"])
                    t_b = p.op("tensor", lambda e, n=n, r=ri, ni=ni: e.matmul(
                        ps_rn[r], cst["bsel4"][:, n * 128:(n + 1) * 128],
                        nrm_bf[:, ni * NF:(ni + 1) * NF], start=True, stop=True), inc="t")
                    t_rn_last = t_b
                    p.wait("vector", "t", t_b)
                    sl_sigsc[ri].gate_writer(p, "vector")
                    v_ss = p.op("vector", lambda e, si=it % 3, r=ri: e.tensor_tensor(
                        sig_sc[:, r * NF:(r + 1) * NF],
                        sig_bf[:, si * NF:(si + 1) * NF],
                        ps_rn[r], ALU.mult), inc="v")
                    sl_rn[ri].read("v", v_ss)
                    sl_sig[it % 3].read("v", v_ss)
                    p.wait("tensor", "v", v_ss)
                    t_c = p.op("tensor", lambda e, n=n, c=ci, b=mbuf, r=ri: e.matmul(
                        ps_c[c],
                        mem_bf[:, b * NB * 128 + n * 128:b * NB * 128 + (n + 1) * 128],
                        sig_sc[:, r * NF:(r + 1) * NF],
                        start=(n == 0), stop=(n == NB - 1)), inc="t")
                    sl_sigsc[ri].read("t", t_c)
                sl_nrmbf[ni].read("t", t_rn_last)
                sl_mem[mbuf].read("t", t_c)
                p.wait("scalar", "t", t_c)
                s_cv = p.op("scalar", lambda e, h=h, c=ci, f0=f0: e.copy(
                    cs_pl[:, h * T_O + f0:h * T_O + f0 + NF], ps_c[c]), inc="s")
                sl_c[ci].read("s", s_cv)

            for slot in range(n_it + 3):
                if 0 <= slot - 3 < n_it:
                    p2_banks(slot - 3)
                if 0 <= slot - 2 < n_it:
                    p2_norms(slot - 2)
                if 0 <= slot - 1 < n_it:
                    p2_sigma(slot - 1)
                if slot < n_it:
                    p2_load(slot)
                    p2_q(slot)
        p.barrier()
        dbg_dump("comb", cs_pl[:, :], ot)

        if True:
            ps_w = [PB[0], PB[1]]
            ps_asum = [PB[4 + i][0:1, :] for i in range(NS)]
            ps_asq = [PB[6 + i][0:1, :] for i in range(NS)]
            sl_w = [Slot(), Slot()]
            n_it = NC_D * NS
            wmeta = {}
            st3 = {}

            def p3_wo(it):
                m, sub = divmod(it, NS)
                if sub == 0:
                    wmeta[m] = ring.load(wo_d[m], NC_D * 128)
                wo_slab, wo_wait, wo_rec = wmeta[m]
                wi = it % 2
                f0 = sub * NF
                if sub == 0:
                    p.wait("tensor", wo_wait[0], wo_wait[1])
                sl_w[wi].gate_writer(p, "tensor")
                t_w = None
                for kc in range(NC_D):
                    t_w = p.op("tensor", lambda e, k=kc, w=wi, f0=f0, wo_slab=wo_slab: e.matmul(
                        ps_w[w], wo_slab[:, k * 128:(k + 1) * 128],
                        cs_pl[:, k * T_O + f0:k * T_O + f0 + NF],
                        start=(k == 0), stop=(k == NC_D - 1)), inc="t")
                if sub == NS - 1:
                    ring.consumed(wo_rec)
                st3[it] = t_w

            def p3_evst(it):
                m, sub = divmod(it, NS)
                wi = it % 2
                f0 = sub * NF
                b = m % 2
                p.wait("scalar", "t", st3[it])
                p.op("scalar", lambda e, m=m, w=wi, f0=f0: e.copy(
                    big_pl[:, m * T_O + f0:m * T_O + f0 + NF], ps_w[w]))
                if sub == 0:
                    sl_xsq[b].gate_writer(p, "scalar")
                s_sq = p.op("scalar", lambda e, w=wi, b=b, f0=f0: e.activation(
                    xsq[:, b * T_O + f0:b * T_O + f0 + NF], ps_w[w],
                    AF.Square), inc="s")
                sl_w[wi].read("s", s_sq)
                p.wait("tensor", "s", s_sq)
                p.op("tensor", lambda e, m=m, ss=sub, f0=f0: e.matmul(
                    ps_asum[ss], cst["ones128_bf"][:, :],
                    big_pl[:, m * T_O + f0:m * T_O + f0 + NF],
                    start=(m == 0), stop=(m == NC_D - 1)))
                tl = p.op("tensor", lambda e, m=m, ss=sub, b=b, f0=f0: e.matmul(
                    ps_asq[ss], cst["ones128_bf"][:, :],
                    xsq[:, b * T_O + f0:b * T_O + f0 + NF],
                    start=(m == 0), stop=(m == NC_D - 1)), inc="t")
                sl_xsq[b].read("t", tl)

            for slot in range(n_it + 1):
                if slot < n_it:
                    p3_wo(slot)
                if 0 <= slot - 1 < n_it:
                    p3_evst(slot - 1)
            v_rows = ln_rows_from(ps_asum, ps_asq)
        if True:
            ps_P = [PB[2 + i][0:E, :] for i in range(NS)]
            ps_rb = [PB[4 + i] for i in range(NS)]
            ps_t1 = [PB[6 + i] for i in range(NS)]
            s_bc = ln_bcast(ps_rb, ps_t1, v_rows)
            p.wait("vector", "s", s_bc)
            p.wait("tensor", "s", s_bc)
            ps_sum = [PB[4 + i][0:1, :] for i in range(NS)]
            ps_sq = [PB[6 + i][0:1, :] for i in range(NS)]
            x1_v = None
            t_P = None
            for kc in range(NC_D):
                c0 = kc * T_O
                s_x1 = None
                for sub in range(NS):
                    f0 = sub * NF
                    b = (kc * NS + sub) % 2
                    sl_tfp[b].gate_writer(p, "vector")
                    p.op("vector", lambda e, c=c0, b=b, f0=f0, ss=sub: e.tensor_tensor(
                        tfp[:, b * NF:(b + 1) * NF],
                        big_pl[:, c + f0:c + f0 + NF],
                        sig_bf[:, ss * NF:(ss + 1) * NF], ALU.mult))
                    p.op("vector", lambda e, b=b, ss=sub: e.tensor_tensor(
                        tfp[:, b * NF:(b + 1) * NF], tfp[:, b * NF:(b + 1) * NF],
                        sig_sc[:, ss * NF:(ss + 1) * NF], ALU.subtract))
                    sl_tfp2[b].gate_writer(p, "vector")
                    v_x1f = p.op("vector", lambda e, c=c0, b=b, f0=f0: e.tensor_tensor(
                        tfp2[:, b * NF:(b + 1) * NF], x_pl[:, c + f0:c + f0 + NF],
                        tfp[:, b * NF:(b + 1) * NF], ALU.add), inc="v")
                    sl_tfp[b].read("v", v_x1f)
                    for ss2 in range(NS):
                        sl_sig[ss2].read("v", v_x1f)
                        sl_sigsc[ss2].read("v", v_x1f)
                    p.wait("scalar", "v", v_x1f)
                    s_x1 = p.op("scalar", lambda e, c=c0, b=b, f0=f0: e.copy(
                        x_pl[:, c + f0:c + f0 + NF], tfp2[:, b * NF:(b + 1) * NF]), inc="s")
                    p.wait("tensor", "v", v_x1f)
                    t_P = p.op("tensor", lambda e, k=kc, b=b, ss=sub: e.matmul(
                        ps_P[ss][:, :], rwt[:, k * E:(k + 1) * E],
                        tfp2[:, b * NF:(b + 1) * NF],
                        start=(k == 0), stop=(k == NC_D - 1)), inc="t")
                    sl_tfp2[b].read("t", t_P)
                    sl_tfp2[b].read("s", s_x1)
                p.wait("scalar", "s", s_x1)
                stats_chunk(x_pl[:, c0:c0 + T_O], kc, ps_sum, ps_sq, kc % 2)
        dbg_dump("attn", big_pl[:, :], ot)
        dbg_dump("x1", x_pl[:, :], ot)

        if True:
            v_rows = ln_rows_from(ps_sum, ps_sq)
        if True:
            ps_rb = [PB[4 + i] for i in range(NS)]
            ps_t1 = [PB[6 + i] for i in range(NS)]
            s_bc = ln_bcast(ps_rb, ps_t1, v_rows)
            p.wait("vector", "s", s_bc)
            ln2_v = None
            for kc in range(NC_D):
                c0 = kc * T_O
                for sub in range(NS):
                    f0 = sub * NF
                    b = (kc * NS + sub) % 2
                    sl_xsq[b].gate_writer(p, "vector")
                    p.op("vector", lambda e, c=c0, b=b, f0=f0, ss=sub: e.tensor_tensor(
                        xsq[:, b * T_O:b * T_O + NF], x_pl[:, c + f0:c + f0 + NF],
                        sig_bf[:, ss * NF:(ss + 1) * NF], ALU.mult))
                    ln2_v = p.op("vector", lambda e, c=c0, b=b, f0=f0, ss=sub: e.tensor_tensor(
                        ln_pl[:, c + f0:c + f0 + NF], xsq[:, b * T_O:b * T_O + NF],
                        sig_sc[:, ss * NF:(ss + 1) * NF], ALU.subtract), inc="v")
                    sl_xsq[b].read("v", ln2_v)
                    for ss2 in range(NS):
                        sl_sig[ss2].read("v", ln2_v)
                        sl_sigsc[ss2].read("v", ln2_v)
            p.wait("tensor", "s", s_bc)
        if True:
            ps_r16 = [PB[4 + i][0:E, :] for i in range(NS)]
            ps_m16 = [PB[6 + i][0:E, :] for i in range(NS)]
            t_b16 = None
            for sub in range(NS):
                f0 = sub * NF
                p.op("tensor", lambda e, ss=sub, f0=f0: e.matmul(
                    ps_r16[ss][:, :], cst["onesr16_f"][0:1, :],
                    rows[0:1, 2 * T_O + f0:2 * T_O + f0 + NF], start=True, stop=True))
                t_b16 = p.op("tensor", lambda e, ss=sub, f0=f0: e.matmul(
                    ps_m16[ss][:, :], cst["onesr16_f"][0:1, :],
                    rows[0:1, 3 * T_O + f0:3 * T_O + f0 + NF], start=True, stop=True), inc="t")
            sl_rows.read("t", t_b16)
            p.wait("vector", "t", t_b16)
            sl_tfp[0].gate_writer(p, "vector")
            sl_tfp[1].gate_writer(p, "vector")
            v_lg = None
            for sub in range(NS):
                f0 = sub * NF
                v_m16 = p.op("vector", lambda e, ss=sub: e.tensor_scalar(
                    ps_m16[ss][:, :], ps_m16[ss][:, :], rwsum[:, 0:1], None, ALU.mult), inc="v")
                p.wait("vector", "v", v_m16)
                p.op("vector", lambda e, ss=sub: e.tensor_copy(
                    tfp[0:E, ss * NF:(ss + 1) * NF], ps_r16[ss][:, :]))
                p.op("vector", lambda e, ss=sub, f0=f0: e.tensor_tensor(
                    lg[:, f0:f0 + NF], ps_P[ss][:, :], tfp[0:E, ss * NF:(ss + 1) * NF], ALU.mult))
                v_lg = p.op("vector", lambda e, ss=sub, f0=f0: e.tensor_tensor(
                    lg[:, f0:f0 + NF], lg[:, f0:f0 + NF], ps_m16[ss][:, :], ALU.subtract), inc="v")
            sl_tfp[0].read("v", v_lg)
            sl_tfp[1].read("v", v_lg)
        p.wait("scalar", "v", v_lg)
        s_sc = p.op("scalar", lambda e: e.activation(
            lg[:, :], lg[:, :], AF.Sigmoid), inc="s")
        def emit_island():
                ps_tt = [PB[4][:, 0:E], PB[5][:, 0:E]]
                ps_cb = [PB[6][0:E, 0:128], PB[7][0:E, 0:128]]
                p.wait("tensor", "s", s_sc)
                v_prev = None
                tt_done = None
                for c in range(NCH):
                    if c >= 2:
                        p.wait("tensor", "v", tt_done)
                    t_t = p.op("tensor", lambda e, c=c: e.transpose(
                        ps_tt[c % 2][:, :],
                        lg[:, c * 128:(c + 1) * 128], cst["ident_f"][0:E, 0:E]), inc="t")
                    p.wait("vector", "t", t_t)
                    v_t = p.op("vector", lambda e, c=c: e.tensor_copy(
                        sc_tm[:, c * E:(c + 1) * E],
                        ps_tt[c % 2][:, :]), inc="v")
                    if c >= 1:
                        tt_done = v_prev
                    v_prev = v_t

                def VO(fn):
                    p.op("vector", fn)

                for c in range(NCH):
                    VO(lambda e, c=c: e.tensor_tensor(
                        IA(0, c), sc_tm[:, c * E:(c + 1) * E], ebias_t[:, :], ALU.add))
                for c in range(NCH):
                    VO(lambda e, c=c: e.tensor_tensor(
                        IA(1, c)[:, 0:NB], IA(0, c)[:, 0:E:4], IA(0, c)[:, 1:E:4], ALU.max))
                for c in range(NCH):
                    VO(lambda e, c=c: e.tensor_tensor(
                        IA(1, c)[:, NB:2 * NB], IA(0, c)[:, 0:E:4], IA(0, c)[:, 1:E:4], ALU.min))
                for c in range(NCH):
                    VO(lambda e, c=c: e.tensor_tensor(
                        IA(1, c)[:, 2 * NB:3 * NB], IA(0, c)[:, 2:E:4], IA(0, c)[:, 3:E:4], ALU.max))
                for c in range(NCH):
                    VO(lambda e, c=c: e.tensor_tensor(
                        IA(1, c)[:, 3 * NB:4 * NB], IA(0, c)[:, 2:E:4], IA(0, c)[:, 3:E:4], ALU.min))
                for c in range(NCH):
                    VO(lambda e, c=c: e.tensor_tensor(
                        IA(2, c)[:, 0:NB], IA(1, c)[:, 0:NB], IA(1, c)[:, 2 * NB:3 * NB], ALU.max))
                for c in range(NCH):
                    VO(lambda e, c=c: e.tensor_tensor(
                        IA(2, c)[:, NB:2 * NB], IA(1, c)[:, 0:NB], IA(1, c)[:, 2 * NB:3 * NB], ALU.min))
                for c in range(NCH):
                    VO(lambda e, c=c: e.tensor_tensor(
                        IA(2, c)[:, 2 * NB:3 * NB], IA(1, c)[:, NB:2 * NB], IA(1, c)[:, 3 * NB:4 * NB], ALU.max))
                for c in range(NCH):
                    VO(lambda e, c=c: e.tensor_tensor(
                        IA(2, c)[:, NB:2 * NB], IA(2, c)[:, NB:2 * NB], IA(2, c)[:, 2 * NB:3 * NB], ALU.max))
                for c in range(NCH):
                    VO(lambda e, c=c: e.tensor_tensor(
                        IA(2, c)[:, 0:NB], IA(2, c)[:, 0:NB], IA(2, c)[:, NB:2 * NB], ALU.add))
                for c in range(NCH):
                    VO(lambda e, c=c: e.tensor_tensor(
                        IA(2, c)[:, 0:NB], IA(2, c)[:, 0:NB], cst["gdelta"][:, :], ALU.subtract))
                rv = None
                for c in range(NCH):
                    rv = p.op("vector", lambda e, c=c: e.reduce_max(
                        red1[:, c:c + 1], IA(2, c)[:, 0:NB], axis=AX.X), inc="v")
                p.wait("vector", "v", rv)
                rv = None
                for c in range(NCH):
                    rv = p.op("vector", lambda e, c=c: e.tensor_scalar(
                        IA(3, c)[:, 0:NB], IA(2, c)[:, 0:NB], red1[:, c:c + 1], None, ALU.is_ge), inc="v")
                p.wait("vector", "v", rv)
                rv = None
                for c in range(NCH):
                    rv = p.op("vector", lambda e, c=c: e.scalar_tensor_tensor(
                        IA(3, c)[:, NB:2 * NB], IA(3, c)[:, 0:NB], -BIG, IA(2, c)[:, 0:NB],
                        ALU.mult, ALU.add), inc="v")
                p.wait("vector", "v", rv)
                rv = None
                for c in range(NCH):
                    rv = p.op("vector", lambda e, c=c: e.reduce_max(
                        red1[:, 1 * NCH + c:1 * NCH + c + 1], IA(3, c)[:, NB:2 * NB], axis=AX.X), inc="v")
                p.wait("vector", "v", rv)
                rv = None
                for c in range(NCH):
                    rv = p.op("vector", lambda e, c=c: e.tensor_scalar(
                        IA(1, c)[:, 0:NB], IA(2, c)[:, 0:NB], red1[:, 1 * NCH + c:1 * NCH + c + 1], None, ALU.is_ge), inc="v")
                p.wait("vector", "v", rv)
                for c in range(NCH):
                    for j in range(4):
                        VO(lambda e, c=c, j=j: e.tensor_copy(IA(2, c)[:, j:E:4], IA(1, c)[:, 0:NB]))
                for c in range(NCH):
                    VO(lambda e, c=c: e.tensor_tensor(IA(4, c), IA(0, c), IA(2, c), ALU.mult))
                for c in range(NCH):
                    VO(lambda e, c=c: e.tensor_tensor(IA(4, c), IA(4, c), cst["edelta"][:, :], ALU.subtract))
                rv = None
                for c in range(NCH):
                    rv = p.op("vector", lambda e, c=c: e.reduce_max(
                        red1[:, 2 * NCH + c:2 * NCH + c + 1], IA(4, c), axis=AX.X), inc="v")
                p.wait("vector", "v", rv)
                rv = None
                for c in range(NCH):
                    rv = p.op("vector", lambda e, c=c: e.tensor_scalar(
                        IA(5, c), IA(4, c), red1[:, 2 * NCH + c:2 * NCH + c + 1], None, ALU.is_ge), inc="v")
                p.wait("vector", "v", rv)
                rv = None
                for c in range(NCH):
                    rv = p.op("vector", lambda e, c=c: e.scalar_tensor_tensor(
                        IA(5, c), IA(5, c), -BIG, IA(4, c), ALU.mult, ALU.add), inc="v")
                p.wait("vector", "v", rv)
                rv = None
                for c in range(NCH):
                    rv = p.op("vector", lambda e, c=c: e.reduce_max(
                        red1[:, 3 * NCH + c:3 * NCH + c + 1], IA(5, c), axis=AX.X), inc="v")
                p.wait("vector", "v", rv)
                rv = None
                for c in range(NCH):
                    rv = p.op("vector", lambda e, c=c: e.tensor_scalar(
                        IA(5, c), IA(4, c), red1[:, 3 * NCH + c:3 * NCH + c + 1], None, ALU.is_ge), inc="v")
                p.wait("vector", "v", rv)
                for c in range(NCH):
                    VO(lambda e, c=c: e.tensor_tensor(
                        IA(4, c), sc_tm[:, c * E:(c + 1) * E], IA(5, c), ALU.mult))
                rv = None
                for c in range(NCH):
                    rv = p.op("vector", lambda e, c=c: e.reduce_sum(
                        red1[:, 4 * NCH + c:4 * NCH + c + 1], IA(4, c), axis=AX.X), inc="v")
                p.wait("vector", "v", rv)
                rv = None
                for c in range(NCH):
                    rv = p.op("vector", lambda e, c=c: e.tensor_scalar(
                        red1[:, 4 * NCH + c:4 * NCH + c + 1], red1[:, 4 * NCH + c:4 * NCH + c + 1], 1e-20, None, ALU.add), inc="v")
                p.wait("vector", "v", rv)
                rv = None
                for c in range(NCH):
                    rv = p.op("vector", lambda e, c=c: e.reciprocal(
                        red1[:, 5 * NCH + c:5 * NCH + c + 1], red1[:, 4 * NCH + c:4 * NCH + c + 1]), inc="v")
                p.wait("vector", "v", rv)
                v_cmb = None
                for c in range(NCH):
                    v_cmb = p.op("vector", lambda e, c=c: e.tensor_scalar(
                        IA(4, c), IA(4, c), red1[:, 5 * NCH + c:5 * NCH + c + 1], None, ALU.mult), inc="v")
                p.wait("tensor", "v", v_cmb)
                v_prev = None
                cb_done = None
                for c in range(NCH):
                    if c >= 2:
                        p.wait("tensor", "v", cb_done)
                    t_t = p.op("tensor", lambda e, c=c: e.transpose(
                        ps_cb[c % 2][:, :],
                        IA(4, c), cst["ident_f"][:, :]), inc="t")
                    p.wait("vector", "t", t_t)
                    v_cp = p.op("vector", lambda e, c=c: e.tensor_copy(
                        comb_bfs[:, c * 128:(c + 1) * 128],
                        ps_cb[c % 2][:, :]), inc="v")
                    if c >= 1:
                        cb_done = v_prev
                    v_prev = v_cp

        if True:
            ps_u = [PB[0], PB[1]]
            ps_dn = [PB[2], PB[3]]
            sl_u = [Slot(), Slot()]
            sl_d = [Slot(), Slot()]
            sl_cs = [Slot() for _ in range(16)]
            routed_dep = {}
            for half in range(2):
                it = 0
                for icl in range(16):
                    ic = half * 16 + icl
                    shu_slab, shu_wait, shu_rec = ring.load(shu_d[ic], NC_D * 128)
                    for sub in range(NS):
                        ui = it % 2
                        f0 = sub * NF
                        if sub == 0:
                            p.wait("tensor", shu_wait[0], shu_wait[1])
                        if half == 0 and it == 0:
                            p.wait("tensor", "v", ln2_v)
                        sl_u[ui].gate_writer(p, "tensor")
                        t_u = None
                        for kc in range(NC_D):
                            t_u = p.op("tensor", lambda e, k=kc, u=ui, f0=f0, shu_slab=shu_slab: e.matmul(
                                ps_u[u], shu_slab[:, k * 128:(k + 1) * 128],
                                ln_pl[:, k * T_O + f0:k * T_O + f0 + NF],
                                start=(k == 0), stop=(k == NC_D - 1)), inc="t")
                        if sub == NS - 1:
                            ring.consumed(shu_rec)
                        p.wait("scalar", "t", t_u)
                        b5 = it % 2
                        sl_tfp[b5].gate_writer(p, "scalar")
                        s_rl = p.op("scalar", lambda e, u=ui, b=b5: e.activation(
                            tfp[:, b * NF:(b + 1) * NF], ps_u[u], AF.Relu), inc="s")
                        sl_u[ui].read("s", s_rl)
                        p.wait("vector", "s", s_rl)
                        sl_cs[icl].gate_writer(p, "vector")
                        v_r2 = p.op("vector", lambda e, b=b5, c=icl, f0=f0: e.tensor_tensor(
                            cs_pl[:, c * T_O + f0:c * T_O + f0 + NF],
                            tfp[:, b * NF:(b + 1) * NF], tfp[:, b * NF:(b + 1) * NF],
                            ALU.mult), inc="v")
                        sl_tfp[b5].read("v", v_r2)
                        it += 1
                act_v = p.val("v")
                it_d = 0
                for m in range(NC_D):
                    shd_src = shd_d[m][:, half * 16 * 128:(half + 1) * 16 * 128]
                    shd_slab, shd_wait, shd_rec = ring.load(shd_src, 16 * 128)
                    for sub in range(NS):
                        di = it_d % 2
                        f0 = sub * NF
                        if sub == 0:
                            p.wait("tensor", shd_wait[0], shd_wait[1])
                        if it_d == 0:
                            p.wait("tensor", "v", act_v)
                            if half == 0:
                                p.wait("tensor", "v", v_lg)
                        sl_d[di].gate_writer(p, "tensor")
                        t_d = None
                        for icl in range(16):
                            t_d = p.op("tensor", lambda e, c=icl, d=di, f0=f0, shd_slab=shd_slab: e.matmul(
                                ps_dn[d], shd_slab[:, c * 128:(c + 1) * 128],
                                cs_pl[:, c * T_O + f0:c * T_O + f0 + NF],
                                start=(c == 0), stop=(c == 15)), inc="t")
                        for icl in range(16):
                            sl_cs[icl].read("t", t_d)
                        if sub == NS - 1:
                            ring.consumed(shd_rec)
                        dst = big_pl[:, m * T_O + f0:m * T_O + f0 + NF]
                        if half == 0:
                            p.wait("scalar", "t", t_d)
                            cnt = p.op("scalar", lambda e, d=di, dst=dst: e.copy(
                                dst, ps_dn[d]), inc="s")
                            sl_d[di].read("s", cnt)
                            routed_dep[(m, sub)] = ("s", cnt)
                        else:
                            is_s, prev = routed_dep[(m, sub)]
                            p.wait("vector", is_s, prev)
                            p.wait("vector", "t", t_d)
                            cnt = p.op("vector", lambda e, d=di, dst=dst: e.tensor_tensor(
                                dst, ps_dn[d], dst, ALU.add), inc="v")
                            sl_d[di].read("v", cnt)
                            routed_dep[(m, sub)] = ("v", cnt)
                        it_d += 1
                if half == 0:
                    emit_island()
        p.barrier()
        dbg_dump("ln2", ln_pl[:, :], ot)
        dbg_dump("x4", x_pl[:, :], ot)
        if DBG and ot == 0:
            dbg_dump_small(dbg_lg, lg[:, :])
            dbg_dump_small(dbg_cw, comb_bfs[:, :])
        dbg_dump("x5", x_pl[:, :], ot)
        if True:
            ps_sum7 = [PB[4 + i][0:1, :] for i in range(NS)]
            ps_sq7 = [PB[6 + i][0:1, :] for i in range(NS)]
            ps_h = [PB[0], PB[1]]
            ps_cbc = [PB[4 + i] for i in range(NS)]
            ps_dd = [PB[2], PB[3]]
            sl_h = [Slot(), Slot()]
            sl_cbc = Slot()
            sl_dd = [Slot(), Slot()]
            sl_a = [Slot() for _ in range(16)]
            for grp in range(E // 2):
                es = (2 * grp, 2 * grp + 1)
                it_h = 0
                for ei, e_ in enumerate(es):
                    sl_cbc.gate_writer(p, "tensor")
                    t_cb = None
                    for sub in range(NS):
                        f0 = sub * NF
                        t_cb = p.op("tensor", lambda e, e_=e_, ss=sub, f0=f0: e.matmul(
                            ps_cbc[ss][:, :], cst["bsel16"][:, e_ * 128:(e_ + 1) * 128],
                            comb_bfs[:, f0:f0 + NF], start=True, stop=True), inc="t")
                    for ic in range(NC_EI):
                        wup_slab, wup_wait, wup_rec = ring.load(wup_d[e_, ic], NC_D * 128)
                        ch = ei * NC_EI + ic
                        for sub in range(NS):
                            hi = it_h % 2
                            f0 = sub * NF
                            if sub == 0:
                                p.wait("tensor", wup_wait[0], wup_wait[1])
                            sl_h[hi].gate_writer(p, "tensor")
                            t_h = None
                            for kc in range(NC_D):
                                t_h = p.op("tensor", lambda e, k=kc, h=hi, f0=f0, wup_slab=wup_slab: e.matmul(
                                    ps_h[h], wup_slab[:, k * 128:(k + 1) * 128],
                                    ln_pl[:, k * T_O + f0:k * T_O + f0 + NF],
                                    start=(k == 0), stop=(k == NC_D - 1)), inc="t")
                            if sub == NS - 1:
                                ring.consumed(wup_rec)
                            p.wait("scalar", "t", t_h)
                            b = it_h % 2
                            sl_tfp[b].gate_writer(p, "scalar")
                            s_rl = p.op("scalar", lambda e, h=hi, b=b: e.activation(
                                tfp[:, b * NF:(b + 1) * NF], ps_h[h], AF.Relu), inc="s")
                            sl_h[hi].read("s", s_rl)
                            p.wait("vector", "s", s_rl)
                            if it_h == 0:
                                p.wait("vector", "t", t_cb)
                            sl_tfp2[b].gate_writer(p, "vector")
                            p.op("vector", lambda e, b=b, ss=sub: e.tensor_tensor(
                                tfp2[:, b * NF:(b + 1) * NF],
                                tfp[:, b * NF:(b + 1) * NF],
                                ps_cbc[ss][:, :], ALU.mult))
                            sl_a[ch].gate_writer(p, "vector")
                            v_a = p.op("vector", lambda e, ch=ch, b=b, f0=f0: e.tensor_tensor(
                                cs_pl[:, ch * T_O + f0:ch * T_O + f0 + NF],
                                tfp2[:, b * NF:(b + 1) * NF],
                                tfp[:, b * NF:(b + 1) * NF], ALU.mult), inc="v")
                            sl_tfp[b].read("v", v_a)
                            sl_tfp2[b].read("v", v_a)
                            sl_cbc.read("v", v_a)
                            it_h += 1
                a_done_v = p.val("v")
                it_d = 0
                for m in range(NC_D):
                    slabs = []
                    for ei2, e2 in enumerate(es):
                        sl2, wt2, rec2 = ring.load(wdn_d[e2, m], NC_EI * 128)
                        slabs.append((sl2, wt2, rec2))
                    for sub in range(NS):
                        di = it_d % 2
                        f0 = sub * NF
                        if sub == 0:
                            for sl2, wt2, rec2 in slabs:
                                p.wait("tensor", wt2[0], wt2[1])
                        if it_d == 0:
                            p.wait("tensor", "v", a_done_v)
                        sl_dd[di].gate_writer(p, "tensor")
                        t_d = None
                        for ei2 in range(2):
                            sl2 = slabs[ei2][0]
                            for ic in range(NC_EI):
                                ch = ei2 * NC_EI + ic
                                t_d = p.op("tensor", lambda e, sl=sl2, ic=ic, ch=ch, d=di, f0=f0, ei2=ei2: e.matmul(
                                    ps_dd[d], sl[:, ic * 128:(ic + 1) * 128],
                                    cs_pl[:, ch * T_O + f0:ch * T_O + f0 + NF],
                                    start=(ei2 == 0 and ic == 0), stop=(ei2 == 1 and ic == NC_EI - 1)),
                                    inc="t")
                        for ch in range(16):
                            sl_a[ch].read("t", t_d)
                        if sub == NS - 1:
                            for sl2, wt2, rec2 in slabs:
                                ring.consumed(rec2)
                        is_s, prev = routed_dep[(m, sub)]
                        p.wait("vector", is_s, prev)
                        p.wait("vector", "t", t_d)
                        dst = big_pl[:, m * T_O + f0:m * T_O + f0 + NF]
                        v_add = p.op("vector", lambda e, d=di, dst=dst: e.tensor_tensor(
                            dst, ps_dd[d], dst, ALU.add), inc="v")
                        sl_dd[di].read("v", v_add)
                        routed_dep[(m, sub)] = ("v", v_add)
                        it_d += 1
                        if grp == E // 2 - 1 and sub == NS - 1:
                            p.wait("scalar", "v", v_add)
                            stats_chunk(big_pl[:, m * T_O:(m + 1) * T_O], m,
                                        ps_sum7, ps_sq7, m % 2)
        p.barrier()
        dbg_dump("x6", x_pl[:, :], ot)
        dbg_dump("ffn", big_pl[:, :], ot)

        if True:
            v_rows = ln_rows_from(ps_sum7, ps_sq7)
        if DBG and ot == 0:
            dbg_dump_small(dbg_rows, rows[:, :])
        if True:
            ps_rb = [PB[4 + i] for i in range(NS)]
            ps_t1 = [PB[6 + i] for i in range(NS)]
            s_bc = ln_bcast(ps_rb, ps_t1, v_rows)
            p.wait("vector", "s", s_bc)
            v_out = None
            for kc in range(NC_D):
                c0 = kc * T_O
                for sub in range(NS):
                    f0 = sub * NF
                    b = (kc * NS + sub) % 2
                    sl_xsq[b].gate_writer(p, "vector")
                    p.op("vector", lambda e, c=c0, b=b, f0=f0, ss=sub: e.tensor_tensor(
                        xsq[:, b * T_O:b * T_O + NF],
                        big_pl[:, c + f0:c + f0 + NF],
                        sig_bf[:, ss * NF:(ss + 1) * NF], ALU.mult))
                    p.op("vector", lambda e, b=b, ss=sub: e.tensor_tensor(
                        xsq[:, b * T_O:b * T_O + NF], xsq[:, b * T_O:b * T_O + NF],
                        sig_sc[:, ss * NF:(ss + 1) * NF], ALU.subtract))
                    v_out = p.op("vector", lambda e, c=c0, b=b, f0=f0: e.tensor_tensor(
                        x_pl[:, c + f0:c + f0 + NF], x_pl[:, c + f0:c + f0 + NF],
                        xsq[:, b * T_O:b * T_O + NF], ALU.add), inc="v")
                    sl_xsq[b].read("v", v_out)
                    for ss2 in range(NS):
                        sl_sig[ss2].read("v", v_out)
                        sl_sigsc[ss2].read("v", v_out)
                sid = kc % 4
                p.wait("gpsimd", "v", v_out)
                if xs_uses[sid] > 0:
                    p.wait("gpsimd", f"xs{sid}", 16 * xs_uses[sid])
                xs_uses[sid] += 1
                g_dma(out_d[ot][:, c0:c0 + T_O], x_pl[:, c0:c0 + T_O], f"xs{sid}")
                if ot + 1 < n_ot:
                    p.wait("gpsimd", f"xs{sid}", 16 * xs_uses[sid])
                    if xl_uses[sid] > 0:
                        p.wait("gpsimd", f"xl{sid}", 16 * xl_uses[sid])
                    xl_uses[sid] += 1
                    xl_at[(ot + 1, kc)] = 16 * xl_uses[sid]
                    g_dma(x_pl[:, c0:c0 + T_O], x_in[ot + 1][:, c0:c0 + T_O], f"xl{sid}")
            p.wait("tensor", "v", v_out)
        p.barrier()

    for i in range(4):
        p.wait("sync", f"xs{i}", 16 * xs_uses[i])

    with nc.Block() as block:
        @block.sync
        def _(eng):
            for f in p.ops["sync"]:
                f(eng)

        @block.gpsimd
        def _(eng):
            for f in p.ops["gpsimd"]:
                f(eng)

        @block.tensor
        def _(eng):
            for f in p.ops["tensor"]:
                f(eng)

        @block.vector
        def _(eng):
            for f in p.ops["vector"]:
                f(eng)

        @block.scalar
        def _(eng):
            for f in p.ops["scalar"]:
                f(eng)

    return nc



def _in_maps(per_core, shared):
    maps = []
    for xs in per_core:
        m = {"x": xs}
        m.update(shared)
        maps.append(m)
    return maps


def _reassemble(results, n_ot, T_O):
    outs = []
    for r in results:
        a = np.asarray(r["out"]).reshape(n_ot, 128, NC_D, T_O)
        a = a.transpose(2, 1, 0, 3).reshape(D, n_ot * T_O).T
        outs.append(a)
    full = np.concatenate(outs, axis=0)
    return np.ascontiguousarray(full.reshape(2, 8192, D).astype(np.float32))


def kernel(**inputs):
    per_core, shared, n_ot = _prep(inputs)
    nc = build(n_ot, CFG["T_O"])
    maps = _in_maps(per_core, shared)
    res = run_bass_kernel_spmd(nc, maps, core_ids=list(range(len(maps))))
    return _reassemble(res.results, n_ot, CFG["T_O"])


# version 12
# speedup vs baseline: 1.0366x; 1.0366x over previous
import contextlib
import numpy as np
import ml_dtypes
import concourse.bass as bass
import concourse.mybir as mybir
from concourse.bass_utils import run_bass_kernel_spmd

F32 = mybir.dt.float32
BF16 = mybir.dt.bfloat16
AF = mybir.ActivationFunctionType
ALU = mybir.AluOpType
AX = mybir.AxisListType

D = 2048
NC_D = 16
H = 16
NB = 4
E = 16
EI = 1024
NC_EI = 8
SI = 4096
NC_SI = 32

EPS_LN = 1e-5
EPS_MEM = 1e-6
BIG = 1.0e30
DELTA = 1e-7

CFG = {"T_O": 1024, "T_CORE": 2048, "CHAIN": False, "DBG": False}



def _tile_lhsT(W):
    K, M = W.shape
    nk, nm = K // 128, M // 128
    return np.ascontiguousarray(
        W.reshape(nk, 128, nm, 128).transpose(2, 1, 0, 3).reshape(nm, 128, nk * 128))


def _prep(inputs):
    T_O = CFG["T_O"]
    Tc = CFG["T_CORE"]
    n_ot = Tc // T_O
    x = np.asarray(inputs["L0_attn_hidden_states"], dtype=np.float32)
    xt = np.ascontiguousarray(x.reshape(-1, D))
    per_core = []
    n_cores = xt.shape[0] // Tc
    for c in range(n_cores):
        xs = xt[c * Tc:(c + 1) * Tc].T
        xs = xs.reshape(NC_D, 128, n_ot, T_O).transpose(2, 1, 0, 3)
        per_core.append(np.ascontiguousarray(xs.reshape(n_ot, 128, NC_D * T_O)))

    Wq = np.asarray(inputs["L0_attn_Wq"], dtype=np.float32)
    Wo = np.asarray(inputs["L0_attn_Wo"], dtype=np.float32)
    shared = {
        "wq": _tile_lhsT(np.ascontiguousarray(Wq.T)),
        "wo": _tile_lhsT(np.ascontiguousarray(Wo.T)),
        "mem": np.ascontiguousarray(
            np.asarray(inputs["L0_attn_memories"], dtype=np.float32)
            .transpose(1, 2, 0, 3).reshape(H, 128, NB * 128)),
        "mnt": np.ascontiguousarray(
            np.asarray(inputs["L0_attn_memory_norms"], dtype=np.float32)
            .transpose(2, 1, 0).reshape(128, H * NB)),
        "mn4": np.ascontiguousarray(
            np.asarray(inputs["L0_attn_memory_norms"], dtype=np.float32)
            .reshape(NB, H * 128)),
        "bankw": np.ascontiguousarray(
            np.asarray(inputs["L0_attn_bank_weights"], dtype=np.float32)),
        "rwt": np.ascontiguousarray(
            np.asarray(inputs["L0_ffn_router_weight"], dtype=np.float32)
            .T.reshape(NC_D, 128, E).transpose(1, 0, 2).reshape(128, NC_D * E)),
        "ebias": np.ascontiguousarray(np.broadcast_to(
            np.asarray(inputs["L0_ffn_e_bias"], dtype=np.float32)[None, :], (128, E))),
        "wup": np.ascontiguousarray(
            np.asarray(inputs["L0_ffn_w_up"], dtype=np.float32)
            .reshape(E, NC_D, 128, NC_EI, 128).transpose(0, 3, 2, 1, 4)
            .reshape(E, NC_EI, 128, NC_D * 128)),
        "wdn": np.ascontiguousarray(
            np.asarray(inputs["L0_ffn_w_down"], dtype=np.float32)
            .reshape(E, NC_EI, 128, NC_D, 128).transpose(0, 3, 2, 1, 4)
            .reshape(E, NC_D, 128, NC_EI * 128)),
        "shu": _tile_lhsT(np.asarray(inputs["L0_ffn_shared_up"], dtype=np.float32)),
        "shd": _tile_lhsT(np.asarray(inputs["L0_ffn_shared_down"], dtype=np.float32)),
    }
    return per_core, shared, n_ot



class Prog:
    ENGINES = ("sync", "gpsimd", "tensor", "vector", "scalar")

    def __init__(self, nc, chain=False):
        self.nc = nc
        self.chain = chain
        self.ops = {e: [] for e in self.ENGINES}
        self.sems = {}
        self.cnt = {}

    def sem(self, name):
        h = self.nc.alloc_semaphore(name)
        self.sems[name] = h
        self.cnt[name] = 0
        return h

    def op(self, eng, build, inc=None, inc_amt=1):
        if self.chain and eng in ("vector", "scalar"):
            own = "v" if eng == "vector" else "s"
            self.wait(eng, own, self.cnt[own])
            if inc is None:
                inc = own
        if inc is None:
            self.ops[eng].append(build)
            return None
        h = self.sems[inc]
        self.cnt[inc] += inc_amt
        val = self.cnt[inc]
        self.ops[eng].append(lambda e, b=build, hh=h, a=inc_amt: b(e).then_inc(hh, a))
        return val

    def wait(self, eng, sem, val):
        if val is None or val <= 0:
            return
        h = self.sems[sem]
        self.ops[eng].append(lambda e, hh=h, v=val: e.wait_ge(hh, v))

    def val(self, sem):
        return self.cnt[sem]

    def barrier(self):
        tv, vv, sv = self.val("t"), self.val("v"), self.val("s")
        self.wait("tensor", "v", vv)
        self.wait("tensor", "s", sv)
        self.wait("vector", "t", tv)
        self.wait("scalar", "t", tv)
        self.wait("vector", "s", sv)
        self.wait("scalar", "v", vv)


class Slot:
    def __init__(self):
        self.rd = {"t": 0, "v": 0, "s": 0}

    def gate_writer(self, p, eng):
        for sem, v in self.rd.items():
            if v > 0:
                p.wait(eng, sem, v)
        self.rd = {"t": 0, "v": 0, "s": 0}

    def read(self, sem, val):
        if val is not None and val > self.rd[sem]:
            self.rd[sem] = val


class Ring:
    def __init__(self, p, buf_ap, n_elems, n_sems=8, tag="wr"):
        self.p = p
        self.buf = buf_ap
        self.cap = n_elems
        self.pos = 0
        self.live = []
        self.sem_names = []
        for i in range(n_sems):
            name = f"{tag}{i}"
            p.sem(name)
            self.sem_names.append(name)
        self.sem_uses = {n: 0 for n in self.sem_names}
        self.next_sem = 0
        self.sid = 0

    def load(self, src_ap, n_free):
        assert n_free <= self.cap
        if self.pos + n_free > self.cap:
            self.pos = 0
        start, end = self.pos, self.pos + n_free
        need_t = 0
        for (s0, e0, done_t, sid0) in self.live:
            if s0 < end and e0 > start:
                assert done_t is not None, f"ring overlap with unconsumed slab {sid0}"
                need_t = max(need_t, done_t)
        self.live = [r for r in self.live if not (r[0] < end and r[1] > start)]
        sem = self.sem_names[self.next_sem]
        self.next_sem = (self.next_sem + 1) % len(self.sem_names)
        if self.sem_uses[sem] > 0:
            self.p.wait("gpsimd", sem, 16 * self.sem_uses[sem])
        if need_t > 0:
            self.p.wait("gpsimd", "t", need_t)
        dst = self.buf[:, start:end]
        self.p.op("gpsimd", lambda e, d=dst, s=src_ap: e.dma_start(out=d, in_=s),
                  inc=sem, inc_amt=16)
        self.sem_uses[sem] += 1
        rec = [start, end, None, self.sid]
        self.live.append(rec)
        self.sid += 1
        self.pos = end
        return dst, (sem, 16 * self.sem_uses[sem]), rec

    def consumed(self, rec):
        rec[2] = self.p.val("t")



def build(n_ot, T_O):
    NF = min(512, T_O)
    NS = T_O // NF
    NCH = T_O // 128
    nc = bass.Bass("TRN2", target_bir_lowering=False)

    x_in = nc.dram_tensor("x", [n_ot, 128, NC_D * T_O], F32, kind="ExternalInput").ap()
    out_d = nc.dram_tensor("out", [n_ot, 128, NC_D * T_O], F32, kind="ExternalOutput").ap()
    wq_d = nc.dram_tensor("wq", [NC_D, 128, NC_D * 128], F32, kind="ExternalInput").ap()
    wo_d = nc.dram_tensor("wo", [NC_D, 128, NC_D * 128], F32, kind="ExternalInput").ap()
    mem_d = nc.dram_tensor("mem", [H, 128, NB * 128], F32, kind="ExternalInput").ap()
    mnt_d = nc.dram_tensor("mnt", [128, H * NB], F32, kind="ExternalInput").ap()
    mn4_d = nc.dram_tensor("mn4", [NB, H * 128], F32, kind="ExternalInput").ap()
    bankw_d = nc.dram_tensor("bankw", [H, NB], F32, kind="ExternalInput").ap()
    rwt_d = nc.dram_tensor("rwt", [128, NC_D * E], F32, kind="ExternalInput").ap()
    ebias_d = nc.dram_tensor("ebias", [128, E], F32, kind="ExternalInput").ap()
    wup_d = nc.dram_tensor("wup", [E, NC_EI, 128, NC_D * 128], F32, kind="ExternalInput").ap()
    wdn_d = nc.dram_tensor("wdn", [E, NC_D, 128, NC_EI * 128], F32, kind="ExternalInput").ap()
    shu_d = nc.dram_tensor("shu", [NC_SI, 128, NC_D * 128], F32, kind="ExternalInput").ap()
    shd_d = nc.dram_tensor("shd", [NC_D, 128, NC_SI * 128], F32, kind="ExternalInput").ap()
    DBG = CFG.get("DBG", False)
    if DBG:
        dbg_d = {k: nc.dram_tensor(f"dbg_{k}", [128, NC_D * T_O], F32, kind="ExternalOutput").ap()
                 for k in ("ln1", "comb", "attn", "x1", "ln2", "ffn", "x4", "x5", "x6")}
        dbg_lg = nc.dram_tensor("dbg_lg", [E, T_O], F32, kind="ExternalOutput").ap()
        dbg_rows = nc.dram_tensor("dbg_rows", [1, 4 * T_O], F32, kind="ExternalOutput").ap()
        dbg_cw = nc.dram_tensor("dbg_cw", [E, T_O], F32, kind="ExternalOutput").ap()
        p_dbg = [0]

    inl = {
        "ones128_bf": np.ones((128, 1), np.float32).astype(ml_dtypes.bfloat16),
        "onesr_bf": np.ones((1, 128), np.float32).astype(ml_dtypes.bfloat16),
        "onesr16_f": np.ones((1, 16), np.float32),
        "ones128_f": np.ones((128, 1), np.float32),
        "ident_f": np.eye(128, dtype=np.float32),
        "gdelta": np.ascontiguousarray(np.broadcast_to(
            (np.arange(NB) * DELTA).astype(np.float32)[None, :], (128, NB))),
        "edelta": np.ascontiguousarray(np.broadcast_to(
            (np.arange(E) * DELTA).astype(np.float32)[None, :], (128, E))),
        "bsel4": np.kron(np.eye(NB, dtype=np.float32),
                         np.ones((1, 128), np.float32)).astype(ml_dtypes.bfloat16),
        "bsel16": np.kron(np.eye(E, dtype=np.float32),
                          np.ones((1, 128), np.float32)).astype(ml_dtypes.bfloat16),
    }
    inl_d = {k: nc.inline_tensor(v, "c_" + k) for k, v in inl.items()}

    p = Prog(nc, chain=CFG.get("CHAIN", False))
    for s in ("t", "v", "s", "ldc", "ldg", "ldm0", "ldm1", "ldm2"):
        p.sem(s)
    for i in range(4):
        p.sem(f"xs{i}")
        p.sem(f"xl{i}")
    xs_uses = [0, 0, 0, 0]
    xl_uses = [0, 0, 0, 0]
    xl_at = {}

    A = nc.alloc_sbuf_tensor
    x_pl = A("x_pl", [128, NC_D * T_O], BF16).ap()
    ln_pl = A("ln_pl", [128, NC_D * T_O], BF16).ap()
    cs_pl = A("cs_pl", [128, NC_D * T_O], BF16).ap()
    big_pl = A("big_pl", [128, NC_D * T_O], BF16).ap()
    xsq = A("xsq", [128, 2 * T_O], BF16).ap()
    sig_bf = A("sig_bf", [128, 3 * NF], BF16).ap()
    sig_sc = A("sig_sc", [128, 2 * NF], BF16).ap()
    expb = A("expb", [128, 2 * NF], F32).ap()
    tfp = A("tfp", [128, 2 * NF], F32).ap()
    tfp2 = A("tfp2", [128, 2 * NF], F32).ap()
    rows = A("rows", [1, 4 * T_O], F32).ap()
    rows_bf = A("rows_bf", [1, 2 * T_O], BF16).ap()
    nrm = A("nrm", [NB, 2 * NF], F32).ap()
    nrm_bf = A("nrm_bf", [NB, 2 * NF], BF16).ap()
    lg = A("lg", [E, T_O], F32).ap()
    sc_tm = A("sc_tm", [128, NCH * E], F32).ap()
    isl = A("isl", [128, 6 * NCH * E], F32).ap()
    red1 = A("red1", [128, 6 * NCH], F32).ap()
    comb_bfs = A("comb_bfs", [E, T_O], BF16).ap()
    mnt = A("mnt_s", [128, H * NB], BF16).ap()
    rwt = A("rwt_s", [128, NC_D * E], F32).ap()
    ebias_t = A("ebias_s", [128, E], F32).ap()
    bankw = A("bankw_s", [H, NB], F32).ap()
    mn4 = A("mn4_s", [NB, H * 128], BF16).ap()
    wTa = A("wTa", [NB, H], F32).ap()
    lnW = A("lnW", [NB, H], F32).ap()
    wsm = A("wsm", [H, NB], F32).ap()
    rwsum = A("rwsum", [E, 1], F32).ap()
    tiny = A("tiny", [E, 8], F32).ap()
    mem_bf = A("mem_bf", [128, 3 * NB * 128], BF16).ap()
    cst = {}
    for k, v in inl.items():
        dt = BF16 if v.dtype == ml_dtypes.bfloat16 else F32
        cst[k] = A("s_" + k, list(v.shape), dt).ap()

    RING_ELEMS = 4096
    ring_buf = A("ring", [128, RING_ELEMS], BF16).ap()
    ring = Ring(p, ring_buf, RING_ELEMS, n_sems=8)

    PB = [nc.alloc_psum_tensor(f"PB{i}", [128, NF], F32).ap() for i in range(8)]

    def IA(slot, c, w=E):
        base = (slot * NCH + c) * E
        return isl[:, base:base + w]

    sl_xsq = [Slot(), Slot()]
    sl_sig = [Slot(), Slot(), Slot()]
    sl_sigsc = [Slot(), Slot()]
    sl_expb = [Slot(), Slot()]
    sl_tfp = [Slot(), Slot()]
    sl_tfp2 = [Slot(), Slot()]
    sl_nrm = Slot()
    sl_nrmbf = [Slot(), Slot()]
    sl_rows = Slot()
    sl_mem = [Slot(), Slot(), Slot()]
    memslab_uses = [0, 0, 0]

    def sync_dma(dst, src, sem):
        return p.op("sync", lambda e, d=dst, s=src: e.dma_start(out=d, in_=s),
                    inc=sem, inc_amt=16)

    def g_dma(dst, src, sem):
        return p.op("gpsimd", lambda e, d=dst, s=src: e.dma_start(out=d, in_=s),
                    inc=sem, inc_amt=16)

    n_const = 0
    for k in cst:
        sync_dma(cst[k][:, :], inl_d[k].ap()[:, :], "ldc")
        n_const += 1
    for dst, src in ((rwt, rwt_d), (ebias_t, ebias_d), (bankw, bankw_d)):
        sync_dma(dst[:, :], src[:, :], "ldc")
        n_const += 1
    ldc_all = 16 * n_const
    g_dma(mnt[:, :], mnt_d[:, :], "ldg")
    g_dma(mn4[:, :], mn4_d[:, :], "ldg")
    ldg_all = 32
    for kc in range(NC_D):
        sid = kc % 4
        if xl_uses[sid] > 0:
            p.wait("gpsimd", f"xl{sid}", 16 * xl_uses[sid])
        xl_uses[sid] += 1
        xl_at[(0, kc)] = 16 * xl_uses[sid]
        g_dma(x_pl[:, kc * T_O:(kc + 1) * T_O], x_in[0][:, kc * T_O:(kc + 1) * T_O], f"xl{sid}")

    for eng in ("vector", "scalar", "tensor"):
        p.wait(eng, "ldc", ldc_all)
        p.wait(eng, "ldg", ldg_all)

    v_rm = p.op("vector", lambda e: e.reduce_max(tiny[0:H, 0:1], bankw[:, :], axis=AX.X), inc="v")
    v_as = p.op("vector", lambda e: e.reduce_sum(tiny[0:NB, 1:2], mn4[:, :], axis=AX.X), inc="v")
    p.wait("vector", "v", v_rm)
    v_sub = p.op("vector", lambda e: e.tensor_scalar(
        wsm[:, :], bankw[:, :], tiny[0:H, 0:1], None, ALU.subtract), inc="v")
    p.wait("scalar", "v", v_sub)
    s_exp = p.op("scalar", lambda e: e.activation(wsm[:, :], wsm[:, :], AF.Exp), inc="s")
    p.wait("vector", "s", s_exp)
    v_sum = p.op("vector", lambda e: e.reduce_sum(tiny[0:H, 2:3], wsm[:, :], axis=AX.X), inc="v")
    v_act = p.op("vector", lambda e: e.tensor_scalar(
        tiny[0:NB, 3:4], tiny[0:NB, 1:2], EPS_MEM, None, ALU.is_ge), inc="v")
    p.wait("vector", "v", v_sum)
    v_rcp = p.op("vector", lambda e: e.reciprocal(tiny[0:H, 4:5], tiny[0:H, 2:3]), inc="v")
    p.wait("vector", "v", v_rcp)
    v_wsm = p.op("vector", lambda e: e.tensor_scalar(
        wsm[:, :], wsm[:, :], tiny[0:H, 4:5], None, ALU.mult), inc="v")
    if True:
        ps_i0 = PB[0][0:NB, 0:H]
        ps_i1 = PB[1][0:E, 0:1]
        p.wait("tensor", "v", v_wsm)
        p.op("tensor", lambda e: e.transpose(
            ps_i0, wsm[:, :], cst["ident_f"][0:H, 0:H]), inc="t")
        t_rw = None
        for kc in range(NC_D):
            t_rw = p.op("tensor", lambda e, k=kc: e.matmul(
                ps_i1, rwt[:, k * E:(k + 1) * E], cst["ones128_f"][:, :],
                start=(k == 0), stop=(k == NC_D - 1)), inc="t")
        p.wait("vector", "t", t_rw)
        p.wait("vector", "v", v_act)
        v_wta = p.op("vector", lambda e: e.tensor_scalar(
            wTa[:, :], ps_i0, tiny[0:NB, 3:4], None, ALU.mult), inc="v")
        p.op("vector", lambda e: e.tensor_copy(rwsum[:, :], ps_i1), inc="v")
        p.wait("scalar", "v", v_wta)
        p.op("scalar", lambda e: e.activation(lnW[:, :], wTa[:, :], AF.Ln), inc="s")
    p.barrier()


    def ln_rows_from(ps_sum, ps_sq):
        t_now = p.val("t")
        p.wait("vector", "t", t_now)
        sl_rows.gate_writer(p, "vector")
        v_ev = None
        for sub in range(NS):
            f0 = sub * NF
            p.op("vector", lambda e, ss=sub, f0=f0: e.tensor_scalar(
                rows[:, f0:f0 + NF], ps_sum[ss], 1.0 / D, None, ALU.mult), inc="v")
            v_ev = p.op("vector", lambda e, ss=sub, f0=f0: e.tensor_scalar(
                rows[:, T_O + f0:T_O + f0 + NF], ps_sq[ss], 1.0 / D, None, ALU.mult), inc="v")
        p.wait("vector", "v", v_ev)
        p.op("vector", lambda e: e.tensor_tensor(
            rows[:, 2 * T_O:3 * T_O], rows[:, 0:T_O], rows[:, 0:T_O], ALU.mult))
        p.op("vector", lambda e: e.tensor_tensor(
            rows[:, 2 * T_O:3 * T_O], rows[:, T_O:2 * T_O], rows[:, 2 * T_O:3 * T_O], ALU.subtract))
        v_var = p.op("vector", lambda e: e.tensor_scalar(
            rows[:, 2 * T_O:3 * T_O], rows[:, 2 * T_O:3 * T_O], EPS_LN, None, ALU.add), inc="v")
        p.wait("vector", "v", v_var)
        p.wait("scalar", "v", v_var)
        p.op("scalar", lambda e: e.activation(
            rows[:, T_O:2 * T_O], rows[:, 2 * T_O:3 * T_O], AF.Ln))
        s_rstd = p.op("scalar", lambda e: e.activation(
            rows[:, 2 * T_O:3 * T_O], rows[:, T_O:2 * T_O], AF.Exp, scale=-0.5), inc="s")
        p.wait("vector", "s", s_rstd)
        p.op("vector", lambda e: e.tensor_tensor(
            rows[:, 3 * T_O:4 * T_O], rows[:, 0:T_O], rows[:, 2 * T_O:3 * T_O], ALU.mult))
        p.op("vector", lambda e: e.tensor_copy(rows_bf[:, 0:T_O], rows[:, 2 * T_O:3 * T_O]))
        v_last = p.op("vector", lambda e: e.tensor_copy(
            rows_bf[:, T_O:2 * T_O], rows[:, 3 * T_O:4 * T_O]), inc="v")
        return v_last

    def ln_bcast(ps_rb, ps_t1, after_v, to_sbuf=True):
        p.wait("tensor", "v", after_v)
        t_last = None
        for sub in range(NS):
            f0 = sub * NF
            p.op("tensor", lambda e, ss=sub, f0=f0, ps_rb=ps_rb: e.matmul(
                ps_rb[ss][:, :], cst["onesr_bf"][0:1, :],
                rows_bf[0:1, f0:f0 + NF], start=True, stop=True))
            t_last = p.op("tensor", lambda e, ss=sub, f0=f0, ps_t1=ps_t1: e.matmul(
                ps_t1[ss][:, :], cst["onesr_bf"][0:1, :],
                rows_bf[0:1, T_O + f0:T_O + f0 + NF], start=True, stop=True), inc="t")
        sl_rows.read("t", t_last)
        if not to_sbuf:
            return t_last
        p.wait("scalar", "t", t_last)
        for _sl in sl_sig:
            _sl.gate_writer(p, "scalar")
        sl_sigsc[0].gate_writer(p, "scalar")
        sl_sigsc[1].gate_writer(p, "scalar")
        s_last = None
        for sub in range(NS):
            p.op("scalar", lambda e, ss=sub: e.copy(
                sig_bf[:, ss * NF:(ss + 1) * NF], ps_rb[ss][:, :]))
            s_last = p.op("scalar", lambda e, ss=sub: e.copy(
                sig_sc[:, ss * NF:(ss + 1) * NF], ps_t1[ss][:, :]), inc="s")
        return s_last

    def stats_chunk(src_bf, kc, ps_sum, ps_sq, buf, dep_v=None, dep_s=None):
        sl_xsq[buf].gate_writer(p, "scalar")
        if dep_v:
            p.wait("scalar", "v", dep_v)
        if dep_s:
            p.wait("scalar", "s", dep_s)
        s_c = p.op("scalar", lambda e, b=buf, s=src_bf: e.activation(
            xsq[:, b * T_O:(b + 1) * T_O], s, AF.Square), inc="s")
        p.wait("tensor", "s", s_c)
        if dep_v:
            p.wait("tensor", "v", dep_v)
        tl = None
        for sub in range(NS):
            f0 = sub * NF
            p.op("tensor", lambda e, k=kc, ss=sub, src=src_bf, f0=f0: e.matmul(
                ps_sum[ss], cst["ones128_bf"][:, :], src[:, f0:f0 + NF],
                start=(k == 0), stop=(k == NC_D - 1)))
            tl = p.op("tensor", lambda e, k=kc, ss=sub, b=buf, f0=f0: e.matmul(
                ps_sq[ss], cst["ones128_bf"][:, :],
                xsq[:, b * T_O + f0:b * T_O + f0 + NF],
                start=(k == 0), stop=(k == NC_D - 1)), inc="t")
        sl_xsq[buf].read("t", tl)
        return tl

    def load_mem_slab(h):
        buf = h % 3
        sem = f"ldm{buf}"
        if memslab_uses[buf] > 0:
            p.wait("gpsimd", sem, 16 * memslab_uses[buf])
            p.wait("gpsimd", "t", sl_mem[buf].rd["t"])
        p.op("gpsimd", lambda e, h=h, b=buf: e.dma_start(
            out=mem_bf[:, b * NB * 128:(b + 1) * NB * 128], in_=mem_d[h]),
            inc=sem, inc_amt=16)
        memslab_uses[buf] += 1
        return buf, (sem, 16 * memslab_uses[buf])

    def dbg_dump(name, src_pl, ot):
        if not DBG or ot != 0:
            return
        p.wait("gpsimd", "v", p.val("v"))
        p.wait("gpsimd", "s", p.val("s"))
        p.wait("gpsimd", "t", p.val("t"))
        p_dbg[0] += 16
        p.op("gpsimd", lambda e, d=dbg_d[name], s=src_pl: e.dma_start(out=d, in_=s),
             inc="ldc", inc_amt=16)
        val = p.val("ldc")
        for eng in ("tensor", "vector", "scalar", "gpsimd"):
            p.wait(eng, "ldc", val)

    def dbg_dump_small(dst, src_ap):
        if not DBG:
            return
        p.wait("gpsimd", "v", p.val("v"))
        p.op("gpsimd", lambda e, d=dst, s=src_ap: e.dma_start(out=d, in_=s),
             inc="ldc", inc_amt=16)
        val = p.val("ldc")
        for eng in ("tensor", "vector", "scalar", "gpsimd"):
            p.wait(eng, "ldc", val)

    for ot in range(n_ot):

        if True:
            ps_sum = [PB[4 + i][0:1, :] for i in range(NS)]
            ps_sq = [PB[6 + i][0:1, :] for i in range(NS)]
            for kc in range(NC_D):
                sid = kc % 4
                p.wait("scalar", f"xl{sid}", xl_at[(ot, kc)])
                p.wait("vector", f"xl{sid}", xl_at[(ot, kc)])
                p.wait("tensor", f"xl{sid}", xl_at[(ot, kc)])
                stats_chunk(x_pl[:, kc * T_O:(kc + 1) * T_O], kc, ps_sum, ps_sq, kc % 2)
            v_rows = ln_rows_from(ps_sum, ps_sq)
        if True:
            ps_rb = [PB[4 + i] for i in range(NS)]
            ps_t1 = [PB[6 + i] for i in range(NS)]
            s_bc = ln_bcast(ps_rb, ps_t1, v_rows)
            p.wait("vector", "s", s_bc)
            ln1_v = None
            for kc in range(NC_D):
                c0 = kc * T_O
                for sub in range(NS):
                    f0 = sub * NF
                    b = (kc * NS + sub) % 2
                    sl_xsq[b].gate_writer(p, "vector")
                    p.op("vector", lambda e, c=c0, b=b, f0=f0, ss=sub: e.tensor_tensor(
                        xsq[:, b * T_O:b * T_O + NF], x_pl[:, c + f0:c + f0 + NF],
                        sig_bf[:, ss * NF:(ss + 1) * NF], ALU.mult))
                    ln1_v = p.op("vector", lambda e, c=c0, b=b, f0=f0, ss=sub: e.tensor_tensor(
                        ln_pl[:, c + f0:c + f0 + NF], xsq[:, b * T_O:b * T_O + NF],
                        sig_sc[:, ss * NF:(ss + 1) * NF], ALU.subtract), inc="v")
                    sl_xsq[b].read("v", ln1_v)
                    for ss2 in range(NS):
                        sl_sig[ss2].read("v", ln1_v)
                        sl_sigsc[ss2].read("v", ln1_v)
            p.wait("tensor", "v", ln1_v)
        p.barrier()
        dbg_dump("ln1", ln_pl[:, :], ot)

        if True:
            ps_q = [PB[0], PB[1]]
            ps_n = PB[6][0:NB, 0:NF]
            ps_rn = [PB[2], PB[3]]
            ps_c = [PB[4], PB[5]]
            sl_q = [Slot(), Slot()]
            sl_n = Slot()
            sl_rn = [Slot(), Slot()]
            sl_c = [Slot(), Slot()]
            n_it = H * NS
            meta = {}
            st2 = {}

            def p2_load(it):
                h, sub = divmod(it, NS)
                if sub == 0:
                    wq_slab, wq_wait, wq_rec = ring.load(wq_d[h], NC_D * 128)
                    mbuf, mem_wait = load_mem_slab(h)
                    meta[h] = (wq_slab, wq_wait, wq_rec, mbuf, mem_wait)

            def p2_q(it):
                h, sub = divmod(it, NS)
                wq_slab, wq_wait, wq_rec, mbuf, mem_wait = meta[h]
                qi = it % 2
                f0 = sub * NF
                if sub == 0:
                    p.wait("tensor", wq_wait[0], wq_wait[1])
                sl_q[qi].gate_writer(p, "tensor")
                t_q = None
                for kc in range(NC_D):
                    t_q = p.op("tensor", lambda e, k=kc, q=qi, f0=f0, wq_slab=wq_slab: e.matmul(
                        ps_q[q], wq_slab[:, k * 128:(k + 1) * 128],
                        ln_pl[:, k * T_O + f0:k * T_O + f0 + NF],
                        start=(k == 0), stop=(k == NC_D - 1)), inc="t")
                if sub == NS - 1:
                    ring.consumed(wq_rec)
                st2[it] = {"t_q": t_q}

            def p2_sigma(it):
                qi = it % 2
                si = it % 3
                p.wait("scalar", "t", st2[it]["t_q"])
                sl_expb[qi].gate_writer(p, "scalar")
                s_e = p.op("scalar", lambda e, q=qi: e.activation(
                    expb[:, q * NF:(q + 1) * NF], ps_q[q], AF.Exp), inc="s")
                p.wait("vector", "s", s_e)
                v_em = p.op("vector", lambda e, q=qi: e.tensor_scalar(
                    expb[:, q * NF:(q + 1) * NF], expb[:, q * NF:(q + 1) * NF],
                    1.0, None, ALU.min), inc="v")
                p.wait("vector", "v", v_em)
                sl_sig[si].gate_writer(p, "vector")
                v_sig = p.op("vector", lambda e, q=qi, si=si: e.scalar_tensor_tensor(
                    sig_bf[:, si * NF:(si + 1) * NF], ps_q[q], 0.0,
                    expb[:, q * NF:(q + 1) * NF], ALU.max, ALU.add), inc="v")
                sl_q[qi].read("s", s_e)
                sl_q[qi].read("v", v_sig)
                sl_expb[qi].read("v", v_sig)
                st2[it]["v_sig"] = v_sig

            def p2_norms(it):
                h, sub = divmod(it, NS)
                si = it % 3
                sl_n.gate_writer(p, "tensor")
                p.wait("tensor", "v", st2[it]["v_sig"])
                t_n = p.op("tensor", lambda e, h=h, si=si: e.matmul(
                    ps_n, mnt[:, h * NB:(h + 1) * NB],
                    sig_bf[:, si * NF:(si + 1) * NF], start=True, stop=True), inc="t")
                sl_sig[si].read("t", t_n)
                p.wait("vector", "t", t_n)
                sl_nrm.gate_writer(p, "vector")
                v_cl = p.op("vector", lambda e: e.tensor_scalar(
                    nrm[:, 0:NF], ps_n, EPS_MEM, None, ALU.max), inc="v")
                sl_n.read("v", v_cl)
                p.wait("scalar", "v", v_cl)
                p.op("scalar", lambda e: e.activation(
                    nrm[:, NF:2 * NF], nrm[:, 0:NF], AF.Ln))
                ni = it % 2
                sl_nrmbf[ni].gate_writer(p, "scalar")
                s_rn = p.op("scalar", lambda e, h=h, ni=ni: e.activation(
                    nrm_bf[:, ni * NF:(ni + 1) * NF], nrm[:, NF:2 * NF], AF.Exp,
                    scale=-1.0, bias=lnW[:, h:h + 1]), inc="s")
                sl_nrm.read("s", s_rn)
                st2[it]["s_rn"] = s_rn

            def p2_banks(it):
                h, sub = divmod(it, NS)
                qi = it % 2
                f0 = sub * NF
                mbuf, mem_wait = meta[h][3], meta[h][4]
                ci = it % 2
                sl_c[ci].gate_writer(p, "tensor")
                if sub == 0:
                    p.wait("tensor", mem_wait[0], mem_wait[1])
                t_c = None
                t_rn_last = None
                ni = it % 2
                for n in range(NB):
                    ri = n % 2
                    sl_rn[ri].gate_writer(p, "tensor")
                    if n == 0:
                        p.wait("tensor", "s", st2[it]["s_rn"])
                    t_b = p.op("tensor", lambda e, n=n, r=ri, ni=ni: e.matmul(
                        ps_rn[r], cst["bsel4"][:, n * 128:(n + 1) * 128],
                        nrm_bf[:, ni * NF:(ni + 1) * NF], start=True, stop=True), inc="t")
                    t_rn_last = t_b
                    p.wait("vector", "t", t_b)
                    sl_sigsc[ri].gate_writer(p, "vector")
                    v_ss = p.op("vector", lambda e, si=it % 3, r=ri: e.tensor_tensor(
                        sig_sc[:, r * NF:(r + 1) * NF],
                        sig_bf[:, si * NF:(si + 1) * NF],
                        ps_rn[r], ALU.mult), inc="v")
                    sl_rn[ri].read("v", v_ss)
                    sl_sig[it % 3].read("v", v_ss)
                    p.wait("tensor", "v", v_ss)
                    t_c = p.op("tensor", lambda e, n=n, c=ci, b=mbuf, r=ri: e.matmul(
                        ps_c[c],
                        mem_bf[:, b * NB * 128 + n * 128:b * NB * 128 + (n + 1) * 128],
                        sig_sc[:, r * NF:(r + 1) * NF],
                        start=(n == 0), stop=(n == NB - 1)), inc="t")
                    sl_sigsc[ri].read("t", t_c)
                sl_nrmbf[ni].read("t", t_rn_last)
                sl_mem[mbuf].read("t", t_c)
                p.wait("scalar", "t", t_c)
                s_cv = p.op("scalar", lambda e, h=h, c=ci, f0=f0: e.copy(
                    cs_pl[:, h * T_O + f0:h * T_O + f0 + NF], ps_c[c]), inc="s")
                sl_c[ci].read("s", s_cv)

            for slot in range(n_it + 3):
                if 0 <= slot - 3 < n_it:
                    p2_banks(slot - 3)
                if 0 <= slot - 2 < n_it:
                    p2_norms(slot - 2)
                if 0 <= slot - 1 < n_it:
                    p2_sigma(slot - 1)
                if slot < n_it:
                    p2_load(slot)
                    p2_q(slot)
        p.barrier()
        dbg_dump("comb", cs_pl[:, :], ot)

        if True:
            ps_w = [PB[0], PB[1]]
            ps_asum = [PB[4 + i][0:1, :] for i in range(NS)]
            ps_asq = [PB[6 + i][0:1, :] for i in range(NS)]
            sl_w = [Slot(), Slot()]
            n_it = NC_D * NS
            wmeta = {}
            st3 = {}

            def p3_wo(it):
                m, sub = divmod(it, NS)
                if sub == 0:
                    wmeta[m] = ring.load(wo_d[m], NC_D * 128)
                wo_slab, wo_wait, wo_rec = wmeta[m]
                wi = it % 2
                f0 = sub * NF
                if sub == 0:
                    p.wait("tensor", wo_wait[0], wo_wait[1])
                sl_w[wi].gate_writer(p, "tensor")
                t_w = None
                for kc in range(NC_D):
                    t_w = p.op("tensor", lambda e, k=kc, w=wi, f0=f0, wo_slab=wo_slab: e.matmul(
                        ps_w[w], wo_slab[:, k * 128:(k + 1) * 128],
                        cs_pl[:, k * T_O + f0:k * T_O + f0 + NF],
                        start=(k == 0), stop=(k == NC_D - 1)), inc="t")
                if sub == NS - 1:
                    ring.consumed(wo_rec)
                st3[it] = t_w

            def p3_evst(it):
                m, sub = divmod(it, NS)
                wi = it % 2
                f0 = sub * NF
                b = m % 2
                p.wait("scalar", "t", st3[it])
                p.op("scalar", lambda e, m=m, w=wi, f0=f0: e.copy(
                    big_pl[:, m * T_O + f0:m * T_O + f0 + NF], ps_w[w]))
                if sub == 0:
                    sl_xsq[b].gate_writer(p, "scalar")
                s_sq = p.op("scalar", lambda e, w=wi, b=b, f0=f0: e.activation(
                    xsq[:, b * T_O + f0:b * T_O + f0 + NF], ps_w[w],
                    AF.Square), inc="s")
                sl_w[wi].read("s", s_sq)
                p.wait("tensor", "s", s_sq)
                p.op("tensor", lambda e, m=m, ss=sub, f0=f0: e.matmul(
                    ps_asum[ss], cst["ones128_bf"][:, :],
                    big_pl[:, m * T_O + f0:m * T_O + f0 + NF],
                    start=(m == 0), stop=(m == NC_D - 1)))
                tl = p.op("tensor", lambda e, m=m, ss=sub, b=b, f0=f0: e.matmul(
                    ps_asq[ss], cst["ones128_bf"][:, :],
                    xsq[:, b * T_O + f0:b * T_O + f0 + NF],
                    start=(m == 0), stop=(m == NC_D - 1)), inc="t")
                sl_xsq[b].read("t", tl)

            for slot in range(n_it + 1):
                if slot < n_it:
                    p3_wo(slot)
                if 0 <= slot - 1 < n_it:
                    p3_evst(slot - 1)
            v_rows = ln_rows_from(ps_asum, ps_asq)
        if True:
            ps_P = [PB[2 + i][0:E, :] for i in range(NS)]
            ps_rb = [PB[4 + i] for i in range(NS)]
            ps_t1 = [PB[6 + i] for i in range(NS)]
            s_bc = ln_bcast(ps_rb, ps_t1, v_rows)
            p.wait("vector", "s", s_bc)
            p.wait("tensor", "s", s_bc)
            ps_sum = [PB[4 + i][0:1, :] for i in range(NS)]
            ps_sq = [PB[6 + i][0:1, :] for i in range(NS)]
            x1_v = None
            t_P = None
            for kc in range(NC_D):
                c0 = kc * T_O
                s_x1 = None
                for sub in range(NS):
                    f0 = sub * NF
                    b = (kc * NS + sub) % 2
                    sl_tfp[b].gate_writer(p, "vector")
                    p.op("vector", lambda e, c=c0, b=b, f0=f0, ss=sub: e.tensor_tensor(
                        tfp[:, b * NF:(b + 1) * NF],
                        big_pl[:, c + f0:c + f0 + NF],
                        sig_bf[:, ss * NF:(ss + 1) * NF], ALU.mult))
                    p.op("vector", lambda e, b=b, ss=sub: e.tensor_tensor(
                        tfp[:, b * NF:(b + 1) * NF], tfp[:, b * NF:(b + 1) * NF],
                        sig_sc[:, ss * NF:(ss + 1) * NF], ALU.subtract))
                    sl_tfp2[b].gate_writer(p, "vector")
                    v_x1f = p.op("vector", lambda e, c=c0, b=b, f0=f0: e.tensor_tensor(
                        tfp2[:, b * NF:(b + 1) * NF], x_pl[:, c + f0:c + f0 + NF],
                        tfp[:, b * NF:(b + 1) * NF], ALU.add), inc="v")
                    sl_tfp[b].read("v", v_x1f)
                    for ss2 in range(NS):
                        sl_sig[ss2].read("v", v_x1f)
                        sl_sigsc[ss2].read("v", v_x1f)
                    p.wait("scalar", "v", v_x1f)
                    s_x1 = p.op("scalar", lambda e, c=c0, b=b, f0=f0: e.copy(
                        x_pl[:, c + f0:c + f0 + NF], tfp2[:, b * NF:(b + 1) * NF]), inc="s")
                    p.wait("tensor", "v", v_x1f)
                    t_P = p.op("tensor", lambda e, k=kc, b=b, ss=sub: e.matmul(
                        ps_P[ss][:, :], rwt[:, k * E:(k + 1) * E],
                        tfp2[:, b * NF:(b + 1) * NF],
                        start=(k == 0), stop=(k == NC_D - 1)), inc="t")
                    sl_tfp2[b].read("t", t_P)
                    sl_tfp2[b].read("s", s_x1)
                p.wait("scalar", "s", s_x1)
                stats_chunk(x_pl[:, c0:c0 + T_O], kc, ps_sum, ps_sq, kc % 2)
        dbg_dump("attn", big_pl[:, :], ot)
        dbg_dump("x1", x_pl[:, :], ot)

        if True:
            v_rows = ln_rows_from(ps_sum, ps_sq)
        if True:
            ps_rb = [PB[4 + i] for i in range(NS)]
            ps_t1 = [PB[6 + i] for i in range(NS)]
            s_bc = ln_bcast(ps_rb, ps_t1, v_rows)
            p.wait("vector", "s", s_bc)
            ln2_v = None
            for kc in range(NC_D):
                c0 = kc * T_O
                for sub in range(NS):
                    f0 = sub * NF
                    b = (kc * NS + sub) % 2
                    sl_xsq[b].gate_writer(p, "vector")
                    p.op("vector", lambda e, c=c0, b=b, f0=f0, ss=sub: e.tensor_tensor(
                        xsq[:, b * T_O:b * T_O + NF], x_pl[:, c + f0:c + f0 + NF],
                        sig_bf[:, ss * NF:(ss + 1) * NF], ALU.mult))
                    ln2_v = p.op("vector", lambda e, c=c0, b=b, f0=f0, ss=sub: e.tensor_tensor(
                        ln_pl[:, c + f0:c + f0 + NF], xsq[:, b * T_O:b * T_O + NF],
                        sig_sc[:, ss * NF:(ss + 1) * NF], ALU.subtract), inc="v")
                    sl_xsq[b].read("v", ln2_v)
                    for ss2 in range(NS):
                        sl_sig[ss2].read("v", ln2_v)
                        sl_sigsc[ss2].read("v", ln2_v)
            p.wait("tensor", "s", s_bc)
        if True:
            ps_r16 = [PB[4 + i][0:E, :] for i in range(NS)]
            ps_m16 = [PB[6 + i][0:E, :] for i in range(NS)]
            t_b16 = None
            for sub in range(NS):
                f0 = sub * NF
                p.op("tensor", lambda e, ss=sub, f0=f0: e.matmul(
                    ps_r16[ss][:, :], cst["onesr16_f"][0:1, :],
                    rows[0:1, 2 * T_O + f0:2 * T_O + f0 + NF], start=True, stop=True))
                t_b16 = p.op("tensor", lambda e, ss=sub, f0=f0: e.matmul(
                    ps_m16[ss][:, :], cst["onesr16_f"][0:1, :],
                    rows[0:1, 3 * T_O + f0:3 * T_O + f0 + NF], start=True, stop=True), inc="t")
            sl_rows.read("t", t_b16)
            p.wait("vector", "t", t_b16)
            sl_tfp[0].gate_writer(p, "vector")
            sl_tfp[1].gate_writer(p, "vector")
            v_lg = None
            for sub in range(NS):
                f0 = sub * NF
                v_m16 = p.op("vector", lambda e, ss=sub: e.tensor_scalar(
                    ps_m16[ss][:, :], ps_m16[ss][:, :], rwsum[:, 0:1], None, ALU.mult), inc="v")
                p.wait("vector", "v", v_m16)
                p.op("vector", lambda e, ss=sub: e.tensor_copy(
                    tfp[0:E, ss * NF:(ss + 1) * NF], ps_r16[ss][:, :]))
                p.op("vector", lambda e, ss=sub, f0=f0: e.tensor_tensor(
                    lg[:, f0:f0 + NF], ps_P[ss][:, :], tfp[0:E, ss * NF:(ss + 1) * NF], ALU.mult))
                v_lg = p.op("vector", lambda e, ss=sub, f0=f0: e.tensor_tensor(
                    lg[:, f0:f0 + NF], lg[:, f0:f0 + NF], ps_m16[ss][:, :], ALU.subtract), inc="v")
            sl_tfp[0].read("v", v_lg)
            sl_tfp[1].read("v", v_lg)
        p.wait("scalar", "v", v_lg)
        s_sc = p.op("scalar", lambda e: e.activation(
            lg[:, :], lg[:, :], AF.Sigmoid), inc="s")
        st_isl = {}

        def emit_island():
                ps_tt = [PB[4][:, 0:E], PB[5][:, 0:E]]
                ps_cb = [PB[6][0:E, 0:128], PB[7][0:E, 0:128]]
                p.wait("tensor", "s", s_sc)
                v_prev = None
                tt_done = None
                for c in range(NCH):
                    if c >= 2:
                        p.wait("tensor", "v", tt_done)
                    t_t = p.op("tensor", lambda e, c=c: e.transpose(
                        ps_tt[c % 2][:, :],
                        lg[:, c * 128:(c + 1) * 128], cst["ident_f"][0:E, 0:E]), inc="t")
                    p.wait("vector", "t", t_t)
                    v_t = p.op("vector", lambda e, c=c: e.tensor_copy(
                        sc_tm[:, c * E:(c + 1) * E],
                        ps_tt[c % 2][:, :]), inc="v")
                    if c >= 1:
                        tt_done = v_prev
                    v_prev = v_t

                def VO(fn):
                    p.op("vector", fn)

                for c in range(NCH):
                    VO(lambda e, c=c: e.tensor_tensor(
                        IA(0, c), sc_tm[:, c * E:(c + 1) * E], ebias_t[:, :], ALU.add))
                for c in range(NCH):
                    VO(lambda e, c=c: e.tensor_tensor(
                        IA(1, c)[:, 0:NB], IA(0, c)[:, 0:E:4], IA(0, c)[:, 1:E:4], ALU.max))
                for c in range(NCH):
                    VO(lambda e, c=c: e.tensor_tensor(
                        IA(1, c)[:, NB:2 * NB], IA(0, c)[:, 0:E:4], IA(0, c)[:, 1:E:4], ALU.min))
                for c in range(NCH):
                    VO(lambda e, c=c: e.tensor_tensor(
                        IA(1, c)[:, 2 * NB:3 * NB], IA(0, c)[:, 2:E:4], IA(0, c)[:, 3:E:4], ALU.max))
                for c in range(NCH):
                    VO(lambda e, c=c: e.tensor_tensor(
                        IA(1, c)[:, 3 * NB:4 * NB], IA(0, c)[:, 2:E:4], IA(0, c)[:, 3:E:4], ALU.min))
                for c in range(NCH):
                    VO(lambda e, c=c: e.tensor_tensor(
                        IA(2, c)[:, 0:NB], IA(1, c)[:, 0:NB], IA(1, c)[:, 2 * NB:3 * NB], ALU.max))
                for c in range(NCH):
                    VO(lambda e, c=c: e.tensor_tensor(
                        IA(2, c)[:, NB:2 * NB], IA(1, c)[:, 0:NB], IA(1, c)[:, 2 * NB:3 * NB], ALU.min))
                for c in range(NCH):
                    VO(lambda e, c=c: e.tensor_tensor(
                        IA(2, c)[:, 2 * NB:3 * NB], IA(1, c)[:, NB:2 * NB], IA(1, c)[:, 3 * NB:4 * NB], ALU.max))
                for c in range(NCH):
                    VO(lambda e, c=c: e.tensor_tensor(
                        IA(2, c)[:, NB:2 * NB], IA(2, c)[:, NB:2 * NB], IA(2, c)[:, 2 * NB:3 * NB], ALU.max))
                for c in range(NCH):
                    VO(lambda e, c=c: e.tensor_tensor(
                        IA(2, c)[:, 0:NB], IA(2, c)[:, 0:NB], IA(2, c)[:, NB:2 * NB], ALU.add))
                for c in range(NCH):
                    VO(lambda e, c=c: e.tensor_tensor(
                        IA(2, c)[:, 0:NB], IA(2, c)[:, 0:NB], cst["gdelta"][:, :], ALU.subtract))
                rv = None
                for c in range(NCH):
                    rv = p.op("vector", lambda e, c=c: e.reduce_max(
                        red1[:, c:c + 1], IA(2, c)[:, 0:NB], axis=AX.X), inc="v")
                p.wait("vector", "v", rv)
                rv = None
                for c in range(NCH):
                    rv = p.op("vector", lambda e, c=c: e.tensor_scalar(
                        IA(3, c)[:, 0:NB], IA(2, c)[:, 0:NB], red1[:, c:c + 1], None, ALU.is_ge), inc="v")
                p.wait("vector", "v", rv)
                rv = None
                for c in range(NCH):
                    rv = p.op("vector", lambda e, c=c: e.scalar_tensor_tensor(
                        IA(3, c)[:, NB:2 * NB], IA(3, c)[:, 0:NB], -BIG, IA(2, c)[:, 0:NB],
                        ALU.mult, ALU.add), inc="v")
                p.wait("vector", "v", rv)
                rv = None
                for c in range(NCH):
                    rv = p.op("vector", lambda e, c=c: e.reduce_max(
                        red1[:, 1 * NCH + c:1 * NCH + c + 1], IA(3, c)[:, NB:2 * NB], axis=AX.X), inc="v")
                p.wait("vector", "v", rv)
                rv = None
                for c in range(NCH):
                    rv = p.op("vector", lambda e, c=c: e.tensor_scalar(
                        IA(1, c)[:, 0:NB], IA(2, c)[:, 0:NB], red1[:, 1 * NCH + c:1 * NCH + c + 1], None, ALU.is_ge), inc="v")
                p.wait("vector", "v", rv)
                for c in range(NCH):
                    for j in range(4):
                        VO(lambda e, c=c, j=j: e.tensor_copy(IA(2, c)[:, j:E:4], IA(1, c)[:, 0:NB]))
                for c in range(NCH):
                    VO(lambda e, c=c: e.tensor_tensor(IA(4, c), IA(0, c), IA(2, c), ALU.mult))
                for c in range(NCH):
                    VO(lambda e, c=c: e.tensor_tensor(IA(4, c), IA(4, c), cst["edelta"][:, :], ALU.subtract))
                rv = None
                for c in range(NCH):
                    rv = p.op("vector", lambda e, c=c: e.reduce_max(
                        red1[:, 2 * NCH + c:2 * NCH + c + 1], IA(4, c), axis=AX.X), inc="v")
                p.wait("vector", "v", rv)
                rv = None
                for c in range(NCH):
                    rv = p.op("vector", lambda e, c=c: e.tensor_scalar(
                        IA(5, c), IA(4, c), red1[:, 2 * NCH + c:2 * NCH + c + 1], None, ALU.is_ge), inc="v")
                p.wait("vector", "v", rv)
                rv = None
                for c in range(NCH):
                    rv = p.op("vector", lambda e, c=c: e.scalar_tensor_tensor(
                        IA(5, c), IA(5, c), -BIG, IA(4, c), ALU.mult, ALU.add), inc="v")
                p.wait("vector", "v", rv)
                rv = None
                for c in range(NCH):
                    rv = p.op("vector", lambda e, c=c: e.reduce_max(
                        red1[:, 3 * NCH + c:3 * NCH + c + 1], IA(5, c), axis=AX.X), inc="v")
                p.wait("vector", "v", rv)
                rv = None
                for c in range(NCH):
                    rv = p.op("vector", lambda e, c=c: e.tensor_scalar(
                        IA(5, c), IA(4, c), red1[:, 3 * NCH + c:3 * NCH + c + 1], None, ALU.is_ge), inc="v")
                p.wait("vector", "v", rv)
                for c in range(NCH):
                    VO(lambda e, c=c: e.tensor_tensor(
                        IA(4, c), sc_tm[:, c * E:(c + 1) * E], IA(5, c), ALU.mult))
                rv = None
                for c in range(NCH):
                    rv = p.op("vector", lambda e, c=c: e.reduce_sum(
                        red1[:, 4 * NCH + c:4 * NCH + c + 1], IA(4, c), axis=AX.X), inc="v")
                p.wait("vector", "v", rv)
                rv = None
                for c in range(NCH):
                    rv = p.op("vector", lambda e, c=c: e.tensor_scalar(
                        red1[:, 4 * NCH + c:4 * NCH + c + 1], red1[:, 4 * NCH + c:4 * NCH + c + 1], 1e-20, None, ALU.add), inc="v")
                p.wait("vector", "v", rv)
                rv = None
                for c in range(NCH):
                    rv = p.op("vector", lambda e, c=c: e.reciprocal(
                        red1[:, 5 * NCH + c:5 * NCH + c + 1], red1[:, 4 * NCH + c:4 * NCH + c + 1]), inc="v")
                p.wait("vector", "v", rv)
                v_cmb = None
                for c in range(NCH):
                    v_cmb = p.op("vector", lambda e, c=c: e.tensor_scalar(
                        IA(4, c), IA(4, c), red1[:, 5 * NCH + c:5 * NCH + c + 1], None, ALU.mult), inc="v")
                st_isl["v_cmb"] = v_cmb

        def emit_island_tail():
            ps_cb = [PB[6][0:E, 0:128], PB[7][0:E, 0:128]]
            p.wait("tensor", "v", st_isl["v_cmb"])
            v_prev = None
            cb_done = None
            for c in range(NCH):
                if c >= 2:
                    p.wait("tensor", "v", cb_done)
                t_t = p.op("tensor", lambda e, c=c: e.transpose(
                    ps_cb[c % 2][:, :],
                    IA(4, c), cst["ident_f"][:, :]), inc="t")
                p.wait("vector", "t", t_t)
                v_cp = p.op("vector", lambda e, c=c: e.tensor_copy(
                    comb_bfs[:, c * 128:(c + 1) * 128],
                    ps_cb[c % 2][:, :]), inc="v")
                if c >= 1:
                    cb_done = v_prev
                v_prev = v_cp

        if True:
            ps_u = [PB[0], PB[1]]
            ps_dn = [PB[2], PB[3]]
            sl_u = [Slot(), Slot()]
            sl_d = [Slot(), Slot()]
            sl_cs = [Slot() for _ in range(16)]
            routed_dep = {}
            for half in range(2):
                it = 0
                for icl in range(16):
                    ic = half * 16 + icl
                    shu_slab, shu_wait, shu_rec = ring.load(shu_d[ic], NC_D * 128)
                    for sub in range(NS):
                        ui = it % 2
                        f0 = sub * NF
                        if sub == 0:
                            p.wait("tensor", shu_wait[0], shu_wait[1])
                        if half == 0 and it == 0:
                            p.wait("tensor", "v", ln2_v)
                        sl_u[ui].gate_writer(p, "tensor")
                        t_u = None
                        for kc in range(NC_D):
                            t_u = p.op("tensor", lambda e, k=kc, u=ui, f0=f0, shu_slab=shu_slab: e.matmul(
                                ps_u[u], shu_slab[:, k * 128:(k + 1) * 128],
                                ln_pl[:, k * T_O + f0:k * T_O + f0 + NF],
                                start=(k == 0), stop=(k == NC_D - 1)), inc="t")
                        if sub == NS - 1:
                            ring.consumed(shu_rec)
                        p.wait("scalar", "t", t_u)
                        b5 = it % 2
                        sl_tfp[b5].gate_writer(p, "scalar")
                        s_rl = p.op("scalar", lambda e, u=ui, b=b5: e.activation(
                            tfp[:, b * NF:(b + 1) * NF], ps_u[u], AF.Relu), inc="s")
                        sl_u[ui].read("s", s_rl)
                        p.wait("vector", "s", s_rl)
                        sl_cs[icl].gate_writer(p, "vector")
                        v_r2 = p.op("vector", lambda e, b=b5, c=icl, f0=f0: e.tensor_tensor(
                            cs_pl[:, c * T_O + f0:c * T_O + f0 + NF],
                            tfp[:, b * NF:(b + 1) * NF], tfp[:, b * NF:(b + 1) * NF],
                            ALU.mult), inc="v")
                        sl_tfp[b5].read("v", v_r2)
                        it += 1
                act_v = p.val("v")
                if half == 1:
                    emit_island_tail()
                it_d = 0
                for m in range(NC_D):
                    shd_src = shd_d[m][:, half * 16 * 128:(half + 1) * 16 * 128]
                    shd_slab, shd_wait, shd_rec = ring.load(shd_src, 16 * 128)
                    for sub in range(NS):
                        di = it_d % 2
                        f0 = sub * NF
                        if sub == 0:
                            p.wait("tensor", shd_wait[0], shd_wait[1])
                        if it_d == 0:
                            p.wait("tensor", "v", act_v)
                            if half == 0:
                                p.wait("tensor", "v", v_lg)
                        sl_d[di].gate_writer(p, "tensor")
                        t_d = None
                        for icl in range(16):
                            t_d = p.op("tensor", lambda e, c=icl, d=di, f0=f0, shd_slab=shd_slab: e.matmul(
                                ps_dn[d], shd_slab[:, c * 128:(c + 1) * 128],
                                cs_pl[:, c * T_O + f0:c * T_O + f0 + NF],
                                start=(c == 0), stop=(c == 15)), inc="t")
                        for icl in range(16):
                            sl_cs[icl].read("t", t_d)
                        if sub == NS - 1:
                            ring.consumed(shd_rec)
                        dst = big_pl[:, m * T_O + f0:m * T_O + f0 + NF]
                        if half == 0:
                            p.wait("scalar", "t", t_d)
                            cnt = p.op("scalar", lambda e, d=di, dst=dst: e.copy(
                                dst, ps_dn[d]), inc="s")
                            sl_d[di].read("s", cnt)
                            routed_dep[(m, sub)] = ("s", cnt)
                        else:
                            is_s, prev = routed_dep[(m, sub)]
                            p.wait("vector", is_s, prev)
                            p.wait("vector", "t", t_d)
                            cnt = p.op("vector", lambda e, d=di, dst=dst: e.tensor_tensor(
                                dst, ps_dn[d], dst, ALU.add), inc="v")
                            sl_d[di].read("v", cnt)
                            routed_dep[(m, sub)] = ("v", cnt)
                        it_d += 1
                if half == 0:
                    emit_island()
        p.barrier()
        dbg_dump("ln2", ln_pl[:, :], ot)
        dbg_dump("x4", x_pl[:, :], ot)
        if DBG and ot == 0:
            dbg_dump_small(dbg_lg, lg[:, :])
            dbg_dump_small(dbg_cw, comb_bfs[:, :])
        dbg_dump("x5", x_pl[:, :], ot)
        if True:
            ps_sum7 = [PB[4 + i][0:1, :] for i in range(NS)]
            ps_sq7 = [PB[6 + i][0:1, :] for i in range(NS)]
            ps_h = [PB[0], PB[1]]
            ps_cbc = [PB[4 + i] for i in range(NS)]
            ps_dd = [PB[2], PB[3]]
            sl_h = [Slot(), Slot()]
            sl_cbc = Slot()
            sl_dd = [Slot(), Slot()]
            sl_a = [Slot() for _ in range(16)]
            for grp in range(E // 2):
                es = (2 * grp, 2 * grp + 1)
                it_h = 0
                for ei, e_ in enumerate(es):
                    sl_cbc.gate_writer(p, "tensor")
                    t_cb = None
                    for sub in range(NS):
                        f0 = sub * NF
                        t_cb = p.op("tensor", lambda e, e_=e_, ss=sub, f0=f0: e.matmul(
                            ps_cbc[ss][:, :], cst["bsel16"][:, e_ * 128:(e_ + 1) * 128],
                            comb_bfs[:, f0:f0 + NF], start=True, stop=True), inc="t")
                    for ic in range(NC_EI):
                        wup_slab, wup_wait, wup_rec = ring.load(wup_d[e_, ic], NC_D * 128)
                        ch = ei * NC_EI + ic
                        for sub in range(NS):
                            hi = it_h % 2
                            f0 = sub * NF
                            if sub == 0:
                                p.wait("tensor", wup_wait[0], wup_wait[1])
                            sl_h[hi].gate_writer(p, "tensor")
                            t_h = None
                            for kc in range(NC_D):
                                t_h = p.op("tensor", lambda e, k=kc, h=hi, f0=f0, wup_slab=wup_slab: e.matmul(
                                    ps_h[h], wup_slab[:, k * 128:(k + 1) * 128],
                                    ln_pl[:, k * T_O + f0:k * T_O + f0 + NF],
                                    start=(k == 0), stop=(k == NC_D - 1)), inc="t")
                            if sub == NS - 1:
                                ring.consumed(wup_rec)
                            p.wait("scalar", "t", t_h)
                            b = it_h % 2
                            sl_tfp[b].gate_writer(p, "scalar")
                            s_rl = p.op("scalar", lambda e, h=hi, b=b: e.activation(
                                tfp[:, b * NF:(b + 1) * NF], ps_h[h], AF.Relu), inc="s")
                            sl_h[hi].read("s", s_rl)
                            p.wait("vector", "s", s_rl)
                            if it_h == 0:
                                p.wait("vector", "t", t_cb)
                            sl_tfp2[b].gate_writer(p, "vector")
                            p.op("vector", lambda e, b=b, ss=sub: e.tensor_tensor(
                                tfp2[:, b * NF:(b + 1) * NF],
                                tfp[:, b * NF:(b + 1) * NF],
                                ps_cbc[ss][:, :], ALU.mult))
                            sl_a[ch].gate_writer(p, "vector")
                            v_a = p.op("vector", lambda e, ch=ch, b=b, f0=f0: e.tensor_tensor(
                                cs_pl[:, ch * T_O + f0:ch * T_O + f0 + NF],
                                tfp2[:, b * NF:(b + 1) * NF],
                                tfp[:, b * NF:(b + 1) * NF], ALU.mult), inc="v")
                            sl_tfp[b].read("v", v_a)
                            sl_tfp2[b].read("v", v_a)
                            sl_cbc.read("v", v_a)
                            it_h += 1
                a_done_v = p.val("v")
                it_d = 0
                for m in range(NC_D):
                    slabs = []
                    for ei2, e2 in enumerate(es):
                        sl2, wt2, rec2 = ring.load(wdn_d[e2, m], NC_EI * 128)
                        slabs.append((sl2, wt2, rec2))
                    for sub in range(NS):
                        di = it_d % 2
                        f0 = sub * NF
                        if sub == 0:
                            for sl2, wt2, rec2 in slabs:
                                p.wait("tensor", wt2[0], wt2[1])
                        if it_d == 0:
                            p.wait("tensor", "v", a_done_v)
                        sl_dd[di].gate_writer(p, "tensor")
                        t_d = None
                        for ei2 in range(2):
                            sl2 = slabs[ei2][0]
                            for ic in range(NC_EI):
                                ch = ei2 * NC_EI + ic
                                t_d = p.op("tensor", lambda e, sl=sl2, ic=ic, ch=ch, d=di, f0=f0, ei2=ei2: e.matmul(
                                    ps_dd[d], sl[:, ic * 128:(ic + 1) * 128],
                                    cs_pl[:, ch * T_O + f0:ch * T_O + f0 + NF],
                                    start=(ei2 == 0 and ic == 0), stop=(ei2 == 1 and ic == NC_EI - 1)),
                                    inc="t")
                        for ch in range(16):
                            sl_a[ch].read("t", t_d)
                        if sub == NS - 1:
                            for sl2, wt2, rec2 in slabs:
                                ring.consumed(rec2)
                        is_s, prev = routed_dep[(m, sub)]
                        p.wait("vector", is_s, prev)
                        p.wait("vector", "t", t_d)
                        dst = big_pl[:, m * T_O + f0:m * T_O + f0 + NF]
                        v_add = p.op("vector", lambda e, d=di, dst=dst: e.tensor_tensor(
                            dst, ps_dd[d], dst, ALU.add), inc="v")
                        sl_dd[di].read("v", v_add)
                        routed_dep[(m, sub)] = ("v", v_add)
                        it_d += 1
                        if grp == E // 2 - 1 and sub == NS - 1 and m >= 1:
                            mm1 = m - 1
                            p.wait("scalar", "v", routed_dep[(mm1, NS - 1)][1])
                            stats_chunk(big_pl[:, mm1 * T_O:(mm1 + 1) * T_O], mm1,
                                        ps_sum7, ps_sq7, mm1 % 2)
            mm1 = NC_D - 1
            p.wait("scalar", "v", routed_dep[(mm1, NS - 1)][1])
            stats_chunk(big_pl[:, mm1 * T_O:(mm1 + 1) * T_O], mm1,
                        ps_sum7, ps_sq7, mm1 % 2)
        p.barrier()
        dbg_dump("x6", x_pl[:, :], ot)
        dbg_dump("ffn", big_pl[:, :], ot)

        if True:
            v_rows = ln_rows_from(ps_sum7, ps_sq7)
        if DBG and ot == 0:
            dbg_dump_small(dbg_rows, rows[:, :])
        if True:
            ps_rb = [PB[4 + i] for i in range(NS)]
            ps_t1 = [PB[6 + i] for i in range(NS)]
            s_bc = ln_bcast(ps_rb, ps_t1, v_rows)
            p.wait("vector", "s", s_bc)
            v_out = None
            for kc in range(NC_D):
                c0 = kc * T_O
                for sub in range(NS):
                    f0 = sub * NF
                    b = (kc * NS + sub) % 2
                    sl_xsq[b].gate_writer(p, "vector")
                    p.op("vector", lambda e, c=c0, b=b, f0=f0, ss=sub: e.tensor_tensor(
                        xsq[:, b * T_O:b * T_O + NF],
                        big_pl[:, c + f0:c + f0 + NF],
                        sig_bf[:, ss * NF:(ss + 1) * NF], ALU.mult))
                    p.op("vector", lambda e, b=b, ss=sub: e.tensor_tensor(
                        xsq[:, b * T_O:b * T_O + NF], xsq[:, b * T_O:b * T_O + NF],
                        sig_sc[:, ss * NF:(ss + 1) * NF], ALU.subtract))
                    v_out = p.op("vector", lambda e, c=c0, b=b, f0=f0: e.tensor_tensor(
                        x_pl[:, c + f0:c + f0 + NF], x_pl[:, c + f0:c + f0 + NF],
                        xsq[:, b * T_O:b * T_O + NF], ALU.add), inc="v")
                    sl_xsq[b].read("v", v_out)
                    for ss2 in range(NS):
                        sl_sig[ss2].read("v", v_out)
                        sl_sigsc[ss2].read("v", v_out)
                sid = kc % 4
                p.wait("gpsimd", "v", v_out)
                if xs_uses[sid] > 0:
                    p.wait("gpsimd", f"xs{sid}", 16 * xs_uses[sid])
                xs_uses[sid] += 1
                g_dma(out_d[ot][:, c0:c0 + T_O], x_pl[:, c0:c0 + T_O], f"xs{sid}")
                if ot + 1 < n_ot:
                    p.wait("gpsimd", f"xs{sid}", 16 * xs_uses[sid])
                    if xl_uses[sid] > 0:
                        p.wait("gpsimd", f"xl{sid}", 16 * xl_uses[sid])
                    xl_uses[sid] += 1
                    xl_at[(ot + 1, kc)] = 16 * xl_uses[sid]
                    g_dma(x_pl[:, c0:c0 + T_O], x_in[ot + 1][:, c0:c0 + T_O], f"xl{sid}")
            p.wait("tensor", "v", v_out)
        p.barrier()

    for i in range(4):
        p.wait("sync", f"xs{i}", 16 * xs_uses[i])

    with nc.Block() as block:
        @block.sync
        def _(eng):
            for f in p.ops["sync"]:
                f(eng)

        @block.gpsimd
        def _(eng):
            for f in p.ops["gpsimd"]:
                f(eng)

        @block.tensor
        def _(eng):
            for f in p.ops["tensor"]:
                f(eng)

        @block.vector
        def _(eng):
            for f in p.ops["vector"]:
                f(eng)

        @block.scalar
        def _(eng):
            for f in p.ops["scalar"]:
                f(eng)

    return nc



def _in_maps(per_core, shared):
    maps = []
    for xs in per_core:
        m = {"x": xs}
        m.update(shared)
        maps.append(m)
    return maps


def _reassemble(results, n_ot, T_O):
    outs = []
    for r in results:
        a = np.asarray(r["out"]).reshape(n_ot, 128, NC_D, T_O)
        a = a.transpose(2, 1, 0, 3).reshape(D, n_ot * T_O).T
        outs.append(a)
    full = np.concatenate(outs, axis=0)
    return np.ascontiguousarray(full.reshape(2, 8192, D).astype(np.float32))


def kernel(**inputs):
    per_core, shared, n_ot = _prep(inputs)
    nc = build(n_ot, CFG["T_O"])
    maps = _in_maps(per_core, shared)
    res = run_bass_kernel_spmd(nc, maps, core_ids=list(range(len(maps))))
    return _reassemble(res.results, n_ot, CFG["T_O"])


# version 13
# speedup vs baseline: 1.0386x; 1.0019x over previous
import contextlib
import numpy as np
import ml_dtypes
import concourse.bass as bass
import concourse.mybir as mybir
from concourse.bass_utils import run_bass_kernel_spmd

F32 = mybir.dt.float32
BF16 = mybir.dt.bfloat16
AF = mybir.ActivationFunctionType
ALU = mybir.AluOpType
AX = mybir.AxisListType

D = 2048
NC_D = 16
H = 16
NB = 4
E = 16
EI = 1024
NC_EI = 8
SI = 4096
NC_SI = 32

EPS_LN = 1e-5
EPS_MEM = 1e-6
BIG = 1.0e30
DELTA = 1e-7

CFG = {"T_O": 1024, "T_CORE": 2048, "CHAIN": False, "DBG": False}



def _tile_lhsT(W):
    K, M = W.shape
    nk, nm = K // 128, M // 128
    return np.ascontiguousarray(
        W.reshape(nk, 128, nm, 128).transpose(2, 1, 0, 3).reshape(nm, 128, nk * 128))


def _prep(inputs):
    T_O = CFG["T_O"]
    Tc = CFG["T_CORE"]
    n_ot = Tc // T_O
    x = np.asarray(inputs["L0_attn_hidden_states"], dtype=np.float32)
    xt = np.ascontiguousarray(x.reshape(-1, D))
    per_core = []
    n_cores = xt.shape[0] // Tc
    for c in range(n_cores):
        xs = xt[c * Tc:(c + 1) * Tc].T
        xs = xs.reshape(NC_D, 128, n_ot, T_O).transpose(2, 1, 0, 3)
        per_core.append(np.ascontiguousarray(xs.reshape(n_ot, 128, NC_D * T_O)))

    Wq = np.asarray(inputs["L0_attn_Wq"], dtype=np.float32)
    Wo = np.asarray(inputs["L0_attn_Wo"], dtype=np.float32)
    shared = {
        "wq": _tile_lhsT(np.ascontiguousarray(Wq.T)),
        "wo": _tile_lhsT(np.ascontiguousarray(Wo.T)),
        "mem": np.ascontiguousarray(
            np.asarray(inputs["L0_attn_memories"], dtype=np.float32)
            .transpose(1, 2, 0, 3).reshape(H, 128, NB * 128)),
        "mnt": np.ascontiguousarray(
            np.asarray(inputs["L0_attn_memory_norms"], dtype=np.float32)
            .transpose(2, 1, 0).reshape(128, H * NB)),
        "mn4": np.ascontiguousarray(
            np.asarray(inputs["L0_attn_memory_norms"], dtype=np.float32)
            .reshape(NB, H * 128)),
        "bankw": np.ascontiguousarray(
            np.asarray(inputs["L0_attn_bank_weights"], dtype=np.float32)),
        "rwt": np.ascontiguousarray(
            np.asarray(inputs["L0_ffn_router_weight"], dtype=np.float32)
            .T.reshape(NC_D, 128, E).transpose(1, 0, 2).reshape(128, NC_D * E)),
        "ebias": np.ascontiguousarray(np.broadcast_to(
            np.asarray(inputs["L0_ffn_e_bias"], dtype=np.float32)[None, :], (128, E))),
        "wup": np.ascontiguousarray(
            np.asarray(inputs["L0_ffn_w_up"], dtype=np.float32)
            .reshape(E, NC_D, 128, NC_EI, 128).transpose(0, 3, 2, 1, 4)
            .reshape(E, NC_EI, 128, NC_D * 128)),
        "wdn": np.ascontiguousarray(
            np.asarray(inputs["L0_ffn_w_down"], dtype=np.float32)
            .reshape(E, NC_EI, 128, NC_D, 128).transpose(0, 3, 2, 1, 4)
            .reshape(E, NC_D, 128, NC_EI * 128)),
        "shu": _tile_lhsT(np.asarray(inputs["L0_ffn_shared_up"], dtype=np.float32)),
        "shd": _tile_lhsT(np.asarray(inputs["L0_ffn_shared_down"], dtype=np.float32)),
    }
    return per_core, shared, n_ot



class Prog:
    ENGINES = ("sync", "gpsimd", "tensor", "vector", "scalar")

    def __init__(self, nc, chain=False):
        self.nc = nc
        self.chain = chain
        self.ops = {e: [] for e in self.ENGINES}
        self.sems = {}
        self.cnt = {}

    def sem(self, name):
        h = self.nc.alloc_semaphore(name)
        self.sems[name] = h
        self.cnt[name] = 0
        return h

    def op(self, eng, build, inc=None, inc_amt=1):
        if self.chain and eng in ("vector", "scalar"):
            own = "v" if eng == "vector" else "s"
            self.wait(eng, own, self.cnt[own])
            if inc is None:
                inc = own
        if inc is None:
            self.ops[eng].append(build)
            return None
        h = self.sems[inc]
        self.cnt[inc] += inc_amt
        val = self.cnt[inc]
        self.ops[eng].append(lambda e, b=build, hh=h, a=inc_amt: b(e).then_inc(hh, a))
        return val

    def wait(self, eng, sem, val):
        if val is None or val <= 0:
            return
        h = self.sems[sem]
        self.ops[eng].append(lambda e, hh=h, v=val: e.wait_ge(hh, v))

    def val(self, sem):
        return self.cnt[sem]

    def barrier(self):
        tv, vv, sv = self.val("t"), self.val("v"), self.val("s")
        self.wait("tensor", "v", vv)
        self.wait("tensor", "s", sv)
        self.wait("vector", "t", tv)
        self.wait("scalar", "t", tv)
        self.wait("vector", "s", sv)
        self.wait("scalar", "v", vv)


class Slot:
    def __init__(self):
        self.rd = {"t": 0, "v": 0, "s": 0}

    def gate_writer(self, p, eng):
        for sem, v in self.rd.items():
            if v > 0:
                p.wait(eng, sem, v)
        self.rd = {"t": 0, "v": 0, "s": 0}

    def read(self, sem, val):
        if val is not None and val > self.rd[sem]:
            self.rd[sem] = val


class Ring:
    def __init__(self, p, buf_ap, n_elems, n_sems=8, tag="wr"):
        self.p = p
        self.buf = buf_ap
        self.cap = n_elems
        self.pos = 0
        self.live = []
        self.sem_names = []
        for i in range(n_sems):
            name = f"{tag}{i}"
            p.sem(name)
            self.sem_names.append(name)
        self.sem_uses = {n: 0 for n in self.sem_names}
        self.next_sem = 0
        self.sid = 0

    def load(self, src_ap, n_free):
        assert n_free <= self.cap
        if self.pos + n_free > self.cap:
            self.pos = 0
        start, end = self.pos, self.pos + n_free
        need_t = 0
        for (s0, e0, done_t, sid0) in self.live:
            if s0 < end and e0 > start:
                assert done_t is not None, f"ring overlap with unconsumed slab {sid0}"
                need_t = max(need_t, done_t)
        self.live = [r for r in self.live if not (r[0] < end and r[1] > start)]
        sem = self.sem_names[self.next_sem]
        self.next_sem = (self.next_sem + 1) % len(self.sem_names)
        if self.sem_uses[sem] > 0:
            self.p.wait("gpsimd", sem, 16 * self.sem_uses[sem])
        if need_t > 0:
            self.p.wait("gpsimd", "t", need_t)
        dst = self.buf[:, start:end]
        self.p.op("gpsimd", lambda e, d=dst, s=src_ap: e.dma_start(out=d, in_=s),
                  inc=sem, inc_amt=16)
        self.sem_uses[sem] += 1
        rec = [start, end, None, self.sid]
        self.live.append(rec)
        self.sid += 1
        self.pos = end
        return dst, (sem, 16 * self.sem_uses[sem]), rec

    def consumed(self, rec):
        rec[2] = self.p.val("t")



def build(n_ot, T_O):
    NF = min(512, T_O)
    NS = T_O // NF
    NCH = T_O // 128
    nc = bass.Bass("TRN2", target_bir_lowering=False)

    x_in = nc.dram_tensor("x", [n_ot, 128, NC_D * T_O], F32, kind="ExternalInput").ap()
    out_d = nc.dram_tensor("out", [n_ot, 128, NC_D * T_O], F32, kind="ExternalOutput").ap()
    wq_d = nc.dram_tensor("wq", [NC_D, 128, NC_D * 128], F32, kind="ExternalInput").ap()
    wo_d = nc.dram_tensor("wo", [NC_D, 128, NC_D * 128], F32, kind="ExternalInput").ap()
    mem_d = nc.dram_tensor("mem", [H, 128, NB * 128], F32, kind="ExternalInput").ap()
    mnt_d = nc.dram_tensor("mnt", [128, H * NB], F32, kind="ExternalInput").ap()
    mn4_d = nc.dram_tensor("mn4", [NB, H * 128], F32, kind="ExternalInput").ap()
    bankw_d = nc.dram_tensor("bankw", [H, NB], F32, kind="ExternalInput").ap()
    rwt_d = nc.dram_tensor("rwt", [128, NC_D * E], F32, kind="ExternalInput").ap()
    ebias_d = nc.dram_tensor("ebias", [128, E], F32, kind="ExternalInput").ap()
    wup_d = nc.dram_tensor("wup", [E, NC_EI, 128, NC_D * 128], F32, kind="ExternalInput").ap()
    wdn_d = nc.dram_tensor("wdn", [E, NC_D, 128, NC_EI * 128], F32, kind="ExternalInput").ap()
    shu_d = nc.dram_tensor("shu", [NC_SI, 128, NC_D * 128], F32, kind="ExternalInput").ap()
    shd_d = nc.dram_tensor("shd", [NC_D, 128, NC_SI * 128], F32, kind="ExternalInput").ap()
    DBG = CFG.get("DBG", False)
    if DBG:
        dbg_d = {k: nc.dram_tensor(f"dbg_{k}", [128, NC_D * T_O], F32, kind="ExternalOutput").ap()
                 for k in ("ln1", "comb", "attn", "x1", "ln2", "ffn", "x4", "x5", "x6")}
        dbg_lg = nc.dram_tensor("dbg_lg", [E, T_O], F32, kind="ExternalOutput").ap()
        dbg_rows = nc.dram_tensor("dbg_rows", [1, 4 * T_O], F32, kind="ExternalOutput").ap()
        dbg_cw = nc.dram_tensor("dbg_cw", [E, T_O], F32, kind="ExternalOutput").ap()
        p_dbg = [0]

    inl = {
        "ones128_bf": np.ones((128, 1), np.float32).astype(ml_dtypes.bfloat16),
        "onesr_bf": np.ones((1, 128), np.float32).astype(ml_dtypes.bfloat16),
        "onesr16_f": np.ones((1, 16), np.float32),
        "ones128_f": np.ones((128, 1), np.float32),
        "ident_f": np.eye(128, dtype=np.float32),
        "gdelta": np.ascontiguousarray(np.broadcast_to(
            (np.arange(NB) * DELTA).astype(np.float32)[None, :], (128, NB))),
        "edelta": np.ascontiguousarray(np.broadcast_to(
            (np.arange(E) * DELTA).astype(np.float32)[None, :], (128, E))),
        "bsel4": np.kron(np.eye(NB, dtype=np.float32),
                         np.ones((1, 128), np.float32)).astype(ml_dtypes.bfloat16),
        "bsel16": np.kron(np.eye(E, dtype=np.float32),
                          np.ones((1, 128), np.float32)).astype(ml_dtypes.bfloat16),
    }
    inl_d = {k: nc.inline_tensor(v, "c_" + k) for k, v in inl.items()}

    p = Prog(nc, chain=CFG.get("CHAIN", False))
    for s in ("t", "v", "s", "ldc", "ldg", "ldm0", "ldm1", "ldm2"):
        p.sem(s)
    for i in range(4):
        p.sem(f"xs{i}")
        p.sem(f"xl{i}")
    xs_uses = [0, 0, 0, 0]
    xl_uses = [0, 0, 0, 0]
    xl_at = {}

    A = nc.alloc_sbuf_tensor
    x_pl = A("x_pl", [128, NC_D * T_O], BF16).ap()
    ln_pl = A("ln_pl", [128, NC_D * T_O], BF16).ap()
    cs_pl = A("cs_pl", [128, NC_D * T_O], BF16).ap()
    big_pl = A("big_pl", [128, NC_D * T_O], BF16).ap()
    xsq = A("xsq", [128, 2 * T_O], BF16).ap()
    sig_bf = A("sig_bf", [128, 3 * NF], BF16).ap()
    sig_sc = A("sig_sc", [128, 2 * NF], BF16).ap()
    expb = A("expb", [128, 2 * NF], F32).ap()
    tfp = A("tfp", [128, 2 * NF], F32).ap()
    tfp2 = A("tfp2", [128, 2 * NF], F32).ap()
    rows = A("rows", [1, 4 * T_O], F32).ap()
    rows_bf = A("rows_bf", [1, 2 * T_O], BF16).ap()
    nrm = A("nrm", [NB, 2 * NF], F32).ap()
    nrm_bf = A("nrm_bf", [NB, 2 * NF], BF16).ap()
    lg = A("lg", [E, T_O], F32).ap()
    sc_tm = A("sc_tm", [128, NCH * E], F32).ap()
    isl = A("isl", [128, 6 * NCH * E], F32).ap()
    red1 = A("red1", [128, 6 * NCH], F32).ap()
    comb_bfs = A("comb_bfs", [E, T_O], BF16).ap()
    mnt = A("mnt_s", [128, H * NB], BF16).ap()
    rwt = A("rwt_s", [128, NC_D * E], F32).ap()
    ebias_t = A("ebias_s", [128, E], F32).ap()
    bankw = A("bankw_s", [H, NB], F32).ap()
    mn4 = A("mn4_s", [NB, H * 128], BF16).ap()
    wTa = A("wTa", [NB, H], F32).ap()
    lnW = A("lnW", [NB, H], F32).ap()
    wsm = A("wsm", [H, NB], F32).ap()
    rwsum = A("rwsum", [E, 1], F32).ap()
    tiny = A("tiny", [E, 8], F32).ap()
    mem_bf = A("mem_bf", [128, 3 * NB * 128], BF16).ap()
    cst = {}
    for k, v in inl.items():
        dt = BF16 if v.dtype == ml_dtypes.bfloat16 else F32
        cst[k] = A("s_" + k, list(v.shape), dt).ap()

    RING_ELEMS = 4096
    ring_buf = A("ring", [128, RING_ELEMS], BF16).ap()
    ring = Ring(p, ring_buf, RING_ELEMS, n_sems=8)

    PB = [nc.alloc_psum_tensor(f"PB{i}", [128, NF], F32).ap() for i in range(8)]

    def IA(slot, c, w=E):
        base = (slot * NCH + c) * E
        return isl[:, base:base + w]

    sl_xsq = [Slot(), Slot()]
    sl_sig = [Slot(), Slot(), Slot()]
    sl_sigsc = [Slot(), Slot()]
    sl_expb = [Slot(), Slot()]
    sl_tfp = [Slot(), Slot()]
    sl_tfp2 = [Slot(), Slot()]
    sl_nrm = Slot()
    sl_nrmbf = [Slot(), Slot()]
    sl_rows = Slot()
    sl_mem = [Slot(), Slot(), Slot()]
    memslab_uses = [0, 0, 0]

    def sync_dma(dst, src, sem):
        return p.op("sync", lambda e, d=dst, s=src: e.dma_start(out=d, in_=s),
                    inc=sem, inc_amt=16)

    def g_dma(dst, src, sem):
        return p.op("gpsimd", lambda e, d=dst, s=src: e.dma_start(out=d, in_=s),
                    inc=sem, inc_amt=16)

    n_const = 0
    for k in cst:
        sync_dma(cst[k][:, :], inl_d[k].ap()[:, :], "ldc")
        n_const += 1
    for dst, src in ((rwt, rwt_d), (ebias_t, ebias_d), (bankw, bankw_d)):
        sync_dma(dst[:, :], src[:, :], "ldc")
        n_const += 1
    ldc_all = 16 * n_const
    g_dma(mnt[:, :], mnt_d[:, :], "ldg")
    g_dma(mn4[:, :], mn4_d[:, :], "ldg")
    ldg_all = 32
    for kc in range(NC_D):
        sid = kc % 4
        if xl_uses[sid] > 0:
            p.wait("gpsimd", f"xl{sid}", 16 * xl_uses[sid])
        xl_uses[sid] += 1
        xl_at[(0, kc)] = 16 * xl_uses[sid]
        g_dma(x_pl[:, kc * T_O:(kc + 1) * T_O], x_in[0][:, kc * T_O:(kc + 1) * T_O], f"xl{sid}")

    for eng in ("vector", "scalar", "tensor"):
        p.wait(eng, "ldc", ldc_all)
        p.wait(eng, "ldg", ldg_all)

    v_rm = p.op("vector", lambda e: e.reduce_max(tiny[0:H, 0:1], bankw[:, :], axis=AX.X), inc="v")
    v_as = p.op("vector", lambda e: e.reduce_sum(tiny[0:NB, 1:2], mn4[:, :], axis=AX.X), inc="v")
    p.wait("vector", "v", v_rm)
    v_sub = p.op("vector", lambda e: e.tensor_scalar(
        wsm[:, :], bankw[:, :], tiny[0:H, 0:1], None, ALU.subtract), inc="v")
    p.wait("scalar", "v", v_sub)
    s_exp = p.op("scalar", lambda e: e.activation(wsm[:, :], wsm[:, :], AF.Exp), inc="s")
    p.wait("vector", "s", s_exp)
    v_sum = p.op("vector", lambda e: e.reduce_sum(tiny[0:H, 2:3], wsm[:, :], axis=AX.X), inc="v")
    v_act = p.op("vector", lambda e: e.tensor_scalar(
        tiny[0:NB, 3:4], tiny[0:NB, 1:2], EPS_MEM, None, ALU.is_ge), inc="v")
    p.wait("vector", "v", v_sum)
    v_rcp = p.op("vector", lambda e: e.reciprocal(tiny[0:H, 4:5], tiny[0:H, 2:3]), inc="v")
    p.wait("vector", "v", v_rcp)
    v_wsm = p.op("vector", lambda e: e.tensor_scalar(
        wsm[:, :], wsm[:, :], tiny[0:H, 4:5], None, ALU.mult), inc="v")
    if True:
        ps_i0 = PB[0][0:NB, 0:H]
        ps_i1 = PB[1][0:E, 0:1]
        p.wait("tensor", "v", v_wsm)
        p.op("tensor", lambda e: e.transpose(
            ps_i0, wsm[:, :], cst["ident_f"][0:H, 0:H]), inc="t")
        t_rw = None
        for kc in range(NC_D):
            t_rw = p.op("tensor", lambda e, k=kc: e.matmul(
                ps_i1, rwt[:, k * E:(k + 1) * E], cst["ones128_f"][:, :],
                start=(k == 0), stop=(k == NC_D - 1)), inc="t")
        p.wait("vector", "t", t_rw)
        p.wait("vector", "v", v_act)
        v_wta = p.op("vector", lambda e: e.tensor_scalar(
            wTa[:, :], ps_i0, tiny[0:NB, 3:4], None, ALU.mult), inc="v")
        p.op("vector", lambda e: e.tensor_copy(rwsum[:, :], ps_i1), inc="v")
        p.wait("scalar", "v", v_wta)
        p.op("scalar", lambda e: e.activation(lnW[:, :], wTa[:, :], AF.Ln), inc="s")
    p.barrier()


    def ln_rows_from(ps_sum, ps_sq):
        t_now = p.val("t")
        p.wait("vector", "t", t_now)
        sl_rows.gate_writer(p, "vector")
        v_ev = None
        for sub in range(NS):
            f0 = sub * NF
            p.op("vector", lambda e, ss=sub, f0=f0: e.tensor_scalar(
                rows[:, f0:f0 + NF], ps_sum[ss], 1.0 / D, None, ALU.mult), inc="v")
            v_ev = p.op("vector", lambda e, ss=sub, f0=f0: e.tensor_scalar(
                rows[:, T_O + f0:T_O + f0 + NF], ps_sq[ss], 1.0 / D, None, ALU.mult), inc="v")
        p.wait("vector", "v", v_ev)
        p.op("vector", lambda e: e.tensor_tensor(
            rows[:, 2 * T_O:3 * T_O], rows[:, 0:T_O], rows[:, 0:T_O], ALU.mult))
        p.op("vector", lambda e: e.tensor_tensor(
            rows[:, 2 * T_O:3 * T_O], rows[:, T_O:2 * T_O], rows[:, 2 * T_O:3 * T_O], ALU.subtract))
        v_var = p.op("vector", lambda e: e.tensor_scalar(
            rows[:, 2 * T_O:3 * T_O], rows[:, 2 * T_O:3 * T_O], EPS_LN, None, ALU.add), inc="v")
        p.wait("vector", "v", v_var)
        p.wait("scalar", "v", v_var)
        p.op("scalar", lambda e: e.activation(
            rows[:, T_O:2 * T_O], rows[:, 2 * T_O:3 * T_O], AF.Ln))
        s_rstd = p.op("scalar", lambda e: e.activation(
            rows[:, 2 * T_O:3 * T_O], rows[:, T_O:2 * T_O], AF.Exp, scale=-0.5), inc="s")
        p.wait("vector", "s", s_rstd)
        p.op("vector", lambda e: e.tensor_tensor(
            rows[:, 3 * T_O:4 * T_O], rows[:, 0:T_O], rows[:, 2 * T_O:3 * T_O], ALU.mult))
        p.op("vector", lambda e: e.tensor_copy(rows_bf[:, 0:T_O], rows[:, 2 * T_O:3 * T_O]))
        v_last = p.op("vector", lambda e: e.tensor_copy(
            rows_bf[:, T_O:2 * T_O], rows[:, 3 * T_O:4 * T_O]), inc="v")
        return v_last

    def ln_bcast(ps_rb, ps_t1, after_v, to_sbuf=True):
        p.wait("tensor", "v", after_v)
        t_last = None
        for sub in range(NS):
            f0 = sub * NF
            p.op("tensor", lambda e, ss=sub, f0=f0, ps_rb=ps_rb: e.matmul(
                ps_rb[ss][:, :], cst["onesr_bf"][0:1, :],
                rows_bf[0:1, f0:f0 + NF], start=True, stop=True))
            t_last = p.op("tensor", lambda e, ss=sub, f0=f0, ps_t1=ps_t1: e.matmul(
                ps_t1[ss][:, :], cst["onesr_bf"][0:1, :],
                rows_bf[0:1, T_O + f0:T_O + f0 + NF], start=True, stop=True), inc="t")
        sl_rows.read("t", t_last)
        if not to_sbuf:
            return t_last
        p.wait("scalar", "t", t_last)
        for _sl in sl_sig:
            _sl.gate_writer(p, "scalar")
        sl_sigsc[0].gate_writer(p, "scalar")
        sl_sigsc[1].gate_writer(p, "scalar")
        s_last = None
        for sub in range(NS):
            p.op("scalar", lambda e, ss=sub: e.copy(
                sig_bf[:, ss * NF:(ss + 1) * NF], ps_rb[ss][:, :]))
            s_last = p.op("scalar", lambda e, ss=sub: e.copy(
                sig_sc[:, ss * NF:(ss + 1) * NF], ps_t1[ss][:, :]), inc="s")
        return s_last

    def stats_chunk(src_bf, kc, ps_sum, ps_sq, buf, dep_v=None, dep_s=None):
        sl_xsq[buf].gate_writer(p, "scalar")
        if dep_v:
            p.wait("scalar", "v", dep_v)
        if dep_s:
            p.wait("scalar", "s", dep_s)
        s_c = p.op("scalar", lambda e, b=buf, s=src_bf: e.activation(
            xsq[:, b * T_O:(b + 1) * T_O], s, AF.Square), inc="s")
        p.wait("tensor", "s", s_c)
        if dep_v:
            p.wait("tensor", "v", dep_v)
        tl = None
        for sub in range(NS):
            f0 = sub * NF
            p.op("tensor", lambda e, k=kc, ss=sub, src=src_bf, f0=f0: e.matmul(
                ps_sum[ss], cst["ones128_bf"][:, :], src[:, f0:f0 + NF],
                start=(k == 0), stop=(k == NC_D - 1)))
            tl = p.op("tensor", lambda e, k=kc, ss=sub, b=buf, f0=f0: e.matmul(
                ps_sq[ss], cst["ones128_bf"][:, :],
                xsq[:, b * T_O + f0:b * T_O + f0 + NF],
                start=(k == 0), stop=(k == NC_D - 1)), inc="t")
        sl_xsq[buf].read("t", tl)
        return tl

    def load_mem_slab(h):
        buf = h % 3
        sem = f"ldm{buf}"
        if memslab_uses[buf] > 0:
            p.wait("gpsimd", sem, 16 * memslab_uses[buf])
            p.wait("gpsimd", "t", sl_mem[buf].rd["t"])
        p.op("gpsimd", lambda e, h=h, b=buf: e.dma_start(
            out=mem_bf[:, b * NB * 128:(b + 1) * NB * 128], in_=mem_d[h]),
            inc=sem, inc_amt=16)
        memslab_uses[buf] += 1
        return buf, (sem, 16 * memslab_uses[buf])

    def dbg_dump(name, src_pl, ot):
        if not DBG or ot != 0:
            return
        p.wait("gpsimd", "v", p.val("v"))
        p.wait("gpsimd", "s", p.val("s"))
        p.wait("gpsimd", "t", p.val("t"))
        p_dbg[0] += 16
        p.op("gpsimd", lambda e, d=dbg_d[name], s=src_pl: e.dma_start(out=d, in_=s),
             inc="ldc", inc_amt=16)
        val = p.val("ldc")
        for eng in ("tensor", "vector", "scalar", "gpsimd"):
            p.wait(eng, "ldc", val)

    def dbg_dump_small(dst, src_ap):
        if not DBG:
            return
        p.wait("gpsimd", "v", p.val("v"))
        p.op("gpsimd", lambda e, d=dst, s=src_ap: e.dma_start(out=d, in_=s),
             inc="ldc", inc_amt=16)
        val = p.val("ldc")
        for eng in ("tensor", "vector", "scalar", "gpsimd"):
            p.wait(eng, "ldc", val)

    for ot in range(n_ot):

        if True:
            ps_sum = [PB[4 + i][0:1, :] for i in range(NS)]
            ps_sq = [PB[6 + i][0:1, :] for i in range(NS)]
            for kc in range(NC_D):
                sid = kc % 4
                p.wait("scalar", f"xl{sid}", xl_at[(ot, kc)])
                p.wait("vector", f"xl{sid}", xl_at[(ot, kc)])
                p.wait("tensor", f"xl{sid}", xl_at[(ot, kc)])
                stats_chunk(x_pl[:, kc * T_O:(kc + 1) * T_O], kc, ps_sum, ps_sq, kc % 2)
            v_rows = ln_rows_from(ps_sum, ps_sq)
        if True:
            ps_rb = [PB[4 + i] for i in range(NS)]
            ps_t1 = [PB[6 + i] for i in range(NS)]
            s_bc = ln_bcast(ps_rb, ps_t1, v_rows)
            p.wait("vector", "s", s_bc)
            ln1_v = None
            for kc in range(NC_D):
                c0 = kc * T_O
                for sub in range(NS):
                    f0 = sub * NF
                    b = (kc * NS + sub) % 2
                    sl_xsq[b].gate_writer(p, "vector")
                    p.op("vector", lambda e, c=c0, b=b, f0=f0, ss=sub: e.tensor_tensor(
                        xsq[:, b * T_O:b * T_O + NF], x_pl[:, c + f0:c + f0 + NF],
                        sig_bf[:, ss * NF:(ss + 1) * NF], ALU.mult))
                    ln1_v = p.op("vector", lambda e, c=c0, b=b, f0=f0, ss=sub: e.tensor_tensor(
                        ln_pl[:, c + f0:c + f0 + NF], xsq[:, b * T_O:b * T_O + NF],
                        sig_sc[:, ss * NF:(ss + 1) * NF], ALU.subtract), inc="v")
                    sl_xsq[b].read("v", ln1_v)
                    for ss2 in range(NS):
                        sl_sig[ss2].read("v", ln1_v)
                        sl_sigsc[ss2].read("v", ln1_v)
            p.wait("tensor", "v", ln1_v)
        p.barrier()
        dbg_dump("ln1", ln_pl[:, :], ot)

        if True:
            ps_q = [PB[0], PB[1]]
            ps_n = PB[6][0:NB, 0:NF]
            ps_rn = [PB[2], PB[3]]
            ps_c = [PB[4], PB[5]]
            sl_q = [Slot(), Slot()]
            sl_n = Slot()
            sl_rn = [Slot(), Slot()]
            sl_c = [Slot(), Slot()]
            n_it = H * NS
            meta = {}
            st2 = {}

            def p2_load(it):
                h, sub = divmod(it, NS)
                if sub == 0:
                    wq_slab, wq_wait, wq_rec = ring.load(wq_d[h], NC_D * 128)
                    mbuf, mem_wait = load_mem_slab(h)
                    meta[h] = (wq_slab, wq_wait, wq_rec, mbuf, mem_wait)

            def p2_q(it):
                h, sub = divmod(it, NS)
                wq_slab, wq_wait, wq_rec, mbuf, mem_wait = meta[h]
                qi = it % 2
                f0 = sub * NF
                if sub == 0:
                    p.wait("tensor", wq_wait[0], wq_wait[1])
                sl_q[qi].gate_writer(p, "tensor")
                t_q = None
                for kc in range(NC_D):
                    t_q = p.op("tensor", lambda e, k=kc, q=qi, f0=f0, wq_slab=wq_slab: e.matmul(
                        ps_q[q], wq_slab[:, k * 128:(k + 1) * 128],
                        ln_pl[:, k * T_O + f0:k * T_O + f0 + NF],
                        start=(k == 0), stop=(k == NC_D - 1)), inc="t")
                if sub == NS - 1:
                    ring.consumed(wq_rec)
                st2[it] = {"t_q": t_q}

            def p2_sigma(it):
                qi = it % 2
                si = it % 3
                p.wait("scalar", "t", st2[it]["t_q"])
                sl_expb[qi].gate_writer(p, "scalar")
                s_e = p.op("scalar", lambda e, q=qi: e.activation(
                    expb[:, q * NF:(q + 1) * NF], ps_q[q], AF.Exp), inc="s")
                p.wait("vector", "s", s_e)
                v_em = p.op("vector", lambda e, q=qi: e.tensor_scalar(
                    expb[:, q * NF:(q + 1) * NF], expb[:, q * NF:(q + 1) * NF],
                    1.0, None, ALU.min), inc="v")
                p.wait("vector", "v", v_em)
                sl_sig[si].gate_writer(p, "vector")
                v_sig = p.op("vector", lambda e, q=qi, si=si: e.scalar_tensor_tensor(
                    sig_bf[:, si * NF:(si + 1) * NF], ps_q[q], 0.0,
                    expb[:, q * NF:(q + 1) * NF], ALU.max, ALU.add), inc="v")
                sl_q[qi].read("s", s_e)
                sl_q[qi].read("v", v_sig)
                sl_expb[qi].read("v", v_sig)
                st2[it]["v_sig"] = v_sig

            def p2_norms(it):
                h, sub = divmod(it, NS)
                si = it % 3
                sl_n.gate_writer(p, "tensor")
                p.wait("tensor", "v", st2[it]["v_sig"])
                t_n = p.op("tensor", lambda e, h=h, si=si: e.matmul(
                    ps_n, mnt[:, h * NB:(h + 1) * NB],
                    sig_bf[:, si * NF:(si + 1) * NF], start=True, stop=True), inc="t")
                sl_sig[si].read("t", t_n)
                p.wait("vector", "t", t_n)
                sl_nrm.gate_writer(p, "vector")
                v_cl = p.op("vector", lambda e: e.tensor_scalar(
                    nrm[:, 0:NF], ps_n, EPS_MEM, None, ALU.max), inc="v")
                sl_n.read("v", v_cl)
                p.wait("scalar", "v", v_cl)
                p.op("scalar", lambda e: e.activation(
                    nrm[:, NF:2 * NF], nrm[:, 0:NF], AF.Ln))
                ni = it % 2
                sl_nrmbf[ni].gate_writer(p, "scalar")
                s_rn = p.op("scalar", lambda e, h=h, ni=ni: e.activation(
                    nrm_bf[:, ni * NF:(ni + 1) * NF], nrm[:, NF:2 * NF], AF.Exp,
                    scale=-1.0, bias=lnW[:, h:h + 1]), inc="s")
                sl_nrm.read("s", s_rn)
                st2[it]["s_rn"] = s_rn

            def p2_banks(it):
                h, sub = divmod(it, NS)
                qi = it % 2
                f0 = sub * NF
                mbuf, mem_wait = meta[h][3], meta[h][4]
                ci = it % 2
                sl_c[ci].gate_writer(p, "tensor")
                if sub == 0:
                    p.wait("tensor", mem_wait[0], mem_wait[1])
                t_c = None
                t_rn_last = None
                ni = it % 2
                for n in range(NB):
                    ri = n % 2
                    sl_rn[ri].gate_writer(p, "tensor")
                    if n == 0:
                        p.wait("tensor", "s", st2[it]["s_rn"])
                    t_b = p.op("tensor", lambda e, n=n, r=ri, ni=ni: e.matmul(
                        ps_rn[r], cst["bsel4"][:, n * 128:(n + 1) * 128],
                        nrm_bf[:, ni * NF:(ni + 1) * NF], start=True, stop=True), inc="t")
                    t_rn_last = t_b
                    p.wait("vector", "t", t_b)
                    sl_sigsc[ri].gate_writer(p, "vector")
                    v_ss = p.op("vector", lambda e, si=it % 3, r=ri: e.tensor_tensor(
                        sig_sc[:, r * NF:(r + 1) * NF],
                        sig_bf[:, si * NF:(si + 1) * NF],
                        ps_rn[r], ALU.mult), inc="v")
                    sl_rn[ri].read("v", v_ss)
                    sl_sig[it % 3].read("v", v_ss)
                    p.wait("tensor", "v", v_ss)
                    t_c = p.op("tensor", lambda e, n=n, c=ci, b=mbuf, r=ri: e.matmul(
                        ps_c[c],
                        mem_bf[:, b * NB * 128 + n * 128:b * NB * 128 + (n + 1) * 128],
                        sig_sc[:, r * NF:(r + 1) * NF],
                        start=(n == 0), stop=(n == NB - 1)), inc="t")
                    sl_sigsc[ri].read("t", t_c)
                sl_nrmbf[ni].read("t", t_rn_last)
                sl_mem[mbuf].read("t", t_c)
                p.wait("scalar", "t", t_c)
                s_cv = p.op("scalar", lambda e, h=h, c=ci, f0=f0: e.copy(
                    cs_pl[:, h * T_O + f0:h * T_O + f0 + NF], ps_c[c]), inc="s")
                sl_c[ci].read("s", s_cv)

            for slot in range(n_it + 3):
                if 0 <= slot - 3 < n_it:
                    p2_banks(slot - 3)
                if 0 <= slot - 2 < n_it:
                    p2_norms(slot - 2)
                if 0 <= slot - 1 < n_it:
                    p2_sigma(slot - 1)
                if slot < n_it:
                    p2_load(slot)
                    p2_q(slot)
        p.barrier()
        dbg_dump("comb", cs_pl[:, :], ot)

        if True:
            ps_w = [PB[0], PB[1], PB[2]]
            ps_asum = [PB[4 + i][0:1, :] for i in range(NS)]
            ps_asq = [PB[6 + i][0:1, :] for i in range(NS)]
            sl_w = [Slot(), Slot(), Slot()]
            n_it = NC_D * NS
            wmeta = {}
            st3 = {}

            def p3_wo(it):
                m, sub = divmod(it, NS)
                if sub == 0:
                    wmeta[m] = ring.load(wo_d[m], NC_D * 128)
                wo_slab, wo_wait, wo_rec = wmeta[m]
                wi = it % 3
                f0 = sub * NF
                if sub == 0:
                    p.wait("tensor", wo_wait[0], wo_wait[1])
                sl_w[wi].gate_writer(p, "tensor")
                t_w = None
                for kc in range(NC_D):
                    t_w = p.op("tensor", lambda e, k=kc, w=wi, f0=f0, wo_slab=wo_slab: e.matmul(
                        ps_w[w], wo_slab[:, k * 128:(k + 1) * 128],
                        cs_pl[:, k * T_O + f0:k * T_O + f0 + NF],
                        start=(k == 0), stop=(k == NC_D - 1)), inc="t")
                if sub == NS - 1:
                    ring.consumed(wo_rec)
                st3[it] = t_w

            def p3_evst(it):
                m, sub = divmod(it, NS)
                wi = it % 3
                f0 = sub * NF
                b = m % 2
                p.wait("scalar", "t", st3[it])
                p.op("scalar", lambda e, m=m, w=wi, f0=f0: e.copy(
                    big_pl[:, m * T_O + f0:m * T_O + f0 + NF], ps_w[w]))
                if sub == 0:
                    sl_xsq[b].gate_writer(p, "scalar")
                s_sq = p.op("scalar", lambda e, w=wi, b=b, f0=f0: e.activation(
                    xsq[:, b * T_O + f0:b * T_O + f0 + NF], ps_w[w],
                    AF.Square), inc="s")
                sl_w[wi].read("s", s_sq)
                p.wait("tensor", "s", s_sq)
                p.op("tensor", lambda e, m=m, ss=sub, f0=f0: e.matmul(
                    ps_asum[ss], cst["ones128_bf"][:, :],
                    big_pl[:, m * T_O + f0:m * T_O + f0 + NF],
                    start=(m == 0), stop=(m == NC_D - 1)))
                tl = p.op("tensor", lambda e, m=m, ss=sub, b=b, f0=f0: e.matmul(
                    ps_asq[ss], cst["ones128_bf"][:, :],
                    xsq[:, b * T_O + f0:b * T_O + f0 + NF],
                    start=(m == 0), stop=(m == NC_D - 1)), inc="t")
                sl_xsq[b].read("t", tl)

            for slot in range(n_it + 1):
                if slot < n_it:
                    p3_wo(slot)
                if 0 <= slot - 1 < n_it:
                    p3_evst(slot - 1)
            v_rows = ln_rows_from(ps_asum, ps_asq)
        if True:
            ps_P = [PB[2 + i][0:E, :] for i in range(NS)]
            ps_rb = [PB[4 + i] for i in range(NS)]
            ps_t1 = [PB[6 + i] for i in range(NS)]
            s_bc = ln_bcast(ps_rb, ps_t1, v_rows)
            p.wait("vector", "s", s_bc)
            p.wait("tensor", "s", s_bc)
            ps_sum = [PB[4 + i][0:1, :] for i in range(NS)]
            ps_sq = [PB[6 + i][0:1, :] for i in range(NS)]
            x1_v = None
            t_P = None
            for kc in range(NC_D):
                c0 = kc * T_O
                s_x1 = None
                for sub in range(NS):
                    f0 = sub * NF
                    b = (kc * NS + sub) % 2
                    sl_tfp[b].gate_writer(p, "vector")
                    p.op("vector", lambda e, c=c0, b=b, f0=f0, ss=sub: e.tensor_tensor(
                        tfp[:, b * NF:(b + 1) * NF],
                        big_pl[:, c + f0:c + f0 + NF],
                        sig_bf[:, ss * NF:(ss + 1) * NF], ALU.mult))
                    p.op("vector", lambda e, b=b, ss=sub: e.tensor_tensor(
                        tfp[:, b * NF:(b + 1) * NF], tfp[:, b * NF:(b + 1) * NF],
                        sig_sc[:, ss * NF:(ss + 1) * NF], ALU.subtract))
                    sl_tfp2[b].gate_writer(p, "vector")
                    v_x1f = p.op("vector", lambda e, c=c0, b=b, f0=f0: e.tensor_tensor(
                        tfp2[:, b * NF:(b + 1) * NF], x_pl[:, c + f0:c + f0 + NF],
                        tfp[:, b * NF:(b + 1) * NF], ALU.add), inc="v")
                    sl_tfp[b].read("v", v_x1f)
                    for ss2 in range(NS):
                        sl_sig[ss2].read("v", v_x1f)
                        sl_sigsc[ss2].read("v", v_x1f)
                    p.wait("scalar", "v", v_x1f)
                    s_x1 = p.op("scalar", lambda e, c=c0, b=b, f0=f0: e.copy(
                        x_pl[:, c + f0:c + f0 + NF], tfp2[:, b * NF:(b + 1) * NF]), inc="s")
                    p.wait("tensor", "v", v_x1f)
                    t_P = p.op("tensor", lambda e, k=kc, b=b, ss=sub: e.matmul(
                        ps_P[ss][:, :], rwt[:, k * E:(k + 1) * E],
                        tfp2[:, b * NF:(b + 1) * NF],
                        start=(k == 0), stop=(k == NC_D - 1)), inc="t")
                    sl_tfp2[b].read("t", t_P)
                    sl_tfp2[b].read("s", s_x1)
                p.wait("scalar", "s", s_x1)
                stats_chunk(x_pl[:, c0:c0 + T_O], kc, ps_sum, ps_sq, kc % 2)
        dbg_dump("attn", big_pl[:, :], ot)
        dbg_dump("x1", x_pl[:, :], ot)

        if True:
            v_rows = ln_rows_from(ps_sum, ps_sq)
        if True:
            ps_rb = [PB[4 + i] for i in range(NS)]
            ps_t1 = [PB[6 + i] for i in range(NS)]
            s_bc = ln_bcast(ps_rb, ps_t1, v_rows)
            p.wait("vector", "s", s_bc)
            ln2_v = None
            for kc in range(NC_D):
                c0 = kc * T_O
                for sub in range(NS):
                    f0 = sub * NF
                    b = (kc * NS + sub) % 2
                    sl_xsq[b].gate_writer(p, "vector")
                    p.op("vector", lambda e, c=c0, b=b, f0=f0, ss=sub: e.tensor_tensor(
                        xsq[:, b * T_O:b * T_O + NF], x_pl[:, c + f0:c + f0 + NF],
                        sig_bf[:, ss * NF:(ss + 1) * NF], ALU.mult))
                    ln2_v = p.op("vector", lambda e, c=c0, b=b, f0=f0, ss=sub: e.tensor_tensor(
                        ln_pl[:, c + f0:c + f0 + NF], xsq[:, b * T_O:b * T_O + NF],
                        sig_sc[:, ss * NF:(ss + 1) * NF], ALU.subtract), inc="v")
                    sl_xsq[b].read("v", ln2_v)
                    for ss2 in range(NS):
                        sl_sig[ss2].read("v", ln2_v)
                        sl_sigsc[ss2].read("v", ln2_v)
            p.wait("tensor", "s", s_bc)
        if True:
            ps_r16 = [PB[4 + i][0:E, :] for i in range(NS)]
            ps_m16 = [PB[6 + i][0:E, :] for i in range(NS)]
            t_b16 = None
            for sub in range(NS):
                f0 = sub * NF
                p.op("tensor", lambda e, ss=sub, f0=f0: e.matmul(
                    ps_r16[ss][:, :], cst["onesr16_f"][0:1, :],
                    rows[0:1, 2 * T_O + f0:2 * T_O + f0 + NF], start=True, stop=True))
                t_b16 = p.op("tensor", lambda e, ss=sub, f0=f0: e.matmul(
                    ps_m16[ss][:, :], cst["onesr16_f"][0:1, :],
                    rows[0:1, 3 * T_O + f0:3 * T_O + f0 + NF], start=True, stop=True), inc="t")
            sl_rows.read("t", t_b16)
            p.wait("vector", "t", t_b16)
            sl_tfp[0].gate_writer(p, "vector")
            sl_tfp[1].gate_writer(p, "vector")
            v_lg = None
            for sub in range(NS):
                f0 = sub * NF
                v_m16 = p.op("vector", lambda e, ss=sub: e.tensor_scalar(
                    ps_m16[ss][:, :], ps_m16[ss][:, :], rwsum[:, 0:1], None, ALU.mult), inc="v")
                p.wait("vector", "v", v_m16)
                p.op("vector", lambda e, ss=sub: e.tensor_copy(
                    tfp[0:E, ss * NF:(ss + 1) * NF], ps_r16[ss][:, :]))
                p.op("vector", lambda e, ss=sub, f0=f0: e.tensor_tensor(
                    lg[:, f0:f0 + NF], ps_P[ss][:, :], tfp[0:E, ss * NF:(ss + 1) * NF], ALU.mult))
                v_lg = p.op("vector", lambda e, ss=sub, f0=f0: e.tensor_tensor(
                    lg[:, f0:f0 + NF], lg[:, f0:f0 + NF], ps_m16[ss][:, :], ALU.subtract), inc="v")
            sl_tfp[0].read("v", v_lg)
            sl_tfp[1].read("v", v_lg)
        p.wait("scalar", "v", v_lg)
        s_sc = p.op("scalar", lambda e: e.activation(
            lg[:, :], lg[:, :], AF.Sigmoid), inc="s")
        st_isl = {}

        def emit_island():
                ps_tt = [PB[4][:, 0:E], PB[5][:, 0:E]]
                ps_cb = [PB[6][0:E, 0:128], PB[7][0:E, 0:128]]
                p.wait("tensor", "s", s_sc)
                v_prev = None
                tt_done = None
                for c in range(NCH):
                    if c >= 2:
                        p.wait("tensor", "v", tt_done)
                    t_t = p.op("tensor", lambda e, c=c: e.transpose(
                        ps_tt[c % 2][:, :],
                        lg[:, c * 128:(c + 1) * 128], cst["ident_f"][0:E, 0:E]), inc="t")
                    p.wait("vector", "t", t_t)
                    v_t = p.op("vector", lambda e, c=c: e.tensor_copy(
                        sc_tm[:, c * E:(c + 1) * E],
                        ps_tt[c % 2][:, :]), inc="v")
                    if c >= 1:
                        tt_done = v_prev
                    v_prev = v_t

                def VO(fn):
                    p.op("vector", fn)

                for c in range(NCH):
                    VO(lambda e, c=c: e.tensor_tensor(
                        IA(0, c), sc_tm[:, c * E:(c + 1) * E], ebias_t[:, :], ALU.add))
                for c in range(NCH):
                    VO(lambda e, c=c: e.tensor_tensor(
                        IA(1, c)[:, 0:NB], IA(0, c)[:, 0:E:4], IA(0, c)[:, 1:E:4], ALU.max))
                for c in range(NCH):
                    VO(lambda e, c=c: e.tensor_tensor(
                        IA(1, c)[:, NB:2 * NB], IA(0, c)[:, 0:E:4], IA(0, c)[:, 1:E:4], ALU.min))
                for c in range(NCH):
                    VO(lambda e, c=c: e.tensor_tensor(
                        IA(1, c)[:, 2 * NB:3 * NB], IA(0, c)[:, 2:E:4], IA(0, c)[:, 3:E:4], ALU.max))
                for c in range(NCH):
                    VO(lambda e, c=c: e.tensor_tensor(
                        IA(1, c)[:, 3 * NB:4 * NB], IA(0, c)[:, 2:E:4], IA(0, c)[:, 3:E:4], ALU.min))
                for c in range(NCH):
                    VO(lambda e, c=c: e.tensor_tensor(
                        IA(2, c)[:, 0:NB], IA(1, c)[:, 0:NB], IA(1, c)[:, 2 * NB:3 * NB], ALU.max))
                for c in range(NCH):
                    VO(lambda e, c=c: e.tensor_tensor(
                        IA(2, c)[:, NB:2 * NB], IA(1, c)[:, 0:NB], IA(1, c)[:, 2 * NB:3 * NB], ALU.min))
                for c in range(NCH):
                    VO(lambda e, c=c: e.tensor_tensor(
                        IA(2, c)[:, 2 * NB:3 * NB], IA(1, c)[:, NB:2 * NB], IA(1, c)[:, 3 * NB:4 * NB], ALU.max))
                for c in range(NCH):
                    VO(lambda e, c=c: e.tensor_tensor(
                        IA(2, c)[:, NB:2 * NB], IA(2, c)[:, NB:2 * NB], IA(2, c)[:, 2 * NB:3 * NB], ALU.max))
                for c in range(NCH):
                    VO(lambda e, c=c: e.tensor_tensor(
                        IA(2, c)[:, 0:NB], IA(2, c)[:, 0:NB], IA(2, c)[:, NB:2 * NB], ALU.add))
                for c in range(NCH):
                    VO(lambda e, c=c: e.tensor_tensor(
                        IA(2, c)[:, 0:NB], IA(2, c)[:, 0:NB], cst["gdelta"][:, :], ALU.subtract))
                rv = None
                for c in range(NCH):
                    rv = p.op("vector", lambda e, c=c: e.reduce_max(
                        red1[:, c:c + 1], IA(2, c)[:, 0:NB], axis=AX.X), inc="v")
                p.wait("vector", "v", rv)
                rv = None
                for c in range(NCH):
                    rv = p.op("vector", lambda e, c=c: e.tensor_scalar(
                        IA(3, c)[:, 0:NB], IA(2, c)[:, 0:NB], red1[:, c:c + 1], None, ALU.is_ge), inc="v")
                p.wait("vector", "v", rv)
                rv = None
                for c in range(NCH):
                    rv = p.op("vector", lambda e, c=c: e.scalar_tensor_tensor(
                        IA(3, c)[:, NB:2 * NB], IA(3, c)[:, 0:NB], -BIG, IA(2, c)[:, 0:NB],
                        ALU.mult, ALU.add), inc="v")
                p.wait("vector", "v", rv)
                rv = None
                for c in range(NCH):
                    rv = p.op("vector", lambda e, c=c: e.reduce_max(
                        red1[:, 1 * NCH + c:1 * NCH + c + 1], IA(3, c)[:, NB:2 * NB], axis=AX.X), inc="v")
                p.wait("vector", "v", rv)
                rv = None
                for c in range(NCH):
                    rv = p.op("vector", lambda e, c=c: e.tensor_scalar(
                        IA(1, c)[:, 0:NB], IA(2, c)[:, 0:NB], red1[:, 1 * NCH + c:1 * NCH + c + 1], None, ALU.is_ge), inc="v")
                p.wait("vector", "v", rv)
                for c in range(NCH):
                    for j in range(4):
                        VO(lambda e, c=c, j=j: e.tensor_copy(IA(2, c)[:, j:E:4], IA(1, c)[:, 0:NB]))
                for c in range(NCH):
                    VO(lambda e, c=c: e.tensor_tensor(IA(4, c), IA(0, c), IA(2, c), ALU.mult))
                for c in range(NCH):
                    VO(lambda e, c=c: e.tensor_tensor(IA(4, c), IA(4, c), cst["edelta"][:, :], ALU.subtract))
                rv = None
                for c in range(NCH):
                    rv = p.op("vector", lambda e, c=c: e.reduce_max(
                        red1[:, 2 * NCH + c:2 * NCH + c + 1], IA(4, c), axis=AX.X), inc="v")
                p.wait("vector", "v", rv)
                rv = None
                for c in range(NCH):
                    rv = p.op("vector", lambda e, c=c: e.tensor_scalar(
                        IA(5, c), IA(4, c), red1[:, 2 * NCH + c:2 * NCH + c + 1], None, ALU.is_ge), inc="v")
                p.wait("vector", "v", rv)
                rv = None
                for c in range(NCH):
                    rv = p.op("vector", lambda e, c=c: e.scalar_tensor_tensor(
                        IA(5, c), IA(5, c), -BIG, IA(4, c), ALU.mult, ALU.add), inc="v")
                p.wait("vector", "v", rv)
                rv = None
                for c in range(NCH):
                    rv = p.op("vector", lambda e, c=c: e.reduce_max(
                        red1[:, 3 * NCH + c:3 * NCH + c + 1], IA(5, c), axis=AX.X), inc="v")
                p.wait("vector", "v", rv)
                rv = None
                for c in range(NCH):
                    rv = p.op("vector", lambda e, c=c: e.tensor_scalar(
                        IA(5, c), IA(4, c), red1[:, 3 * NCH + c:3 * NCH + c + 1], None, ALU.is_ge), inc="v")
                p.wait("vector", "v", rv)
                for c in range(NCH):
                    VO(lambda e, c=c: e.tensor_tensor(
                        IA(4, c), sc_tm[:, c * E:(c + 1) * E], IA(5, c), ALU.mult))
                rv = None
                for c in range(NCH):
                    rv = p.op("vector", lambda e, c=c: e.reduce_sum(
                        red1[:, 4 * NCH + c:4 * NCH + c + 1], IA(4, c), axis=AX.X), inc="v")
                p.wait("vector", "v", rv)
                rv = None
                for c in range(NCH):
                    rv = p.op("vector", lambda e, c=c: e.tensor_scalar(
                        red1[:, 4 * NCH + c:4 * NCH + c + 1], red1[:, 4 * NCH + c:4 * NCH + c + 1], 1e-20, None, ALU.add), inc="v")
                p.wait("vector", "v", rv)
                rv = None
                for c in range(NCH):
                    rv = p.op("vector", lambda e, c=c: e.reciprocal(
                        red1[:, 5 * NCH + c:5 * NCH + c + 1], red1[:, 4 * NCH + c:4 * NCH + c + 1]), inc="v")
                p.wait("vector", "v", rv)
                v_cmb = None
                for c in range(NCH):
                    v_cmb = p.op("vector", lambda e, c=c: e.tensor_scalar(
                        IA(4, c), IA(4, c), red1[:, 5 * NCH + c:5 * NCH + c + 1], None, ALU.mult), inc="v")
                st_isl["v_cmb"] = v_cmb

        def emit_island_tail():
            ps_cb = [PB[4][0:E, 0:128], PB[5][0:E, 0:128]]
            p.wait("tensor", "v", st_isl["v_cmb"])
            v_prev = None
            cb_done = None
            for c in range(NCH):
                if c >= 2:
                    p.wait("tensor", "v", cb_done)
                t_t = p.op("tensor", lambda e, c=c: e.transpose(
                    ps_cb[c % 2][:, :],
                    IA(4, c), cst["ident_f"][:, :]), inc="t")
                p.wait("vector", "t", t_t)
                v_cp = p.op("vector", lambda e, c=c: e.tensor_copy(
                    comb_bfs[:, c * 128:(c + 1) * 128],
                    ps_cb[c % 2][:, :]), inc="v")
                if c >= 1:
                    cb_done = v_prev
                v_prev = v_cp

        if True:
            ps_u = [PB[0], PB[1], PB[6]]
            ps_dn = [PB[2], PB[3], PB[7]]
            sl_u = [Slot(), Slot(), Slot()]
            sl_d = [Slot(), Slot(), Slot()]
            sl_cs = [Slot() for _ in range(16)]
            routed_dep = {}
            for half in range(2):
                it = 0
                for icl in range(16):
                    ic = half * 16 + icl
                    shu_slab, shu_wait, shu_rec = ring.load(shu_d[ic], NC_D * 128)
                    for sub in range(NS):
                        ui = it % 3
                        f0 = sub * NF
                        if sub == 0:
                            p.wait("tensor", shu_wait[0], shu_wait[1])
                        if half == 0 and it == 0:
                            p.wait("tensor", "v", ln2_v)
                            p.wait("tensor", "v", v_lg)
                        sl_u[ui].gate_writer(p, "tensor")
                        t_u = None
                        for kc in range(NC_D):
                            t_u = p.op("tensor", lambda e, k=kc, u=ui, f0=f0, shu_slab=shu_slab: e.matmul(
                                ps_u[u], shu_slab[:, k * 128:(k + 1) * 128],
                                ln_pl[:, k * T_O + f0:k * T_O + f0 + NF],
                                start=(k == 0), stop=(k == NC_D - 1)), inc="t")
                        if sub == NS - 1:
                            ring.consumed(shu_rec)
                        p.wait("scalar", "t", t_u)
                        b5 = it % 2
                        sl_tfp[b5].gate_writer(p, "scalar")
                        s_rl = p.op("scalar", lambda e, u=ui, b=b5: e.activation(
                            tfp[:, b * NF:(b + 1) * NF], ps_u[u], AF.Relu), inc="s")
                        sl_u[ui].read("s", s_rl)
                        p.wait("vector", "s", s_rl)
                        sl_cs[icl].gate_writer(p, "vector")
                        v_r2 = p.op("vector", lambda e, b=b5, c=icl, f0=f0: e.tensor_tensor(
                            cs_pl[:, c * T_O + f0:c * T_O + f0 + NF],
                            tfp[:, b * NF:(b + 1) * NF], tfp[:, b * NF:(b + 1) * NF],
                            ALU.mult), inc="v")
                        sl_tfp[b5].read("v", v_r2)
                        it += 1
                act_v = p.val("v")
                if half == 1:
                    emit_island_tail()
                it_d = 0
                for m in range(NC_D):
                    shd_src = shd_d[m][:, half * 16 * 128:(half + 1) * 16 * 128]
                    shd_slab, shd_wait, shd_rec = ring.load(shd_src, 16 * 128)
                    for sub in range(NS):
                        di = it_d % 3
                        f0 = sub * NF
                        if sub == 0:
                            p.wait("tensor", shd_wait[0], shd_wait[1])
                        if it_d == 0:
                            p.wait("tensor", "v", act_v)
                            if half == 0:
                                p.wait("tensor", "v", v_lg)
                        sl_d[di].gate_writer(p, "tensor")
                        t_d = None
                        for icl in range(16):
                            t_d = p.op("tensor", lambda e, c=icl, d=di, f0=f0, shd_slab=shd_slab: e.matmul(
                                ps_dn[d], shd_slab[:, c * 128:(c + 1) * 128],
                                cs_pl[:, c * T_O + f0:c * T_O + f0 + NF],
                                start=(c == 0), stop=(c == 15)), inc="t")
                        for icl in range(16):
                            sl_cs[icl].read("t", t_d)
                        if sub == NS - 1:
                            ring.consumed(shd_rec)
                        dst = big_pl[:, m * T_O + f0:m * T_O + f0 + NF]
                        if half == 0:
                            p.wait("scalar", "t", t_d)
                            cnt = p.op("scalar", lambda e, d=di, dst=dst: e.copy(
                                dst, ps_dn[d]), inc="s")
                            sl_d[di].read("s", cnt)
                            routed_dep[(m, sub)] = ("s", cnt)
                        else:
                            is_s, prev = routed_dep[(m, sub)]
                            p.wait("vector", is_s, prev)
                            p.wait("vector", "t", t_d)
                            cnt = p.op("vector", lambda e, d=di, dst=dst: e.tensor_tensor(
                                dst, ps_dn[d], dst, ALU.add), inc="v")
                            sl_d[di].read("v", cnt)
                            routed_dep[(m, sub)] = ("v", cnt)
                        it_d += 1
                if half == 0:
                    emit_island()
        p.barrier()
        dbg_dump("ln2", ln_pl[:, :], ot)
        dbg_dump("x4", x_pl[:, :], ot)
        if DBG and ot == 0:
            dbg_dump_small(dbg_lg, lg[:, :])
            dbg_dump_small(dbg_cw, comb_bfs[:, :])
        dbg_dump("x5", x_pl[:, :], ot)
        if True:
            ps_sum7 = [PB[4 + i][0:1, :] for i in range(NS)]
            ps_sq7 = [PB[6 + i][0:1, :] for i in range(NS)]
            ps_h = [PB[0], PB[1], PB[6]]
            ps_cbc = [PB[4 + i] for i in range(NS)]
            ps_dd = [PB[2], PB[3], PB[7]]
            sl_h = [Slot(), Slot(), Slot()]
            sl_cbc = Slot()
            sl_dd = [Slot(), Slot(), Slot()]
            sl_a = [Slot() for _ in range(16)]
            for grp in range(E // 2):
                es = (2 * grp, 2 * grp + 1)
                it_h = 0
                for ei, e_ in enumerate(es):
                    sl_cbc.gate_writer(p, "tensor")
                    t_cb = None
                    for sub in range(NS):
                        f0 = sub * NF
                        t_cb = p.op("tensor", lambda e, e_=e_, ss=sub, f0=f0: e.matmul(
                            ps_cbc[ss][:, :], cst["bsel16"][:, e_ * 128:(e_ + 1) * 128],
                            comb_bfs[:, f0:f0 + NF], start=True, stop=True), inc="t")
                    for ic in range(NC_EI):
                        wup_slab, wup_wait, wup_rec = ring.load(wup_d[e_, ic], NC_D * 128)
                        ch = ei * NC_EI + ic
                        for sub in range(NS):
                            hi = it_h % (2 if grp == E // 2 - 1 else 3)
                            f0 = sub * NF
                            if sub == 0:
                                p.wait("tensor", wup_wait[0], wup_wait[1])
                            sl_h[hi].gate_writer(p, "tensor")
                            t_h = None
                            for kc in range(NC_D):
                                t_h = p.op("tensor", lambda e, k=kc, h=hi, f0=f0, wup_slab=wup_slab: e.matmul(
                                    ps_h[h], wup_slab[:, k * 128:(k + 1) * 128],
                                    ln_pl[:, k * T_O + f0:k * T_O + f0 + NF],
                                    start=(k == 0), stop=(k == NC_D - 1)), inc="t")
                            if sub == NS - 1:
                                ring.consumed(wup_rec)
                            p.wait("scalar", "t", t_h)
                            b = it_h % 2
                            sl_tfp[b].gate_writer(p, "scalar")
                            s_rl = p.op("scalar", lambda e, h=hi, b=b: e.activation(
                                tfp[:, b * NF:(b + 1) * NF], ps_h[h], AF.Relu), inc="s")
                            sl_h[hi].read("s", s_rl)
                            p.wait("vector", "s", s_rl)
                            if it_h == 0:
                                p.wait("vector", "t", t_cb)
                            sl_tfp2[b].gate_writer(p, "vector")
                            p.op("vector", lambda e, b=b, ss=sub: e.tensor_tensor(
                                tfp2[:, b * NF:(b + 1) * NF],
                                tfp[:, b * NF:(b + 1) * NF],
                                ps_cbc[ss][:, :], ALU.mult))
                            sl_a[ch].gate_writer(p, "vector")
                            v_a = p.op("vector", lambda e, ch=ch, b=b, f0=f0: e.tensor_tensor(
                                cs_pl[:, ch * T_O + f0:ch * T_O + f0 + NF],
                                tfp2[:, b * NF:(b + 1) * NF],
                                tfp[:, b * NF:(b + 1) * NF], ALU.mult), inc="v")
                            sl_tfp[b].read("v", v_a)
                            sl_tfp2[b].read("v", v_a)
                            sl_cbc.read("v", v_a)
                            it_h += 1
                a_done_v = p.val("v")
                it_d = 0
                for m in range(NC_D):
                    slabs = []
                    for ei2, e2 in enumerate(es):
                        sl2, wt2, rec2 = ring.load(wdn_d[e2, m], NC_EI * 128)
                        slabs.append((sl2, wt2, rec2))
                    for sub in range(NS):
                        di = it_d % (2 if grp == E // 2 - 1 else 3)
                        f0 = sub * NF
                        if sub == 0:
                            for sl2, wt2, rec2 in slabs:
                                p.wait("tensor", wt2[0], wt2[1])
                        if it_d == 0:
                            p.wait("tensor", "v", a_done_v)
                        sl_dd[di].gate_writer(p, "tensor")
                        t_d = None
                        for ei2 in range(2):
                            sl2 = slabs[ei2][0]
                            for ic in range(NC_EI):
                                ch = ei2 * NC_EI + ic
                                t_d = p.op("tensor", lambda e, sl=sl2, ic=ic, ch=ch, d=di, f0=f0, ei2=ei2: e.matmul(
                                    ps_dd[d], sl[:, ic * 128:(ic + 1) * 128],
                                    cs_pl[:, ch * T_O + f0:ch * T_O + f0 + NF],
                                    start=(ei2 == 0 and ic == 0), stop=(ei2 == 1 and ic == NC_EI - 1)),
                                    inc="t")
                        for ch in range(16):
                            sl_a[ch].read("t", t_d)
                        if sub == NS - 1:
                            for sl2, wt2, rec2 in slabs:
                                ring.consumed(rec2)
                        is_s, prev = routed_dep[(m, sub)]
                        p.wait("vector", is_s, prev)
                        p.wait("vector", "t", t_d)
                        dst = big_pl[:, m * T_O + f0:m * T_O + f0 + NF]
                        v_add = p.op("vector", lambda e, d=di, dst=dst: e.tensor_tensor(
                            dst, ps_dd[d], dst, ALU.add), inc="v")
                        sl_dd[di].read("v", v_add)
                        routed_dep[(m, sub)] = ("v", v_add)
                        it_d += 1
                        if grp == E // 2 - 1 and sub == NS - 1 and m >= 1:
                            mm1 = m - 1
                            p.wait("scalar", "v", routed_dep[(mm1, NS - 1)][1])
                            stats_chunk(big_pl[:, mm1 * T_O:(mm1 + 1) * T_O], mm1,
                                        ps_sum7, ps_sq7, mm1 % 2)
            mm1 = NC_D - 1
            p.wait("scalar", "v", routed_dep[(mm1, NS - 1)][1])
            stats_chunk(big_pl[:, mm1 * T_O:(mm1 + 1) * T_O], mm1,
                        ps_sum7, ps_sq7, mm1 % 2)
        p.barrier()
        dbg_dump("x6", x_pl[:, :], ot)
        dbg_dump("ffn", big_pl[:, :], ot)

        if True:
            v_rows = ln_rows_from(ps_sum7, ps_sq7)
        if DBG and ot == 0:
            dbg_dump_small(dbg_rows, rows[:, :])
        if True:
            ps_rb = [PB[4 + i] for i in range(NS)]
            ps_t1 = [PB[6 + i] for i in range(NS)]
            s_bc = ln_bcast(ps_rb, ps_t1, v_rows)
            p.wait("vector", "s", s_bc)
            v_out = None
            for kc in range(NC_D):
                c0 = kc * T_O
                for sub in range(NS):
                    f0 = sub * NF
                    b = (kc * NS + sub) % 2
                    sl_xsq[b].gate_writer(p, "vector")
                    p.op("vector", lambda e, c=c0, b=b, f0=f0, ss=sub: e.tensor_tensor(
                        xsq[:, b * T_O:b * T_O + NF],
                        big_pl[:, c + f0:c + f0 + NF],
                        sig_bf[:, ss * NF:(ss + 1) * NF], ALU.mult))
                    p.op("vector", lambda e, b=b, ss=sub: e.tensor_tensor(
                        xsq[:, b * T_O:b * T_O + NF], xsq[:, b * T_O:b * T_O + NF],
                        sig_sc[:, ss * NF:(ss + 1) * NF], ALU.subtract))
                    v_out = p.op("vector", lambda e, c=c0, b=b, f0=f0: e.tensor_tensor(
                        x_pl[:, c + f0:c + f0 + NF], x_pl[:, c + f0:c + f0 + NF],
                        xsq[:, b * T_O:b * T_O + NF], ALU.add), inc="v")
                    sl_xsq[b].read("v", v_out)
                    for ss2 in range(NS):
                        sl_sig[ss2].read("v", v_out)
                        sl_sigsc[ss2].read("v", v_out)
                sid = kc % 4
                p.wait("gpsimd", "v", v_out)
                if xs_uses[sid] > 0:
                    p.wait("gpsimd", f"xs{sid}", 16 * xs_uses[sid])
                xs_uses[sid] += 1
                g_dma(out_d[ot][:, c0:c0 + T_O], x_pl[:, c0:c0 + T_O], f"xs{sid}")
                if ot + 1 < n_ot:
                    p.wait("gpsimd", f"xs{sid}", 16 * xs_uses[sid])
                    if xl_uses[sid] > 0:
                        p.wait("gpsimd", f"xl{sid}", 16 * xl_uses[sid])
                    xl_uses[sid] += 1
                    xl_at[(ot + 1, kc)] = 16 * xl_uses[sid]
                    g_dma(x_pl[:, c0:c0 + T_O], x_in[ot + 1][:, c0:c0 + T_O], f"xl{sid}")
            p.wait("tensor", "v", v_out)
        p.barrier()

    for i in range(4):
        p.wait("sync", f"xs{i}", 16 * xs_uses[i])

    with nc.Block() as block:
        @block.sync
        def _(eng):
            for f in p.ops["sync"]:
                f(eng)

        @block.gpsimd
        def _(eng):
            for f in p.ops["gpsimd"]:
                f(eng)

        @block.tensor
        def _(eng):
            for f in p.ops["tensor"]:
                f(eng)

        @block.vector
        def _(eng):
            for f in p.ops["vector"]:
                f(eng)

        @block.scalar
        def _(eng):
            for f in p.ops["scalar"]:
                f(eng)

    return nc



def _in_maps(per_core, shared):
    maps = []
    for xs in per_core:
        m = {"x": xs}
        m.update(shared)
        maps.append(m)
    return maps


def _reassemble(results, n_ot, T_O):
    outs = []
    for r in results:
        a = np.asarray(r["out"]).reshape(n_ot, 128, NC_D, T_O)
        a = a.transpose(2, 1, 0, 3).reshape(D, n_ot * T_O).T
        outs.append(a)
    full = np.concatenate(outs, axis=0)
    return np.ascontiguousarray(full.reshape(2, 8192, D).astype(np.float32))


def kernel(**inputs):
    per_core, shared, n_ot = _prep(inputs)
    nc = build(n_ot, CFG["T_O"])
    maps = _in_maps(per_core, shared)
    res = run_bass_kernel_spmd(nc, maps, core_ids=list(range(len(maps))))
    return _reassemble(res.results, n_ot, CFG["T_O"])
